# version 03
# baseline (speedup 1.0000x reference)
import contextlib
import numpy as np
import ml_dtypes
import concourse.bass as bass
import concourse.mybir as mybir
from concourse.bass_utils import run_bass_kernel_spmd
from concourse.alu_op_type import AluOpType as OP

F32 = mybir.dt.float32
BF16 = mybir.dt.bfloat16
AF = mybir.ActivationFunctionType
AX = mybir.AxisListType
BF16NP = ml_dtypes.bfloat16

L = 4096
DM = 1024
DIN = 2048
DST = 128
NH = 32
HD = 64
KCONV = 4
CONVD = DIN + 2 * DST
DPROJ = 2 * DIN + 2 * DST + NH
CH = 128
GT = 512
NGRP = L // GT
CPG = GT // CH
NEXP = 8
IEXP = 512
ISH = 2048
NPE = 12
EPS_LN = 1e-5
EPS_RMS = 1e-12

ENGS = ("sync", "tensor", "vector", "scalar", "gpsimd")


class Prog:
    def __init__(self, nc):
        self.nc = nc
        self.thunks = {e: [] for e in ENGS}
        self.cnt = {e: 0 for e in ENGS}
        self.cnt["gpsimdc"] = 0
        self.sem = {}
        self.last_fence = {}

    def raw(self, eng, fn):
        self.thunks[eng].append((fn, 0, None))

    def sig(self, eng, fn, inc=1):
        ch = "gpsimdc" if (eng == "gpsimd" and inc == 1) else eng
        if inc == 1 and eng in ("vector", "scalar", "gpsimd"):
            v = self.cnt[ch]
            if v > 0 and self.last_fence.get(ch + "_self", -1) < v:
                self.thunks[eng].append(
                    (lambda e, s=ch, vv=v: e.wait_ge(self.sem[s], vv), 0, None))
                self.last_fence[ch + "_self"] = v
        self.thunks[eng].append((fn, inc, ch))
        self.cnt[ch] += inc
        return self.cnt[ch]

    def dma(self, eng, fn):
        ch = eng
        self.thunks[eng].append((fn, 16, ch))
        self.cnt[ch] += 16
        return self.cnt[ch]

    def tsync(self, eng):
        self.wait(eng, eng, self.cnt[eng])

    def wait(self, eng, src, val=None):
        if val is None or src in ("sync", "gpsimd"):
            val = self.cnt[src]
            if src in ("sync", "gpsimd") and src != eng:
                if self.last_fence.get(src, -1) < val:
                    self.thunks[src].append(
                        (lambda e, s=src, v=val: e.wait_ge(self.sem[s], v), 0, None))
                    self.last_fence[src] = val
        if val <= 0:
            return
        self.thunks[eng].append((lambda e, s=src, v=val: e.wait_ge(self.sem[s], v), 0, None))

    def build(self):
        nc = self.nc
        with contextlib.ExitStack() as stack:
            for e in list(ENGS) + ["gpsimdc"]:
                self.sem[e] = stack.enter_context(nc.semaphore(f"sem_{e}"))
            with nc.Block() as block:
                for e in ENGS:
                    def body(eng, e=e):
                        for fn, inc, ch in self.thunks[e]:
                            ins = fn(eng)
                            if inc:
                                ins.then_inc(self.sem[ch], inc)
                    getattr(block, e)(body)


class Bank:

    def __init__(self, p, ap):
        self.p = p
        self.ap = ap
        self.evac = None

    def acquire(self):
        if self.evac is not None:
            self.p.wait("tensor", *self.evac)
            self.evac = None

    def release(self, eng, sig):
        self.evac = (eng, sig)


def build_core(nc, dbg=(), upto="all", ngroups=NGRP, full_groups=None):
    p = Prog(nc)
    dbgset = set(dbg)
    T = {}

    def din(name, shape, dt=F32):
        T[name] = nc.dram_tensor(name, list(shape), dt, kind="ExternalInput").ap()
        return T[name]

    def dout(name, shape, dt=F32):
        T[name] = nc.dram_tensor(name, list(shape), dt, kind="ExternalOutput").ap()
        return T[name]

    def dint(name, shape, dt):
        T[name] = nc.dram_tensor(name, list(shape), dt, kind="Internal").ap()
        return T[name]

    u_h = din("u", [L, DM])
    W_inT_h = din("W_inT", [DM, DPROJ])
    W_outT_h = din("W_outT", [DIN, DM])
    conv_w_h = din("conv_w_fm", [128, 18 * KCONV])
    conv_b_h = din("conv_b_fm", [128, 18])
    b_xbc_h = din("b_xbc_fm", [128, 18])
    b_z_h = din("b_z_row", [1, DIN])
    b_dt_h = din("b_dt_col", [NH, 1])
    dt_bias_h = din("dt_bias_col", [NH, 1])
    A_log_h = din("A_log_col", [NH, 1])
    D_h = din("D_row", [1, NH])
    nw_h = din("nw_row", [1, DIN])
    b_out_h = din("b_out_row", [1, DM])
    rw_h = din("router_w_fm", [128, 8 * NEXP])
    rb_h = din("router_bias11", [1, 1])
    gate_h = din("gate_w", [NEXP, DM, IEXP])
    up_h = din("up_w", [NEXP, DM, IEXP])
    down_h = din("down_w", [NEXP, IEXP, DM])
    sg_h = din("s_gate_w", [DM, ISH])
    su_h = din("s_up_w", [DM, ISH])
    sd_h = din("s_down_w", [ISH, DM])
    ident_h = din("ident", [64, 64])
    ident_bf_h = din("ident_bf", [128, 128], BF16)
    tri_h = din("tri", [128, 128], BF16)
    sel_h = din("sel_bf", [64, NH * 128], BF16)
    sel16_h = din("sel16_bf", [16, NEXP * 128], BF16)
    strictu_h = din("strictu_bf", [128, 128], BF16)
    maski_h = din("maski_bf", [128, 512], BF16)
    ones32_h = din("ones32", [32, 128])

    out_h = dout("out", [L, DM])

    def dtap(name, shape, dt=F32):
        return dout("dbg_" + name, shape, dt) if name in dbgset else None

    def tap_dma(fn):
        sig = p.dma("sync", fn)
        for e_ in ("vector", "scalar", "gpsimd", "tensor"):
            p.wait(e_, "sync", sig)

    tp_lnu = dtap("lnu", [L, DM], BF16)
    tp_xbc = dtap("xbc", [18 * 128, L], BF16)
    tp_z = dtap("z", [L, DIN], BF16)
    tp_dt = dtap("dt", [NH, L])
    tp_dacs = dtap("dacs", [NH, L])
    tp_y = dtap("y", [L, DIN], BF16)
    tp_yn = dtap("yn", [L, DIN], BF16)
    tp_x1 = dtap("x1", [L, DM])
    tp_h = dtap("h", [L, DM], BF16)
    tp_w = dtap("w", [L, NEXP])
    tp_ffn = dtap("ffn", [L, DM])
    tp_st = dtap("st", [128, 16])
    tp_u = dtap("u0", [128, DM])

    lnu_d = [dint(f"lnu_d{i}", [GT, DM], BF16) for i in range(2)]
    xs_d = [dint(f"xs_d{i}", [DIN, GT], BF16) for i in range(2)]
    bb_d = [dint(f"bb_d{i}", [128, GT], BF16) for i in range(2)]
    yb_d = [dint(f"yb_d{i}", [GT, DIN], BF16) for i in range(2)]
    z_d = [dint(f"z_d{i}", [GT, DIN], BF16) for i in range(2)]
    e1_d = [dint(f"e1_d{i}", [64, 128], BF16) for i in range(2)]
    x1_d = dint("x1_d", [L, DM], F32)
    hb_d = dint("hb_d", [L, DM], BF16)
    gate_bd = dint("gate_bd", [NEXP, 128, 8 * IEXP], BF16)
    up_bd = dint("up_bd", [NEXP, 128, 8 * IEXP], BF16)
    down_bd = dint("down_bd", [NEXP, 128, 4 * DM], BF16)
    sg_bd = dint("sg_bd", [128, 8 * ISH], BF16)
    su_bd = dint("su_bd", [128, 8 * ISH], BF16)
    sd_bd = dint("sd_bd", [128, 16 * DM], BF16)

    sbuf = lambda name, shape, dt=F32: nc.alloc_sbuf_tensor(name, list(shape), dt).ap()

    conv_w = sbuf("conv_w", [128, 18 * KCONV])
    conv_b = sbuf("conv_b", [128, 18])
    b_xbc = sbuf("b_xbc", [128, 18])
    b_z_bf = sbuf("b_z_bf", [1, DIN], BF16)
    dtb = sbuf("dtb", [NH, 1])
    A_col = sbuf("A_col", [NH, 1])
    D_cols = sbuf("D_cols", [128, NH])
    nw_bc = sbuf("nw_bc", [128, DIN], BF16)
    b_out_bf = sbuf("b_out_bf", [1, DM], BF16)
    rw = sbuf("rw", [128, 8 * NEXP], BF16)
    ident_f = sbuf("ident_f", [64, 64])
    ident_b = sbuf("ident_b", [128, 128], BF16)
    tri_m = sbuf("tri_m", [128, 128], BF16)
    sel16 = sbuf("sel16", [16, NEXP * 128], BF16)
    strictu = sbuf("strictu", [128, 128], BF16)
    maski = sbuf("maski", [128, 512], BF16)
    ones_row_bf = sbuf("ones_row_bf", [1, 128], BF16)
    ones32x128 = sbuf("ones32x128", [32, 128])
    eps_ln_col = sbuf("eps_ln_col", [128, 1])
    eps_rms_col = sbuf("eps_rms_col", [128, 1])
    one32c = sbuf("one32c", [NH, 1])
    rb_sb = sbuf("rb_sb", [1, 1], BF16)
    rb_col = sbuf("rb_col", [128, 1])
    stat = sbuf("stat", [128, 16])
    x1_t = [sbuf(f"x1_t{i}", [128, DM]) for i in range(2)]
    h_bf = [sbuf(f"h_bf{i}", [128, DM], BF16) for i in range(2)]
    D_row_sb = sbuf("D_row_sb", [1, NH], BF16)

    PA = nc.alloc_psum_tensor("PA", [128, 1024], F32).ap()
    PB = nc.alloc_psum_tensor("PB", [128, 1024], F32).ap()
    PC = nc.alloc_psum_tensor("PC", [128, 1024], F32).ap()
    PD = nc.alloc_psum_tensor("PD", [128, 1024], F32).ap()
    bA0, bA1 = Bank(p, PA[:, 0:512]), Bank(p, PA[:, 512:1024])
    bB0, bB1 = Bank(p, PB[:, 0:512]), Bank(p, PB[:, 512:1024])
    bC = Bank(p, PC)
    bD = Bank(p, PD)

    def tt_b(e, out, in0, col, op):
        a = col.shape[1]
        return e.tensor_tensor(
            out.rearrange("q (x y) -> q x y", x=a),
            in0.rearrange("q (x y) -> q x y", x=a),
            col.unsqueeze(2).broadcast_to([128, a, HD]), op)

    for dst, src in [(conv_w, conv_w_h), (conv_b, conv_b_h), (b_xbc, b_xbc_h),
                     (ident_f, ident_h), (ident_b, ident_bf_h), (tri_m, tri_h),
                     (ones32x128, ones32_h), (dtb, b_dt_h),
                     (A_col, A_log_h),
                     (strictu, strictu_h), (maski, maski_h),
                     (sel16, sel16_h)]:
        p.dma("sync", lambda e, d=dst, s=src: e.dma_start(out=d, in_=s))
    p.dma("sync", lambda e: e.dma_start(out=stat[0:NH, 15:16], in_=dt_bias_h))
    p.dma("sync", lambda e: e.dma_start(out=stat[0:NH, 14:15], in_=b_dt_h))
    p.dma("sync", lambda e: e.dma_start(out=stat[0:NH, 13:14], in_=A_log_h))
    p.dma("gpsimd", lambda e: e.dma_start(out=b_z_bf, in_=b_z_h))
    p.dma("gpsimd", lambda e: e.dma_start(out=b_out_bf, in_=b_out_h))
    p.dma("gpsimd", lambda e: e.dma_start(out=rw, in_=rw_h))
    p.dma("gpsimd", lambda e: e.dma_start(out=D_row_sb, in_=D_h))
    p.dma("gpsimd", lambda e: e.dma_start(out=rb_sb, in_=rb_h))
    ld_sync0 = p.cnt["sync"]

    for e_ in range(NEXP):
        p.dma("gpsimd", lambda e, i=e_: e.dma_start(
            out=gate_bd[i].rearrange("q (dc i) -> q dc i", dc=8),
            in_=gate_h[i].rearrange("(dc q) i -> q dc i", q=128)))
        p.dma("gpsimd", lambda e, i=e_: e.dma_start(
            out=up_bd[i].rearrange("q (dc i) -> q dc i", dc=8),
            in_=up_h[i].rearrange("(dc q) i -> q dc i", q=128)))
        p.dma("gpsimd", lambda e, i=e_: e.dma_start(
            out=down_bd[i].rearrange("q (ic d) -> q ic d", ic=4),
            in_=down_h[i].rearrange("(ic q) d -> q ic d", q=128)))
    p.dma("gpsimd", lambda e: e.dma_start(
        out=sg_bd.rearrange("q (dc m) -> q dc m", dc=8),
        in_=sg_h.rearrange("(dc q) m -> q dc m", q=128)))
    p.dma("gpsimd", lambda e: e.dma_start(
        out=su_bd.rearrange("q (dc m) -> q dc m", dc=8),
        in_=su_h.rearrange("(dc q) m -> q dc m", q=128)))
    p.dma("gpsimd", lambda e: e.dma_start(
        out=sd_bd.rearrange("q (mc d) -> q mc d", mc=16),
        in_=sd_h.rearrange("(mc q) d -> q mc d", q=128)))
    ffn_precast = p.cnt["gpsimd"]

    p.wait("vector", "sync", ld_sync0)
    for ap_, v in [(ones_row_bf, 1.0), (eps_ln_col, EPS_LN),
                   (eps_rms_col, EPS_RMS), (one32c, 1.0)]:
        p.sig("vector", lambda e, a=ap_, x=v: e.memset(a, x))
    p.sig("vector", lambda e: e.tensor_tensor(dtb, stat[0:NH, 14:15], stat[0:NH, 15:16], OP.add))
    p.wait("scalar", "sync", ld_sync0)
    s_a = p.sig("scalar", lambda e: e.activation(out=stat[0:NH, 12:13], in_=stat[0:NH, 13:14], func=AF.Exp))
    p.wait("vector", "scalar", s_a)
    v_init = p.sig("vector", lambda e: e.tensor_scalar_mul(A_col, stat[0:NH, 12:13], -1.0))
    p.wait("tensor", "vector", v_init)
    p.wait("tensor", "gpsimd")
    bA0.acquire()
    t_d = p.sig("tensor", lambda e: e.matmul(PA[:, 0:NH], ones_row_bf, D_row_sb, start=True, stop=True))
    p.wait("vector", "tensor", t_d)
    vd = p.sig("vector", lambda e: e.tensor_copy(out=D_cols, in_=PA[:, 0:NH]))
    bA0.release("vector", vd)
    bA0.acquire()
    t_rb = p.sig("tensor", lambda e: e.matmul(PA[:, 0:1], ones_row_bf, rb_sb, start=True, stop=True))
    p.wait("vector", "tensor", t_rb)
    v_rb = p.sig("vector", lambda e: e.tensor_copy(out=rb_col, in_=PA[:, 0:1]))
    bA0.release("vector", v_rb)

    def emit_ln_stats(src_ap, width, eps_col, dump, psum_src=False):
        v_red = p.sig("vector", lambda e, a=src_ap: e.tensor_reduce(
            stat[:, 0:1], a, AX.X, OP.add))
        if psum_src:
            p.wait("scalar", "vector", v_red)
        s_sq = p.sig("scalar", lambda e, a=src_ap, w=width, d=dump: e.activation(
            out=d[:, 0:w], in_=a, func=AF.Square, accum_out=stat[:, 1:2]))
        p.wait("vector", "scalar", s_sq)
        p.tsync("vector")
        p.sig("vector", lambda e, w=width: e.tensor_scalar_mul(stat[:, 2:3], stat[:, 0:1], 1.0 / w))
        p.sig("vector", lambda e, w=width: e.tensor_scalar_mul(stat[:, 3:4], stat[:, 1:2], 1.0 / w))
        p.tsync("vector")
        p.sig("vector", lambda e: e.tensor_tensor(stat[:, 4:5], stat[:, 2:3], stat[:, 2:3], OP.mult))
        p.tsync("vector")
        v_var = p.sig("vector", lambda e: e.tensor_tensor(stat[:, 6:7], stat[:, 3:4], stat[:, 4:5], OP.subtract))
        p.wait("scalar", "vector", v_var)
        p.sig("scalar", lambda e, c=eps_col: e.activation(out=stat[:, 5:6], in_=stat[:, 6:7], func=AF.Ln, bias=c))
        p.tsync("scalar")
        return p.sig("scalar", lambda e: e.activation(out=stat[:, 6:7], in_=stat[:, 5:6], func=AF.Exp, scale=-0.5))

    hb_done = 0

    with contextlib.ExitStack() as astack:
        asb = lambda name, shape, dt=F32: astack.enter_context(
            nc.sbuf_tensor(name, list(shape), dt)).ap()

        W_inT = asb("W_in_sb", [128, 8 * DPROJ], BF16)
        wout_b = [asb(f"wout_b{f}", [128, DM], BF16) for f in range(8)]
        u_t = [asb(f"u_t{t}", [128, DM]) for t in range(2)]
        lnuT = [[asb(f"lnuT{b}_{d}", [128, GT], BF16) for d in range(8)] for b in range(2)]
        z_sil = asb("z_sil", [128, DIN], BF16)
        B_fm = asb("B_fm", [128, GT], BF16)
        C_fm = asb("C_fm", [128, GT], BF16)
        dt_fm_g = asb("dt_fm_g", [NH, GT])
        xbc_w = [asb(f"xbc_w{i}", [128, GT + 3], BF16) for i in range(2)]
        halo = asb("halo", [128, 18 * 3], BF16)
        xsil = [asb(f"xsil{i}", [128, GT], BF16) for i in range(2)]
        tri64a = asb("tri64a", [64, 128])
        tri64b = asb("tri64b", [64, 128])
        dA_fm = asb("dA_fm", [NH, 128])
        neg_last = asb("neg_last", [NH, 1])
        tok64a = asb("tok64a", [128, 64])
        tok64b = asb("tok64b", [128, 64])
        Edec = asb("Edec", [128, 64])
        lhsE1 = asb("lhsE1", [66, 128], BF16)
        rhsE1 = asb("rhsE1", [66, NH * 128], BF16)
        x_tok = asb("x_tok", [128, DIN], BF16)
        B_tok = asb("B_tok", [128, 128], BF16)
        xdt = asb("xdt", [128, DIN], BF16)
        xdtd = asb("xdtd", [128, DIN], BF16)
        xD = asb("xD", [128, DIN], BF16)
        CBTm = asb("CBTm", [128, 128], BF16)
        es = [asb(f"es{i}", [128, GT], BF16) for i in range(2)]
        Mp = [asb(f"Mp{i}", [128, GT], BF16) for i in range(2)]
        S_f = asb("S_f", [128, DIN])
        S_bf = asb("S_bf", [128, DIN], BF16)
        ysb = asb("ysb", [128, DIN], BF16)
        tmp1k = asb("tmp1k", [128, 1024])
        y_n = [asb(f"y_n{i}", [128, DIN], BF16) for i in range(2)]
        mix_ln = tmp1k
        ynT = None
        y2_bf = xdt
        lnu_bf = [y_n[0][:, 0:DM], y_n[1][:, 0:DM]]
        conv_o = [ysb.bitcast(F32)[:, 0:GT], ysb.bitcast(F32)[:, GT:2 * GT],
                  xdtd.bitcast(F32)[:, 0:GT], xdtd.bitcast(F32)[:, GT:2 * GT]]
        sp_a = tmp1k[0:NH, 0:GT]
        scr32 = tmp1k[0:NH, 0:128]
        sp_l = tmp1k[0:NH, GT:2 * GT]
        ynT = [es[0], es[1], Mp[0], Mp[1], xsil[0], xsil[1], B_fm, C_fm]

        p.dma("gpsimd", lambda e: e.dma_start(
            out=W_inT.rearrange("q (dc f) -> q dc f", dc=8),
            in_=W_inT_h.rearrange("(dc q) f -> q dc f", q=128)))
        ld_w = p.cnt["gpsimd"]
        p.wait("tensor", "gpsimd", ld_w)
        p.wait("vector", "sync", ld_sync0)
        p.sig("vector", lambda e: e.memset(lhsE1[64:66, :], 1.0))
        p.dma("gpsimd", lambda e: e.dma_start(out=ysb[0:1, 0:DIN], in_=nw_h))
        p.wait("tensor", "gpsimd")
        for k in range(4):
            bA1.acquire()
            tk = p.sig("tensor", lambda e, k=k: e.matmul(
                PA[:, 512:1024], ones_row_bf, ysb[0:1, k * 512:(k + 1) * 512],
                start=True, stop=True))
            p.wait("vector", "tensor", tk)
            vk = p.sig("vector", lambda e, k=k: e.tensor_copy(
                out=nw_bc[:, k * 512:(k + 1) * 512], in_=PA[:, 512:1024]))
            bA1.release("vector", vk)
        p.sig("vector", lambda e: e.memset(S_f, 0.0))
        p.sig("vector", lambda e: e.memset(S_bf, 0.0))
        p.sig("vector", lambda e: e.memset(halo, 0.0))
        p.dma("sync", lambda e: e.dma_start(out=rhsE1[0:64, :], in_=sel_h))
        p.wait("tensor", "sync")
        p.wait("tensor", "vector")

        xbc_conv_v = [0, 0]
        conv_silu_s = [0, 0]
        xsil_st = [0, 0]
        lnu_st_par = [0, 0]
        yn_st = [0, 0]
        x1_par = [0, 0]
        hb_par = [0, 0]
        es_v = [0, 0]
        mp_pe = [0, 0]
        u_war_v = [0, 0]
        yn_v_last = 0
        z_rd_war = 0
        rhsE1_war = 0
        lnuT_guard = [0, 0]
        lnu_d_war = [0, 0]
        xs_d_war = [0, 0]
        bb_d_war = [0, 0]
        yb_d_war = [0, 0]
        e1_d_war = [0, 0]
        bc_fm_guard = 0
        ynT_guard = 0
        xk_v = 0
        xk_g = 0
        btok_pe = 0
        tok_g_guard = 0

        for g in range(ngroups):
            gb = g % 2
            st_last = 0
            u_norm_v = [0, 0]
            for tc in range(CPG):
                p.wait("sync", "vector", u_war_v[tc % 2])
                u_ld = p.dma("sync", lambda e, t=tc, g=g: e.dma_start(
                    out=u_t[t % 2], in_=u_h[(g * GT + t * CH):(g * GT + (t + 1) * CH), :]))
                p.wait("vector", "sync", u_ld)
                p.wait("scalar", "sync", u_ld)
                s_rstd = emit_ln_stats(u_t[tc % 2], DM, eps_ln_col, tmp1k)
                p.wait("vector", "scalar", s_rstd)
                p.wait("vector", "gpsimd", max(lnu_st_par[tc % 2], yn_st[tc % 2]))
                v_norm = p.sig("vector", lambda e, t=tc: e.tensor_scalar(
                    lnu_bf[t % 2], u_t[t % 2], stat[:, 2:3], stat[:, 6:7],
                    OP.subtract, OP.mult))
                u_war_v[tc % 2] = v_norm
                if tp_st is not None and g == 0 and tc == 0:
                    p.wait("sync", "vector", v_norm)
                    tap_dma(lambda e: e.dma_start(out=tp_st, in_=stat))
                    p.dma("sync", lambda e: e.dma_start(out=tp_u, in_=u_t[0]))
                p.wait("gpsimd", "vector", v_norm)
                st_last = p.dma("gpsimd", lambda e, t=tc, gb=gb: e.dma_start(
                    out=lnu_d[gb][t * CH:(t + 1) * CH, :], in_=lnu_bf[t % 2]))
                lnu_st_par[tc % 2] = st_last
                if tp_lnu is not None:
                    p.wait("sync", "gpsimd", st_last)
                    tap_dma(lambda e, t=tc, b=gb, g=g: e.dma_start(
                        out=tp_lnu[(g * GT + t * CH):(g * GT + (t + 1) * CH), :],
                        in_=lnu_d[b][t * CH:(t + 1) * CH, :]))
            p.wait("sync", "gpsimd", st_last)
            p.wait("sync", "tensor", lnuT_guard[gb])
            lnuT_ld = 0
            for dc in range(8):
                lnuT_ld = p.dma("sync", lambda e, d=dc, gb=gb: e.dma_start(
                    out=lnuT[gb][d], in_=lnu_d[gb][:, d * 128:(d + 1) * 128], transpose=True))
            lnu_d_war[gb] = lnuT_ld

            p.wait("tensor", "sync", lnuT_ld)
            for fc in range(18):
                pb = fc % 2
                bank = [bA0, bA1][pb]
                bank.acquire()
                pe_s = 0
                for kc in range(8):
                    fn = lambda e, f=fc, k=kc, b=bank, gb=gb: e.matmul(
                        b.ap, W_inT[:, k * DPROJ + DIN + f * 128: k * DPROJ + DIN + (f + 1) * 128],
                        lnuT[gb][k], start=(k == 0), stop=(k == 7))
                    if kc < 7:
                        p.raw("tensor", fn)
                    else:
                        pe_s = p.sig("tensor", fn)
                p.wait("scalar", "tensor", pe_s)
                p.wait("scalar", "vector", xbc_conv_v[pb])
                ev = p.sig("scalar", lambda e, f=fc, b=bank: e.activation(
                    out=xbc_w[f % 2][:, 3:GT + 3], in_=b.ap, func=AF.Identity, bias=b_xbc[:, f:f + 1]))
                bank.release("scalar", ev)
                p.wait("vector", "scalar", ev)
                p.sig("vector", lambda e, f=fc: e.tensor_copy(
                    out=xbc_w[f % 2][:, 0:3], in_=halo[:, f * 3:f * 3 + 3]))
                ca, cb = conv_o[2 * pb], conv_o[2 * pb + 1]
                p.wait("vector", "scalar", conv_silu_s[pb])
                p.tsync("vector")
                p.sig("vector", lambda e, f=fc, a=ca: e.tensor_scalar(
                    a, xbc_w[f % 2][:, 0:GT], conv_w[:, f * 4:f * 4 + 1],
                    conv_b[:, f:f + 1], OP.mult, OP.add))
                p.sig("vector", lambda e, f=fc, a=ca, b=cb: e.scalar_tensor_tensor(
                    b, xbc_w[f % 2][:, 1:GT + 1], conv_w[:, f * 4 + 1:f * 4 + 2], a, OP.mult, OP.add))
                p.sig("vector", lambda e, f=fc, a=ca, b=cb: e.scalar_tensor_tensor(
                    a, xbc_w[f % 2][:, 2:GT + 2], conv_w[:, f * 4 + 2:f * 4 + 3], b, OP.mult, OP.add))
                cv = p.sig("vector", lambda e, f=fc, a=ca, b=cb: e.scalar_tensor_tensor(
                    b, xbc_w[f % 2][:, 3:GT + 3], conv_w[:, f * 4 + 3:f * 4 + 4], a, OP.mult, OP.add))
                hv = p.sig("vector", lambda e, f=fc: e.tensor_copy(
                    out=halo[:, f * 3:f * 3 + 3], in_=xbc_w[f % 2][:, GT:GT + 3]))
                xbc_conv_v[pb] = hv
                p.wait("scalar", "vector", cv)
                if fc < 16:
                    p.wait("scalar", "gpsimd", xsil_st[pb])
                    sl = p.sig("scalar", lambda e, f=fc, b=cb: e.activation(
                        out=xsil[f % 2], in_=b, func=AF.Silu))
                    conv_silu_s[pb] = sl
                    p.wait("gpsimd", "scalar", sl)
                    p.wait("gpsimd", "sync", xs_d_war[gb])
                    xsil_st[pb] = p.dma("gpsimd", lambda e, f=fc, gb=gb: e.dma_start(
                        out=xs_d[gb][f * 128:(f + 1) * 128, :], in_=xsil[f % 2]))
                    if tp_xbc is not None:
                        p.wait("sync", "scalar", sl)
                        tap_dma(lambda e, f=fc, g=g: e.dma_start(
                            out=tp_xbc[f * 128:(f + 1) * 128, g * GT:(g + 1) * GT],
                            in_=xsil[f % 2]))
                else:
                    dst = B_fm if fc == 16 else C_fm
                    p.wait("scalar", "tensor", bc_fm_guard)
                    sl = p.sig("scalar", lambda e, b=cb, d=dst: e.activation(
                        out=d, in_=b, func=AF.Silu))
                    conv_silu_s[pb] = sl
                    if fc == 16:
                        p.wait("gpsimd", "scalar", sl)
                        p.wait("gpsimd", "sync", bb_d_war[gb])
                        p.dma("gpsimd", lambda e, gb=gb: e.dma_start(out=bb_d[gb], in_=B_fm))
                    if tp_xbc is not None:
                        p.wait("sync", "scalar", sl)
                        tap_dma(lambda e, f=fc, d=dst, g=g: e.dma_start(
                            out=tp_xbc[f * 128:(f + 1) * 128, g * GT:(g + 1) * GT], in_=d))
            xs_store_done = p.cnt["gpsimd"]

            zstage = [xdt, ysb]
            zst_par = [0, 0]
            for tc in range(CPG):
                zp = tc % 2
                for ng in range(4):
                    pb = (tc * 4 + ng) % 2
                    bank = [bA0, bA1][pb]
                    bank.acquire()
                    for kc in range(8):
                        p.raw("tensor", lambda e, t=tc, n=ng, k=kc, b=bank, gb=gb: e.matmul(
                            b.ap, lnuT[gb][k][:, t * CH:(t + 1) * CH],
                            W_inT[:, k * DPROJ + n * 512:k * DPROJ + (n + 1) * 512],
                            start=(k == 0), stop=False))
                    pe_s = p.sig("tensor", lambda e, n=ng, b=bank: e.matmul(
                        b.ap, ones_row_bf, b_z_bf[:, n * 512:(n + 1) * 512],
                        start=False, stop=True))
                    p.wait("scalar", "tensor", pe_s)
                    if g > 0 and tc == 0 and ng == 0:
                        p.wait("scalar", "vector", yn_v_last)
                    p.wait("scalar", "gpsimd", zst_par[zp])
                    sz = p.sig("scalar", lambda e, t=tc, n=ng, b=bank, s=zstage[zp]: e.activation(
                        out=s[:, n * 512:(n + 1) * 512], in_=b.ap, func=AF.Silu))
                    bank.release("scalar", sz)
                p.wait("gpsimd", "scalar", sz)
                zst_par[zp] = p.dma("gpsimd", lambda e, t=tc, s=zstage[zp], gb=gb: e.dma_start(
                    out=z_d[gb][t * CH:(t + 1) * CH, :], in_=s))
                if tp_z is not None:
                    p.wait("sync", "scalar", sz)
                    tap_dma(lambda e, t=tc, s=zstage[zp], g=g: e.dma_start(
                        out=tp_z[(g * GT + t * CH):(g * GT + (t + 1) * CH), :], in_=s))
            z_store_done = p.cnt["gpsimd"]

            bB0.acquire()
            pe_dt = 0
            for kc in range(8):
                fn = lambda e, k=kc, gb=gb: e.matmul(
                    PB[0:NH, 0:512], W_inT[:, k * DPROJ + DIN + CONVD:k * DPROJ + DPROJ],
                    lnuT[gb][k], start=(k == 0), stop=(k == 7))
                if kc < 7:
                    p.raw("tensor", fn)
                else:
                    pe_dt = p.sig("tensor", fn)
            lnuT_guard[gb] = pe_dt
            p.wait("scalar", "tensor", pe_dt)
            p.sig("scalar", lambda e: e.activation(
                out=sp_a, in_=PB[0:NH, 0:512], func=AF.Abs, bias=dtb))
            p.sig("scalar", lambda e: e.activation(out=sp_l, in_=sp_a, func=AF.Exp, scale=-1.0))
            p.sig("scalar", lambda e: e.activation(out=sp_a, in_=sp_l, func=AF.Ln, bias=one32c))
            s_r = p.sig("scalar", lambda e: e.activation(
                out=sp_l, in_=PB[0:NH, 0:512], func=AF.Relu, bias=dtb))
            bB0.release("scalar", s_r)
            p.wait("vector", "scalar", s_r)
            v_dt = p.sig("vector", lambda e: e.tensor_tensor(dt_fm_g, sp_a, sp_l, OP.add))
            if tp_dt is not None:
                p.wait("sync", "vector", v_dt)
                tap_dma(lambda e, g=g: e.dma_start(
                    out=tp_dt[:, g * GT:(g + 1) * GT], in_=dt_fm_g))

            if full_groups is not None and g >= full_groups:
                continue
            for cc in range(CPG):
                ch = g * CPG + cc
                cb2 = ch % 2
                if cc == 0:
                    p.wait("gpsimd", "scalar")
                p.sig("vector", lambda e, c=cc: e.tensor_scalar_mul(
                    dA_fm, dt_fm_g[:, c * CH:(c + 1) * CH], A_col))
                p.tsync("vector")
                p.sig("vector", lambda e: e.tensor_tensor_scan(
                    tri64a[0:32, :], ones32x128, dA_fm, 0.0, OP.mult, OP.add))
                p.sig("vector", lambda e, c=cc: e.tensor_copy(
                    out=tri64b[0:32, :], in_=dt_fm_g[:, c * CH:(c + 1) * CH]))
                if tp_dacs is not None:
                    p.wait("sync", "vector")
                    tap_dma(lambda e, c=ch: e.dma_start(
                        out=tp_dacs[:, c * CH:(c + 1) * CH], in_=tri64a[0:32, :]))
                p.wait("scalar", "vector")
                s_hi = p.sig("scalar", lambda e: e.activation(
                    out=lhsE1[0:32, :], in_=tri64a[0:32, :], func=AF.Copy))
                p.wait("vector", "scalar", s_hi)
                p.tsync("vector")
                v_lo = p.sig("vector", lambda e: e.tensor_tensor(
                    lhsE1[32:64, :], tri64a[0:32, :], lhsE1[0:32, :], OP.subtract))
                p.wait("gpsimd", "vector", v_lo)
                p.wait("gpsimd", "sync", e1_d_war[cb2])
                g_st = p.dma("gpsimd", lambda e, b=cb2: e.dma_start(
                    out=e1_d[b], in_=lhsE1[0:64, :]))
                p.wait("sync", "gpsimd", g_st)
                p.wait("sync", "tensor", rhsE1_war)
                rhs_ld = p.dma("sync", lambda e, b=cb2: e.dma_start(
                    out=rhsE1[64:66, :],
                    in_=e1_d[b].rearrange("(s h) i -> s (h i)", s=2)))
                e1_d_war[cb2] = rhs_ld
                p.tsync("vector")
                p.sig("vector", lambda e: e.tensor_copy(
                    out=neg_last, in_=tri64a[0:32, 127:128]))
                p.sig("vector", lambda e: e.tensor_scalar_mul(
                    tri64a[32:64, :], ones32x128, tri64a[0:32, 127:128]))
                p.wait("scalar", "vector")
                s_wd = p.sig("scalar", lambda e: e.activation(
                    out=scr32, in_=tri64a[0:32, :], func=AF.Exp, scale=-1.0, bias=neg_last))
                p.wait("vector", "scalar", s_wd)
                p.wait("vector", "gpsimdc", tok_g_guard)
                v_wdt = p.sig("vector", lambda e: e.tensor_tensor(
                    tri64b[32:64, :], scr32, tri64b[0:32, :], OP.mult))
                p.wait("tensor", "vector", v_wdt)
                bB0.acquire()
                t_t1 = p.sig("tensor", lambda e: e.transpose(
                    PB[:, 0:64], tri64a, ident_f))
                p.wait("vector", "tensor", t_t1)
                v_t1 = p.sig("vector", lambda e: e.tensor_copy(out=tok64a, in_=PB[:, 0:64]))
                bB0.release("vector", v_t1)
                bB0.acquire()
                t_t2 = p.sig("tensor", lambda e: e.transpose(
                    PB[:, 0:64], tri64b, ident_f))
                p.wait("vector", "tensor", t_t2)
                v_t2 = p.sig("vector", lambda e: e.tensor_copy(out=tok64b, in_=PB[:, 0:64]))
                bB0.release("vector", v_t2)
                p.wait("scalar", "vector", v_t1)
                s_ed = p.sig("scalar", lambda e: e.activation(out=Edec, in_=tok64a, func=AF.Exp))

                p.wait("sync", "gpsimd", xs_store_done)
                p.wait("sync", "vector", xk_v)
                p.wait("sync", "gpsimdc", xk_g)
                x_ld = p.dma("sync", lambda e, c=cc, gb=gb: e.dma_start(
                    out=x_tok, in_=xs_d[gb][:, c * CH:(c + 1) * CH], transpose=True))
                xs_d_war[gb] = x_ld
                p.wait("sync", "tensor", btok_pe)
                b_ld = p.dma("sync", lambda e, c=cc, gb=gb: e.dma_start(
                    out=B_tok, in_=bb_d[gb][:, c * CH:(c + 1) * CH], transpose=True))
                bb_d_war[gb] = b_ld
                p.wait("sync", "gpsimd", z_store_done)
                p.wait("sync", "vector", z_rd_war)
                z_ld = p.dma("sync", lambda e, c=cc, gb=gb: e.dma_start(
                    out=z_sil, in_=z_d[gb][c * CH:(c + 1) * CH, :]))
                p.wait("vector", "sync", x_ld)
                p.wait("vector", "gpsimd", z_store_done)
                p.tsync("vector")
                xk_v = p.sig("vector", lambda e: tt_b(
                    e, xdt, x_tok, tok64b[:, 0:32], OP.mult))
                p.wait("gpsimd", "sync", x_ld)
                p.wait("gpsimd", "vector", v_t2)
                g_xdtd = p.sig("gpsimd", lambda e: tt_b(
                    e, xdtd, x_tok, tok64b[:, 32:64], OP.mult))
                g_xd = p.sig("gpsimd", lambda e: tt_b(
                    e, xD, x_tok, D_cols, OP.mult))
                xk_g = g_xd
                tok_g_guard = g_xdtd
                bB0.acquire()
                t_cbt = p.sig("tensor", lambda e, c=cc: e.matmul(
                    PB[:, 0:128], B_fm[:, c * CH:(c + 1) * CH], C_fm[:, c * CH:(c + 1) * CH],
                    start=True, stop=True))
                p.wait("vector", "tensor", t_cbt)
                v_cbt = p.sig("vector", lambda e: e.tensor_tensor(
                    CBTm, PB[:, 0:128], tri_m, OP.mult))
                bB0.release("vector", v_cbt)

                p.wait("tensor", "sync", rhs_ld)
                pe_yoff = pe_ydiag = pe_st = 0
                for hf in range(2):
                    ho = hf * 1024
                    for q4 in range(4):
                        qb = q4 % 2
                        bank = [bB0, bB1][qb]
                        bank.acquire()
                        p.raw("tensor", lambda e, h=hf, q=q4, b=bank: e.matmul(
                            b.ap, lhsE1,
                            rhsE1[:, (h * 16 + q * 4) * 128:(h * 16 + q * 4 + 4) * 128],
                            start=True, stop=False))
                        pe_e1 = p.sig("tensor", lambda e, b=bank: e.matmul(
                            b.ap, strictu, maski, start=False, stop=True))
                        rhsE1_war = pe_e1
                        p.wait("scalar", "tensor", pe_e1)
                        p.wait("scalar", "vector", es_v[qb])
                        s_es = p.sig("scalar", lambda e, q=q4, b=bank: e.activation(
                            out=es[q % 2], in_=b.ap, func=AF.Exp))
                        bank.release("scalar", s_es)
                        p.wait("vector", "scalar", s_es)
                        p.wait("vector", "tensor", mp_pe[qb])
                        p.tsync("vector")
                        v_mp = p.sig("vector", lambda e, q=q4: e.scalar_tensor_tensor(
                            Mp[q % 2].rearrange("q (a b) -> q a b", a=4),
                            es[q % 2].rearrange("q (a b) -> q a b", a=4), 1e30,
                            CBTm.unsqueeze(1).broadcast_to([128, 4, 128]),
                            OP.min, OP.mult))
                        es_v[qb] = v_mp
                        p.wait("tensor", "vector", v_mp)
                        if q4 == 0:
                            bC.acquire()
                        for hh in range(4):
                            h16 = q4 * 4 + hh
                            h = hf * 16 + h16
                            fn = lambda e, q=q4, hh=hh, h=h, h16=h16: e.matmul(
                                PC[:, h16 * 64:(h16 + 1) * 64],
                                Mp[q % 2][:, hh * 128:(hh + 1) * 128],
                                xdt[:, h * 64:(h + 1) * 64],
                                start=(h16 % 8 == 0), stop=False)
                            if hh < 3:
                                p.raw("tensor", fn)
                            else:
                                mp_pe[qb] = p.sig("tensor", fn)
                    p.wait("tensor", "gpsimdc", g_xd)
                    for nn in range(2):
                        fn = lambda e, n=nn, o=ho: e.matmul(
                            PC[:, n * 512:(n + 1) * 512], ident_b,
                            xD[:, o + n * 512:o + (n + 1) * 512], start=False, stop=True)
                        if nn == 0:
                            p.raw("tensor", fn)
                        else:
                            pe_ydiag = p.sig("tensor", fn)
                    bD.acquire()
                    for nn in range(2):
                        fn = lambda e, n=nn, c=cc, o=ho: e.matmul(
                            PD[:, n * 512:(n + 1) * 512], C_fm[:, c * CH:(c + 1) * CH],
                            S_bf[:, o + n * 512:o + (n + 1) * 512], start=True, stop=True)
                        if nn == 0:
                            p.raw("tensor", fn)
                        else:
                            pe_yoff = p.sig("tensor", fn)
                    p.wait("vector", "tensor", pe_yoff)
                    p.wait("vector", "scalar", s_ed)
                    v_o1 = p.sig("vector", lambda e, h=hf: tt_b(
                        e, tmp1k, PD, Edec[:, h * 16:(h + 1) * 16], OP.mult))
                    bD.release("vector", v_o1)
                    bD.acquire()
                    p.wait("tensor", "sync", b_ld)
                    p.wait("tensor", "gpsimdc", g_xdtd)
                    for nn in range(2):
                        fn = lambda e, n=nn, o=ho: e.matmul(
                            PD[:, n * 512:(n + 1) * 512], B_tok,
                            xdtd[:, o + n * 512:o + (n + 1) * 512], start=True, stop=True)
                        if nn == 0:
                            p.raw("tensor", fn)
                        else:
                            pe_st = p.sig("tensor", fn)
                    btok_pe = pe_st
                    p.wait("vector", "tensor", pe_ydiag)
                    v_o2 = p.sig("vector", lambda e, o=ho: e.tensor_tensor(
                        ysb[:, o:o + 1024], tmp1k, PC, OP.add))
                    bC.release("vector", v_o2)
                    p.sig("vector", lambda e, h=hf, o=ho: tt_b(
                        e, tmp1k, S_f[:, o:o + 1024],
                        Edec[:, 32 + h * 16:32 + (h + 1) * 16], OP.mult))
                    p.wait("vector", "tensor", pe_st)
                    v_sf = p.sig("vector", lambda e, o=ho: e.tensor_tensor(
                        S_f[:, o:o + 1024], tmp1k, PD, OP.add))
                    bD.release("vector", v_sf)
                    p.wait("scalar", "vector", v_sf)
                    p.wait("scalar", "tensor", pe_yoff)
                    p.sig("scalar", lambda e, o=ho: e.activation(
                        out=S_bf[:, o:o + 1024], in_=S_f[:, o:o + 1024], func=AF.Copy))
                bc_fm_guard = pe_yoff

                if tp_y is not None:
                    p.wait("sync", "vector")
                    tap_dma(lambda e, c=ch: e.dma_start(
                        out=tp_y[c * CH:(c + 1) * CH, :], in_=ysb))
                p.wait("vector", "sync", z_ld)
                v_y2 = p.sig("vector", lambda e: e.tensor_tensor(
                    y2_bf, ysb, z_sil, OP.mult))
                z_rd_war = v_y2
                p.wait("scalar", "vector", v_y2)
                p.sig("scalar", lambda e: e.activation(
                    out=xdtd, in_=y2_bf, func=AF.Square, accum_out=stat[:, 7:8]))
                p.tsync("scalar")
                p.sig("scalar", lambda e: e.activation(
                    out=stat[:, 8:9], in_=stat[:, 7:8], func=AF.Ln,
                    bias=eps_rms_col, scale=1.0 / DIN))
                p.tsync("scalar")
                s_rs = p.sig("scalar", lambda e: e.activation(
                    out=stat[:, 9:10], in_=stat[:, 8:9], func=AF.Exp, scale=-0.5))
                p.wait("vector", "scalar", s_rs)
                p.wait("vector", "gpsimd", max(yn_st[cc % 2], lnu_st_par[cc % 2]))
                v_yn = p.sig("vector", lambda e, c=cc: e.scalar_tensor_tensor(
                    y_n[c % 2], y2_bf, stat[:, 9:10], nw_bc, OP.mult, OP.mult))
                yn_v_last = v_yn
                p.wait("gpsimd", "vector", v_yn)
                p.wait("gpsimd", "sync", yb_d_war[gb])
                yn_st[cc % 2] = p.dma("gpsimd", lambda e, c=cc, gb=gb: e.dma_start(
                    out=yb_d[gb][c * CH:(c + 1) * CH, :], in_=y_n[c % 2]))
                if tp_yn is not None:
                    p.wait("sync", "vector", v_yn)
                    tap_dma(lambda e, c=ch, cc=cc: e.dma_start(
                        out=tp_yn[c * CH:(c + 1) * CH, :], in_=y_n[cc % 2]))

            p.wait("tensor", "vector")
            p.wait("tensor", "scalar")
            pe_op = 0
            for fp in range(2):
                p.wait("sync", "gpsimd", max(yn_st))
                p.wait("sync", "tensor")
                ynT_ld = 0
                for f8 in range(8):
                    ynT_ld = p.dma("sync", lambda e, f=f8, o=fp * 8, gb=gb: e.dma_start(
                        out=ynT[f], in_=yb_d[gb][:, (o + f) * 128:(o + f + 1) * 128],
                        transpose=True))
                yb_d_war[gb] = ynT_ld
                p.wait("gpsimd", "tensor", ynT_guard)
                wout_ld = 0
                for f8 in range(8):
                    wout_ld = p.dma("gpsimd", lambda e, f=f8, o=fp * 8: e.dma_start(
                        out=wout_b[f], in_=W_outT_h[(o + f) * 128:(o + f + 1) * 128, :]))
                p.wait("tensor", "sync", ynT_ld)
                p.wait("tensor", "gpsimd", wout_ld)
                mm_fns = []
                for tc in range(CPG):
                    bank_ap = [PA, PB, PC, PD][tc]
                    for ng in range(2):
                        for f8 in range(8):
                            mm_fns.append(lambda e, b=bank_ap, t=tc, n=ng, f=f8,
                                          fp=fp: e.matmul(
                                b[:, n * 512:(n + 1) * 512],
                                ynT[f][:, t * CH:(t + 1) * CH],
                                wout_b[f][:, n * 512:(n + 1) * 512],
                                start=(fp == 0 and f == 0), stop=False))
                        if fp == 1:
                            mm_fns.append(lambda e, b=bank_ap, n=ng: e.matmul(
                                b[:, n * 512:(n + 1) * 512], ones_row_bf,
                                b_out_bf[:, n * 512:(n + 1) * 512], start=False, stop=True))
                for fn_ in mm_fns[:-1]:
                    p.raw("tensor", fn_)
                pe_op = p.sig("tensor", mm_fns[-1])
                ynT_guard = pe_op
            evac_v = []
            for tc in range(CPG):
                bank_ap = [PA, PB, PC, PD][tc]
                p.wait("vector", "tensor", pe_op)
                p.wait("scalar", "tensor", pe_op)
                s_rstd = emit_ln_stats(bank_ap, DM, eps_ln_col, tmp1k, psum_src=True)
                p.wait("vector", "scalar", s_rstd)
                v_ml = p.sig("vector", lambda e, b=bank_ap: e.tensor_scalar(
                    mix_ln, b, stat[:, 2:3], stat[:, 6:7], OP.subtract, OP.mult))
                evac_v.append(v_ml)
                p.wait("sync", "vector", u_war_v[tc % 2])
                u_ld2 = p.dma("sync", lambda e, t=tc, g=g: e.dma_start(
                    out=u_t[t % 2], in_=u_h[(g * GT + t * CH):(g * GT + (t + 1) * CH), :]))
                p.wait("vector", "sync", u_ld2)
                p.wait("vector", "gpsimd", x1_par[tc % 2])
                v_x1 = p.sig("vector", lambda e, t=tc: e.tensor_tensor(
                    x1_t[t % 2], mix_ln, u_t[t % 2], OP.add))
                u_war_v[tc % 2] = v_x1
                p.wait("gpsimd", "vector", v_x1)
                x1_par[tc % 2] = p.dma("gpsimd", lambda e, t=tc, g=g: e.dma_start(
                    out=x1_d[(g * GT + t * CH):(g * GT + (t + 1) * CH), :], in_=x1_t[t % 2]))
                if tp_x1 is not None:
                    p.wait("sync", "vector", v_x1)
                    tap_dma(lambda e, t=tc, g=g: e.dma_start(
                        out=tp_x1[(g * GT + t * CH):(g * GT + (t + 1) * CH), :], in_=x1_t[t % 2]))
                p.wait("scalar", "vector", v_x1)
                s_rstd2 = emit_ln_stats(x1_t[tc % 2], DM, eps_ln_col, tmp1k)
                p.wait("vector", "scalar", s_rstd2)
                p.wait("vector", "gpsimd", hb_par[tc % 2])
                v_h = p.sig("vector", lambda e, t=tc: e.tensor_scalar(
                    h_bf[t % 2], x1_t[t % 2], stat[:, 2:3], stat[:, 6:7], OP.subtract, OP.mult))
                p.wait("gpsimd", "vector", v_h)
                hb_par[tc % 2] = p.dma("gpsimd", lambda e, t=tc, g=g: e.dma_start(
                    out=hb_d[(g * GT + t * CH):(g * GT + (t + 1) * CH), :], in_=h_bf[t % 2]))
                if tp_h is not None:
                    p.wait("sync", "vector", v_h)
                    tap_dma(lambda e, t=tc, g=g: e.dma_start(
                        out=tp_h[(g * GT + t * CH):(g * GT + (t + 1) * CH), :], in_=h_bf[t % 2]))
            bA0.evac = bA1.evac = ("vector", evac_v[0])
            bB0.evac = bB1.evac = ("vector", evac_v[1])
            bC.evac = ("vector", evac_v[2])
            bD.evac = ("vector", evac_v[3])

        hb_done = p.cnt["gpsimd"]

    if upto == "a":
        p.wait("gpsimd", "sync", p.cnt["sync"])
        p.wait("gpsimd", "gpsimd", p.cnt["gpsimd"])
        p.raw("gpsimd", lambda e: e.nop())
        return p, T

    with contextlib.ExitStack() as bstack:
        bsb = lambda name, shape, dt=F32: bstack.enter_context(
            nc.sbuf_tensor(name, list(shape), dt)).ap()

        hT = [bsb(f"hT{d}", [128, GT], BF16) for d in range(8)]
        gw = [bsb(f"gw{i}", [128, 8 * IEXP], BF16) for i in range(2)]
        uw = [bsb(f"uw{i}", [128, 8 * IEXP], BF16) for i in range(2)]
        de4 = bsb("de4", [128, 16 * DM], BF16)
        actAll = bsb("actAll", [128, NPE * 4 * 512], BF16)
        scr512 = bsb("scr512", [128, 512], BF16)
        scr512b = bsb("scr512b", [128, 512], BF16)
        scr8 = bsb("scr8", [128, 8])
        scr8b = bsb("scr8b", [128, 8])
        scr8c = bsb("scr8c", [128, 8])
        w_q = bsb("w_q", [128, 4 * NEXP])
        w8c = bsb("w8c", [128, NEXP], BF16)
        w8T = bsb("w8T", [16, GT], BF16)
        ffn_ln = bsb("ffn_ln", [128, DM])
        out_t = [bsb(f"out_t{i}", [128, DM]) for i in range(2)]

        p.wait("gpsimd", "gpsimd", ffn_precast)
        p.wait("sync", "gpsimd", hb_done)
        p.wait("sync", "vector")
        p.sig("vector", lambda e: e.memset(w8T, 0.0))

        dn_banks = [(Bank(p, PA), (bA0, bA1)), (Bank(p, PB), (bB0, bB1)),
                    (Bank(p, PC), (bC,)), (Bank(p, PD), (bD,))]
        out_st = [0, 0]
        hT_pe_guard = 0
        w8T_pe_guard = 0
        act_pe_guard = 0
        gw_pe_guard = [0, 0]
        de4_pe_guard = 0
        silu_v_guard = 0
        x1_v_guard = [0, 0]

        for q in range(ngroups):
            qb = q % 2
            p.wait("sync", "tensor", hT_pe_guard)
            hT_ld = 0
            for dc in range(8):
                hT_ld = p.dma("sync", lambda e, d=dc, q=q: e.dma_start(
                    out=hT[d], in_=hb_d[q * GT:(q + 1) * GT, d * 128:(d + 1) * 128],
                    transpose=True))
            for tc in range(CPG):
                bD.acquire()
                p.wait("tensor", "sync", hT_ld)
                pe_r = 0
                for kc in range(8):
                    fn = lambda e, t=tc, k=kc: e.matmul(
                        PD[:, 0:NEXP], hT[k][:, t * CH:(t + 1) * CH],
                        rw[:, k * NEXP:(k + 1) * NEXP], start=(k == 0), stop=(k == 7))
                    if kc < 7:
                        p.raw("tensor", fn)
                    else:
                        pe_r = p.sig("tensor", fn)
                p.wait("vector", "tensor", pe_r)
                v_mx = p.sig("vector", lambda e: e.tensor_reduce(
                    stat[:, 9:10], PD[:, 0:NEXP], AX.X, OP.max))
                p.tsync("vector")
                v_nm = p.sig("vector", lambda e: e.tensor_scalar_mul(
                    stat[:, 10:11], stat[:, 9:10], -1.0))
                p.wait("scalar", "vector", v_nm)
                s_ex = p.sig("scalar", lambda e: e.activation(
                    out=scr8, in_=PD[:, 0:NEXP], func=AF.Exp, bias=stat[:, 10:11]))
                bD.release("scalar", s_ex)
                p.wait("vector", "scalar", s_ex)
                p.sig("vector", lambda e: e.tensor_reduce(stat[:, 11:12], scr8, AX.X, OP.add))
                p.tsync("vector")
                p.sig("vector", lambda e: e.reciprocal(stat[:, 12:13], stat[:, 11:12]))
                p.tsync("vector")
                p.sig("vector", lambda e, t=tc: e.tensor_scalar_mul(
                    w_q[:, t * NEXP:(t + 1) * NEXP], scr8, stat[:, 12:13]))
                p.tsync("vector")
                p.sig("vector", lambda e, t=tc: e.tensor_scalar(
                    scr8, w_q[:, t * NEXP:(t + 1) * NEXP], rb_col, None, OP.add))
                cur = 0
                for k4 in range(4):
                    srcv = [scr8, scr8b][cur]
                    p.tsync("vector")
                    p.sig("vector", lambda e, s=srcv: e.tensor_reduce(
                        stat[:, 13:14], s, AX.X, OP.max))
                    if k4 < 3:
                        dstp = [scr8b, scr8][cur]
                        p.tsync("vector")
                        p.sig("vector", lambda e, s=srcv: e.tensor_scalar(
                            scr8c, s, stat[:, 13:14], None, OP.is_ge))
                        p.tsync("vector")
                        p.sig("vector", lambda e, s=srcv, d=dstp: e.scalar_tensor_tensor(
                            d, scr8c, -2.0, s, OP.mult, OP.add))
                        cur = 1 - cur
                p.tsync("vector")
                p.sig("vector", lambda e, t=tc: e.tensor_scalar(
                    scr8b, w_q[:, t * NEXP:(t + 1) * NEXP], rb_col, None, OP.add))
                p.tsync("vector")
                p.sig("vector", lambda e: e.tensor_scalar(
                    scr8c, scr8b, stat[:, 13:14], None, OP.is_ge))
                p.tsync("vector")
                p.sig("vector", lambda e, t=tc: e.tensor_tensor(
                    scr8, w_q[:, t * NEXP:(t + 1) * NEXP], scr8c, OP.mult))
                p.tsync("vector")
                p.sig("vector", lambda e: e.tensor_reduce(stat[:, 11:12], scr8, AX.X, OP.add))
                p.tsync("vector")
                p.sig("vector", lambda e: e.tensor_scalar(
                    stat[:, 14:15], stat[:, 11:12], 1e-20, None, OP.add))
                p.tsync("vector")
                p.sig("vector", lambda e: e.reciprocal(stat[:, 12:13], stat[:, 14:15]))
                p.tsync("vector")
                v_w = p.sig("vector", lambda e, t=tc: e.tensor_scalar_mul(
                    w_q[:, t * NEXP:(t + 1) * NEXP], scr8, stat[:, 12:13]))
                p.wait("scalar", "vector", v_w)
                s_wc = p.sig("scalar", lambda e, t=tc: e.activation(
                    out=w8c, in_=w_q[:, t * NEXP:(t + 1) * NEXP], func=AF.Copy))
                bB0.acquire()
                p.wait("tensor", "scalar", s_wc)
                t_wt = p.sig("tensor", lambda e: e.transpose(
                    PB.bitcast(BF16)[0:NEXP, 0:128], w8c, ident_b))
                p.wait("vector", "tensor", t_wt)
                p.wait("vector", "tensor", w8T_pe_guard)
                v_wt = p.sig("vector", lambda e, t=tc: e.tensor_copy(
                    out=w8T[0:NEXP, t * CH:(t + 1) * CH], in_=PB.bitcast(BF16)[0:NEXP, 0:128]))
                bB0.release("vector", v_wt)
                if tp_w is not None:
                    p.wait("sync", "vector", v_w)
                    tap_dma(lambda e, t=tc, q=q: e.dma_start(
                        out=tp_w[(q * GT + t * CH):(q * GT + (t + 1) * CH), :],
                        in_=w_q[:, t * NEXP:(t + 1) * NEXP]))
            w8T_v = p.cnt["vector"]

            p.wait("vector", "tensor", act_pe_guard)
            for pe_i in range(NPE):
                wb = pe_i % 2
                p.wait("sync", "tensor", gw_pe_guard[wb])
                if pe_i < NEXP:
                    p.dma("sync", lambda e, i=pe_i, b=wb: e.dma_start(
                        out=gw[b], in_=gate_bd[i]))
                    g_ld = p.dma("sync", lambda e, i=pe_i, b=wb: e.dma_start(
                        out=uw[b], in_=up_bd[i]))
                else:
                    sp = pe_i - NEXP
                    p.dma("sync", lambda e, s=sp, b=wb: e.dma_start(
                        out=gw[b].rearrange("q (dc m) -> q dc m", dc=8),
                        in_=sg_bd.rearrange("q (dc m) -> q dc m", dc=8)[:, :, s * 512:(s + 1) * 512]))
                    g_ld = p.dma("sync", lambda e, s=sp, b=wb: e.dma_start(
                        out=uw[b].rearrange("q (dc m) -> q dc m", dc=8),
                        in_=su_bd.rearrange("q (dc m) -> q dc m", dc=8)[:, :, s * 512:(s + 1) * 512]))
                if pe_i < NEXP:
                    bC.acquire()
                    p.wait("tensor", "vector", w8T_v)
                    pe_wbc = p.sig("tensor", lambda e, i=pe_i: e.matmul(
                        PC[:, 0:512], sel16[:, i * 128:(i + 1) * 128], w8T,
                        start=True, stop=True))
                    w8T_pe_guard = pe_wbc
                p.wait("tensor", "sync", g_ld)
                for ic in range(4):
                    pbk = ic % 2
                    bkg = [bA0, bA1][pbk]
                    bku = [bB0, bB1][pbk]
                    bkg.acquire()
                    pe_g = 0
                    for kc in range(8):
                        fn = lambda e, i=ic, k=kc, b=bkg, w=wb: e.matmul(
                            b.ap, gw[w][:, k * 512 + i * 128:k * 512 + (i + 1) * 128],
                            hT[k], start=(k == 0), stop=(k == 7))
                        if kc < 7:
                            p.raw("tensor", fn)
                        else:
                            pe_g = p.sig("tensor", fn)
                    bku.acquire()
                    pe_u = 0
                    for kc in range(8):
                        fn = lambda e, i=ic, k=kc, b=bku, w=wb: e.matmul(
                            b.ap, uw[w][:, k * 512 + i * 128:k * 512 + (i + 1) * 128],
                            hT[k], start=(k == 0), stop=(k == 7))
                        if kc < 7:
                            p.raw("tensor", fn)
                        else:
                            pe_u = p.sig("tensor", fn)
                    gw_pe_guard[wb] = pe_u
                    p.wait("scalar", "tensor", pe_g)
                    p.wait("scalar", "vector", silu_v_guard)
                    s_sg = p.sig("scalar", lambda e, b=bkg: e.activation(
                        out=scr512, in_=b.ap, func=AF.Silu))
                    bkg.release("scalar", s_sg)
                    p.wait("vector", "scalar", s_sg)
                    p.wait("vector", "tensor", pe_u)
                    blk = pe_i * 4 + ic
                    if pe_i < NEXP:
                        v_a1 = p.sig("vector", lambda e, b=bku: e.tensor_tensor(
                            scr512b, scr512, b.ap, OP.mult))
                        bku.release("vector", v_a1)
                        silu_v_guard = v_a1
                        v_a2 = p.sig("vector", lambda e, k=blk: e.tensor_tensor(
                            actAll[:, k * 512:(k + 1) * 512], scr512b, PC[:, 0:512], OP.mult))
                        if ic == 3:
                            bC.release("vector", v_a2)
                    else:
                        v_a2 = p.sig("vector", lambda e, k=blk, b=bku: e.tensor_tensor(
                            actAll[:, k * 512:(k + 1) * 512], scr512, b.ap, OP.mult))
                        bku.release("vector", v_a2)
                        silu_v_guard = v_a2
            acts_done_v = p.cnt["vector"]
            acts_done_s = p.cnt["scalar"]

            p.wait("tensor", "vector", acts_done_v)
            p.wait("tensor", "scalar", acts_done_s)
            pe_down_last = 0
            for ps_i in range(3):
                p.wait("sync", "tensor", de4_pe_guard)
                d_ld = 0
                for j in range(4):
                    pe_j = ps_i * 4 + j
                    if pe_j < NEXP:
                        d_ld = p.dma("sync", lambda e, i=pe_j, j=j: e.dma_start(
                            out=de4[:, j * 4 * DM:(j + 1) * 4 * DM], in_=down_bd[i]))
                    else:
                        sp = pe_j - NEXP
                        d_ld = p.dma("sync", lambda e, s=sp, j=j: e.dma_start(
                            out=de4[:, j * 4 * DM:(j + 1) * 4 * DM],
                            in_=sd_bd[:, s * 4 * DM:(s + 1) * 4 * DM]))
                p.wait("tensor", "sync", d_ld)
                mm_fns = []
                for tc in range(CPG):
                    bank_ap = [PA, PB, PC, PD][tc]
                    for j in range(4):
                        pe_j = ps_i * 4 + j
                        for ic in range(4):
                            blk = pe_j * 4 + ic
                            for nn in range(2):
                                first = (ps_i == 0 and j == 0 and ic == 0)
                                last = (ps_i == 2 and j == 3 and ic == 3)
                                mm_fns.append(lambda e, b=bank_ap, j=j, ic=ic, nn=nn,
                                      t=tc, k=blk, f=first, l=last: e.matmul(
                                    b[:, nn * 512:(nn + 1) * 512],
                                    actAll[:, k * 512 + t * CH:k * 512 + (t + 1) * CH],
                                    de4[:, (j * 4 + ic) * DM + nn * 512:
                                         (j * 4 + ic) * DM + (nn + 1) * 512],
                                    start=f, stop=l))
                for fn_ in mm_fns[:-1]:
                    p.raw("tensor", fn_)
                pe_down_last = p.sig("tensor", mm_fns[-1])
                de4_pe_guard = pe_down_last
            act_pe_guard = pe_down_last
            hT_pe_guard = pe_down_last

            evac_sigs = []
            for tc in range(CPG):
                bank_ap = [PA, PB, PC, PD][tc]
                p.wait("vector", "tensor", pe_down_last)
                p.wait("scalar", "tensor", pe_down_last)
                if tp_ffn is not None:
                    p.sig("vector", lambda e, b=bank_ap: e.tensor_copy(out=ffn_ln, in_=b))
                    p.wait("sync", "vector")
                    tap_dma(lambda e, t=tc, q=q: e.dma_start(
                        out=tp_ffn[(q * GT + t * CH):(q * GT + (t + 1) * CH), :], in_=ffn_ln))
                s_rstd = emit_ln_stats(bank_ap, DM, eps_ln_col, ffn_ln, psum_src=True)
                p.wait("vector", "scalar", s_rstd)
                v_ln = p.sig("vector", lambda e, b=bank_ap: e.tensor_scalar(
                    ffn_ln, b, stat[:, 2:3], stat[:, 6:7], OP.subtract, OP.mult))
                evac_sigs.append(v_ln)
                p.wait("sync", "vector", x1_v_guard[tc % 2])
                x1_ld = p.dma("sync", lambda e, t=tc, q=q: e.dma_start(
                    out=x1_t[t % 2], in_=x1_d[(q * GT + t * CH):(q * GT + (t + 1) * CH), :]))
                p.wait("vector", "sync", x1_ld)
                p.wait("vector", "gpsimd", out_st[tc % 2])
                v_out = p.sig("vector", lambda e, t=tc: e.tensor_tensor(
                    out_t[t % 2], ffn_ln, x1_t[t % 2], OP.add))
                x1_v_guard[tc % 2] = v_out
                p.wait("gpsimd", "vector", v_out)
                out_st[tc % 2] = p.dma("gpsimd", lambda e, t=tc, q=q: e.dma_start(
                    out=out_h[(q * GT + t * CH):(q * GT + (t + 1) * CH), :], in_=out_t[t % 2]))
            for tc, (full, parts) in enumerate(dn_banks):
                for b_ in parts:
                    b_.release("vector", evac_sigs[tc])

        p.wait("gpsimd", "gpsimd", p.cnt["gpsimd"])
        p.raw("gpsimd", lambda e: e.nop())

    return p, T


def host_layout(inputs):
    f32c = lambda a: np.ascontiguousarray(np.asarray(a), dtype=np.float32)
    W_in = f32c(inputs["L0_attn_W_in"])
    W_out = f32c(inputs["L0_attn_W_out"])
    conv_w = f32c(inputs["L0_attn_conv_w"])
    conv_b = f32c(inputs["L0_attn_conv_b"])
    b_in = f32c(inputs["L0_attn_b_in"])
    sel = np.zeros((64, NH * 128), np.float32)
    for r in range(64):
        h = r % 32
        sel[r, h * 128:(h + 1) * 128] = -1.0
    sel16 = np.zeros((16, NEXP * 128), np.float32)
    for e_ in range(NEXP):
        sel16[e_, e_ * 128:(e_ + 1) * 128] = 1.0
    strict_u = (np.arange(128)[:, None] < np.arange(128)[None, :]).astype(np.float32)
    mask_i = np.tile(-1e38 * np.eye(128, dtype=np.float32), (1, 4))
    common = {
        "W_inT": np.ascontiguousarray(W_in.T),
        "W_outT": np.ascontiguousarray(W_out.T),
        "conv_w_fm": np.ascontiguousarray(
            conv_w.reshape(18, 128, KCONV).transpose(1, 0, 2).reshape(128, 18 * KCONV)),
        "conv_b_fm": np.ascontiguousarray(conv_b.reshape(18, 128).T),
        "b_xbc_fm": np.ascontiguousarray(b_in[DIN:DIN + CONVD].reshape(18, 128).T),
        "b_z_row": np.ascontiguousarray(b_in[:DIN].reshape(1, DIN)),
        "b_dt_col": np.ascontiguousarray(b_in[DIN + CONVD:].reshape(NH, 1)),
        "dt_bias_col": np.ascontiguousarray(f32c(inputs["L0_attn_dt_bias"]).reshape(NH, 1)),
        "A_log_col": np.ascontiguousarray(f32c(inputs["L0_attn_A_log"]).reshape(NH, 1)),
        "D_row": np.ascontiguousarray(f32c(inputs["L0_attn_D"]).reshape(1, NH)),
        "nw_row": np.ascontiguousarray(f32c(inputs["L0_attn_norm_w"]).reshape(1, DIN)),
        "b_out_row": np.ascontiguousarray(f32c(inputs["L0_attn_b_out"]).reshape(1, DM)),
        "router_bias11": np.ascontiguousarray(
            f32c(inputs["L0_ffn_router_bias"]).reshape(1, 1)),
        "router_w_fm": np.ascontiguousarray(
            f32c(inputs["L0_ffn_router_w"]).reshape(8, 128, NEXP).transpose(1, 0, 2)
            .reshape(128, 8 * NEXP)),
        "gate_w": f32c(inputs["L0_ffn_gate_w"]),
        "up_w": f32c(inputs["L0_ffn_up_w"]),
        "down_w": f32c(inputs["L0_ffn_down_w"]),
        "s_gate_w": f32c(inputs["L0_ffn_s_gate_w"]),
        "s_up_w": f32c(inputs["L0_ffn_s_up_w"]),
        "s_down_w": f32c(inputs["L0_ffn_s_down_w"]),
        "ident": np.eye(64, dtype=np.float32),
        "ident_bf": np.eye(128, dtype=np.float32).astype(BF16NP),
        "tri": np.ascontiguousarray(np.tril(np.ones((128, 128), np.float32)).T).astype(BF16NP),
        "sel_bf": sel.astype(BF16NP),
        "sel16_bf": sel16.astype(BF16NP),
        "strictu_bf": strict_u.astype(BF16NP),
        "maski_bf": mask_i.astype(BF16NP),
        "ones32": np.ones((32, 128), np.float32),
    }
    u = f32c(inputs["L0_attn_u"])
    return u, common


def kernel(**inputs):
    u, common = host_layout(inputs)
    nc = bass.Bass("TRN2", target_bir_lowering=False)
    p, T = build_core(nc)
    p.build()
    in_maps = []
    for c in range(8):
        m = dict(common)
        m["u"] = np.ascontiguousarray(u[c])
        in_maps.append(m)
    res = run_bass_kernel_spmd(nc, in_maps, core_ids=list(range(8)))
    out = np.stack([np.asarray(r["out"], dtype=np.float32) for r in res.results], axis=0)
    return out


# version 06
# speedup vs baseline: 1.0585x; 1.0585x over previous
import contextlib
import numpy as np
import ml_dtypes
import concourse.bass as bass
import concourse.mybir as mybir
from concourse.bass_utils import run_bass_kernel_spmd
from concourse.alu_op_type import AluOpType as OP

F32 = mybir.dt.float32
BF16 = mybir.dt.bfloat16
AF = mybir.ActivationFunctionType
AX = mybir.AxisListType
BF16NP = ml_dtypes.bfloat16

L = 4096
DM = 1024
DIN = 2048
DST = 128
NH = 32
HD = 64
KCONV = 4
CONVD = DIN + 2 * DST
DPROJ = 2 * DIN + 2 * DST + NH
CH = 128
GT = 512
NGRP = L // GT
CPG = GT // CH
NEXP = 8
IEXP = 512
ISH = 2048
NPE = 12
EPS_LN = 1e-5
EPS_RMS = 1e-12
PRE_HT = True
PRE_GU = True
PRE_DE4 = True
TWT_PD7 = False

ENGS = ("sync", "tensor", "vector", "scalar", "gpsimd")


class Prog:
    def __init__(self, nc):
        self.nc = nc
        self.thunks = {e: [] for e in ENGS}
        self.cnt = {e: 0 for e in ENGS}
        self.cnt["gpsimdc"] = 0
        self.sem = {}
        self.last_fence = {}

    def raw(self, eng, fn):
        self.thunks[eng].append((fn, 0, None))

    def sig(self, eng, fn, inc=1):
        ch = "gpsimdc" if (eng == "gpsimd" and inc == 1) else eng
        if inc == 1 and eng in ("vector", "scalar", "gpsimd"):
            v = self.cnt[ch]
            if v > 0 and self.last_fence.get(ch + "_self", -1) < v:
                self.thunks[eng].append(
                    (lambda e, s=ch, vv=v: e.wait_ge(self.sem[s], vv), 0, None))
                self.last_fence[ch + "_self"] = v
        self.thunks[eng].append((fn, inc, ch))
        self.cnt[ch] += inc
        return self.cnt[ch]

    def dma(self, eng, fn):
        ch = eng
        self.thunks[eng].append((fn, 16, ch))
        self.cnt[ch] += 16
        return self.cnt[ch]

    def tsync(self, eng):
        self.wait(eng, eng, self.cnt[eng])

    def wait(self, eng, src, val=None):
        if val is None or src in ("sync", "gpsimd"):
            val = self.cnt[src]
            if src in ("sync", "gpsimd") and src != eng:
                if self.last_fence.get(src, -1) < val:
                    self.thunks[src].append(
                        (lambda e, s=src, v=val: e.wait_ge(self.sem[s], v), 0, None))
                    self.last_fence[src] = val
        if val <= 0:
            return
        self.thunks[eng].append((lambda e, s=src, v=val: e.wait_ge(self.sem[s], v), 0, None))

    def build(self):
        nc = self.nc
        with contextlib.ExitStack() as stack:
            for e in list(ENGS) + ["gpsimdc"]:
                self.sem[e] = stack.enter_context(nc.semaphore(f"sem_{e}"))
            with nc.Block() as block:
                for e in ENGS:
                    def body(eng, e=e):
                        for fn, inc, ch in self.thunks[e]:
                            ins = fn(eng)
                            if inc:
                                ins.then_inc(self.sem[ch], inc)
                    getattr(block, e)(body)


class Bank:

    def __init__(self, p, ap):
        self.p = p
        self.ap = ap
        self.evac = None

    def acquire(self):
        if self.evac is not None:
            self.p.wait("tensor", *self.evac)
            self.evac = None

    def release(self, eng, sig):
        self.evac = (eng, sig)


def build_core(nc, dbg=(), upto="all", ngroups=NGRP, full_groups=None):
    p = Prog(nc)
    dbgset = set(dbg)
    T = {}

    def din(name, shape, dt=F32):
        T[name] = nc.dram_tensor(name, list(shape), dt, kind="ExternalInput").ap()
        return T[name]

    def dout(name, shape, dt=F32):
        T[name] = nc.dram_tensor(name, list(shape), dt, kind="ExternalOutput").ap()
        return T[name]

    def dint(name, shape, dt):
        T[name] = nc.dram_tensor(name, list(shape), dt, kind="Internal").ap()
        return T[name]

    u_h = din("u", [L, DM])
    W_inT_h = din("W_inT", [DM, DPROJ])
    W_outT_h = din("W_outT", [DIN, DM])
    conv_w_h = din("conv_w_fm", [128, 18 * KCONV])
    conv_b_h = din("conv_b_fm", [128, 18])
    b_xbc_h = din("b_xbc_fm", [128, 18])
    b_z_h = din("b_z_row", [1, DIN])
    b_dt_h = din("b_dt_col", [NH, 1])
    dt_bias_h = din("dt_bias_col", [NH, 1])
    A_log_h = din("A_log_col", [NH, 1])
    D_h = din("D_row", [1, NH])
    nw_h = din("nw_row", [1, DIN])
    b_out_h = din("b_out_row", [1, DM])
    rw_h = din("router_w_fm", [128, 8 * NEXP])
    rb_h = din("router_bias11", [1, 1])
    gate_h = din("gate_w", [NEXP, DM, IEXP])
    up_h = din("up_w", [NEXP, DM, IEXP])
    down_h = din("down_w", [NEXP, IEXP, DM])
    sg_h = din("s_gate_w", [DM, ISH])
    su_h = din("s_up_w", [DM, ISH])
    sd_h = din("s_down_w", [ISH, DM])
    ident_h = din("ident", [64, 64])
    ident_bf_h = din("ident_bf", [128, 128], BF16)
    tri_h = din("tri", [128, 128], BF16)
    sel_h = din("sel_bf", [64, NH * 128], BF16)
    sel16_h = din("sel16_bf", [16, NEXP * 128], BF16)
    strictu_h = din("strictu_bf", [128, 128], BF16)
    maski_h = din("maski_bf", [128, 512], BF16)
    ones32_h = din("ones32", [32, 128])

    out_h = dout("out", [L, DM])

    def dtap(name, shape, dt=F32):
        return dout("dbg_" + name, shape, dt) if name in dbgset else None

    def tap_dma(fn):
        sig = p.dma("sync", fn)
        for e_ in ("vector", "scalar", "gpsimd", "tensor"):
            p.wait(e_, "sync", sig)

    tp_lnu = dtap("lnu", [L, DM], BF16)
    tp_xbc = dtap("xbc", [18 * 128, L], BF16)
    tp_z = dtap("z", [L, DIN], BF16)
    tp_dt = dtap("dt", [NH, L])
    tp_dacs = dtap("dacs", [NH, L])
    tp_y = dtap("y", [L, DIN], BF16)
    tp_yn = dtap("yn", [L, DIN], BF16)
    tp_x1 = dtap("x1", [L, DM])
    tp_h = dtap("h", [L, DM], BF16)
    tp_w = dtap("w", [L, NEXP])
    tp_ffn = dtap("ffn", [L, DM])
    tp_st = dtap("st", [128, 16])
    tp_u = dtap("u0", [128, DM])

    lnu_d = [dint(f"lnu_d{i}", [GT, DM], BF16) for i in range(2)]
    xs_d = [dint(f"xs_d{i}", [DIN, GT], BF16) for i in range(2)]
    bb_d = [dint(f"bb_d{i}", [128, GT], BF16) for i in range(2)]
    yb_d = [dint(f"yb_d{i}", [GT, DIN], BF16) for i in range(2)]
    z_d = [dint(f"z_d{i}", [GT, DIN], BF16) for i in range(2)]
    e1_d = [dint(f"e1_d{i}", [64, 128], BF16) for i in range(2)]
    x1_d = dint("x1_d", [L, DM], F32)
    hb_d = dint("hb_d", [L, DM], BF16)
    gate_bd = dint("gate_bd", [NEXP, 128, 8 * IEXP], BF16)
    up_bd = dint("up_bd", [NEXP, 128, 8 * IEXP], BF16)
    down_bd = dint("down_bd", [NEXP, 128, 4 * DM], BF16)
    sg_bd = dint("sg_bd", [128, 8 * ISH], BF16)
    su_bd = dint("su_bd", [128, 8 * ISH], BF16)
    sd_bd = dint("sd_bd", [128, 16 * DM], BF16)

    sbuf = lambda name, shape, dt=F32: nc.alloc_sbuf_tensor(name, list(shape), dt).ap()

    conv_w = sbuf("conv_w", [128, 18 * KCONV])
    conv_b = sbuf("conv_b", [128, 18])
    b_xbc = sbuf("b_xbc", [128, 18])
    b_z_bf = sbuf("b_z_bf", [1, DIN], BF16)
    dtb = sbuf("dtb", [NH, 1])
    A_col = sbuf("A_col", [NH, 1])
    D_cols = sbuf("D_cols", [128, NH])
    nw_bc = sbuf("nw_bc", [128, DIN], BF16)
    b_out_bf = sbuf("b_out_bf", [1, DM], BF16)
    rw = sbuf("rw", [128, 8 * NEXP], BF16)
    ident_f = sbuf("ident_f", [64, 64])
    ident_b = sbuf("ident_b", [128, 128], BF16)
    tri_m = sbuf("tri_m", [128, 128], BF16)
    sel16 = sbuf("sel16", [16, NEXP * 128], BF16)
    strictu = sbuf("strictu", [128, 128], BF16)
    maski = sbuf("maski", [128, 512], BF16)
    ones_row_bf = sbuf("ones_row_bf", [1, 128], BF16)
    ones32x128 = sbuf("ones32x128", [32, 128])
    eps_ln_col = sbuf("eps_ln_col", [128, 1])
    eps_rms_col = sbuf("eps_rms_col", [128, 1])
    one32c = sbuf("one32c", [NH, 1])
    rb_sb = sbuf("rb_sb", [1, 1], BF16)
    rb_col = sbuf("rb_col", [128, 1])
    stat = sbuf("stat", [128, 16])
    x1_t = [sbuf(f"x1_t{i}", [128, DM]) for i in range(2)]
    h_bf = [sbuf(f"h_bf{i}", [128, DM], BF16) for i in range(2)]
    D_row_sb = sbuf("D_row_sb", [1, NH], BF16)

    PA = nc.alloc_psum_tensor("PA", [128, 1024], F32).ap()
    PB = nc.alloc_psum_tensor("PB", [128, 1024], F32).ap()
    PC = nc.alloc_psum_tensor("PC", [128, 1024], F32).ap()
    PD = nc.alloc_psum_tensor("PD", [128, 1024], F32).ap()
    bA0, bA1 = Bank(p, PA[:, 0:512]), Bank(p, PA[:, 512:1024])
    bB0, bB1 = Bank(p, PB[:, 0:512]), Bank(p, PB[:, 512:1024])
    bC = Bank(p, PC)
    bD = Bank(p, PD)

    def tt_b(e, out, in0, col, op):
        a = col.shape[1]
        return e.tensor_tensor(
            out.rearrange("q (x y) -> q x y", x=a),
            in0.rearrange("q (x y) -> q x y", x=a),
            col.unsqueeze(2).broadcast_to([128, a, HD]), op)

    for dst, src in [(conv_w, conv_w_h), (conv_b, conv_b_h), (b_xbc, b_xbc_h),
                     (ident_f, ident_h), (ident_b, ident_bf_h), (tri_m, tri_h),
                     (ones32x128, ones32_h), (dtb, b_dt_h),
                     (A_col, A_log_h),
                     (strictu, strictu_h), (maski, maski_h),
                     (sel16, sel16_h)]:
        p.dma("sync", lambda e, d=dst, s=src: e.dma_start(out=d, in_=s))
    p.dma("sync", lambda e: e.dma_start(out=stat[0:NH, 15:16], in_=dt_bias_h))
    p.dma("sync", lambda e: e.dma_start(out=stat[0:NH, 14:15], in_=b_dt_h))
    p.dma("sync", lambda e: e.dma_start(out=stat[0:NH, 13:14], in_=A_log_h))
    p.dma("gpsimd", lambda e: e.dma_start(out=b_z_bf, in_=b_z_h))
    p.dma("gpsimd", lambda e: e.dma_start(out=b_out_bf, in_=b_out_h))
    p.dma("gpsimd", lambda e: e.dma_start(out=rw, in_=rw_h))
    p.dma("gpsimd", lambda e: e.dma_start(out=D_row_sb, in_=D_h))
    p.dma("gpsimd", lambda e: e.dma_start(out=rb_sb, in_=rb_h))
    ld_sync0 = p.cnt["sync"]

    for e_ in range(NEXP):
        p.dma("gpsimd", lambda e, i=e_: e.dma_start(
            out=gate_bd[i].rearrange("q (dc i) -> q dc i", dc=8),
            in_=gate_h[i].rearrange("(dc q) i -> q dc i", q=128)))
        p.dma("gpsimd", lambda e, i=e_: e.dma_start(
            out=up_bd[i].rearrange("q (dc i) -> q dc i", dc=8),
            in_=up_h[i].rearrange("(dc q) i -> q dc i", q=128)))
        p.dma("gpsimd", lambda e, i=e_: e.dma_start(
            out=down_bd[i].rearrange("q (ic d) -> q ic d", ic=4),
            in_=down_h[i].rearrange("(ic q) d -> q ic d", q=128)))
    p.dma("gpsimd", lambda e: e.dma_start(
        out=sg_bd.rearrange("q (dc m) -> q dc m", dc=8),
        in_=sg_h.rearrange("(dc q) m -> q dc m", q=128)))
    p.dma("gpsimd", lambda e: e.dma_start(
        out=su_bd.rearrange("q (dc m) -> q dc m", dc=8),
        in_=su_h.rearrange("(dc q) m -> q dc m", q=128)))
    p.dma("gpsimd", lambda e: e.dma_start(
        out=sd_bd.rearrange("q (mc d) -> q mc d", mc=16),
        in_=sd_h.rearrange("(mc q) d -> q mc d", q=128)))
    ffn_precast = p.cnt["gpsimd"]

    p.wait("vector", "sync", ld_sync0)
    for ap_, v in [(ones_row_bf, 1.0), (eps_ln_col, EPS_LN),
                   (eps_rms_col, EPS_RMS), (one32c, 1.0)]:
        p.sig("vector", lambda e, a=ap_, x=v: e.memset(a, x))
    p.sig("vector", lambda e: e.tensor_tensor(dtb, stat[0:NH, 14:15], stat[0:NH, 15:16], OP.add))
    p.wait("scalar", "sync", ld_sync0)
    s_a = p.sig("scalar", lambda e: e.activation(out=stat[0:NH, 12:13], in_=stat[0:NH, 13:14], func=AF.Exp))
    p.wait("vector", "scalar", s_a)
    v_init = p.sig("vector", lambda e: e.tensor_scalar_mul(A_col, stat[0:NH, 12:13], -1.0))
    p.wait("tensor", "vector", v_init)
    p.wait("tensor", "gpsimd")
    bA0.acquire()
    t_d = p.sig("tensor", lambda e: e.matmul(PA[:, 0:NH], ones_row_bf, D_row_sb, start=True, stop=True))
    p.wait("vector", "tensor", t_d)
    vd = p.sig("vector", lambda e: e.tensor_copy(out=D_cols, in_=PA[:, 0:NH]))
    bA0.release("vector", vd)
    bA0.acquire()
    t_rb = p.sig("tensor", lambda e: e.matmul(PA[:, 0:1], ones_row_bf, rb_sb, start=True, stop=True))
    p.wait("vector", "tensor", t_rb)
    v_rb = p.sig("vector", lambda e: e.tensor_copy(out=rb_col, in_=PA[:, 0:1]))
    bA0.release("vector", v_rb)

    def emit_ln_stats(src_ap, width, eps_col, dump, psum_src=False):
        v_red = p.sig("vector", lambda e, a=src_ap: e.tensor_reduce(
            stat[:, 0:1], a, AX.X, OP.add))
        if psum_src:
            p.wait("scalar", "vector", v_red)
        s_sq = p.sig("scalar", lambda e, a=src_ap, w=width, d=dump: e.activation(
            out=d[:, 0:w], in_=a, func=AF.Square, accum_out=stat[:, 1:2]))
        p.wait("vector", "scalar", s_sq)
        p.tsync("vector")
        p.sig("vector", lambda e, w=width: e.tensor_scalar_mul(stat[:, 2:3], stat[:, 0:1], 1.0 / w))
        p.sig("vector", lambda e, w=width: e.tensor_scalar_mul(stat[:, 3:4], stat[:, 1:2], 1.0 / w))
        p.tsync("vector")
        p.sig("vector", lambda e: e.tensor_tensor(stat[:, 4:5], stat[:, 2:3], stat[:, 2:3], OP.mult))
        p.tsync("vector")
        v_var = p.sig("vector", lambda e: e.tensor_tensor(stat[:, 6:7], stat[:, 3:4], stat[:, 4:5], OP.subtract))
        p.wait("scalar", "vector", v_var)
        p.sig("scalar", lambda e, c=eps_col: e.activation(out=stat[:, 5:6], in_=stat[:, 6:7], func=AF.Ln, bias=c))
        p.tsync("scalar")
        return p.sig("scalar", lambda e: e.activation(out=stat[:, 6:7], in_=stat[:, 5:6], func=AF.Exp, scale=-0.5))

    hb_done = 0

    with contextlib.ExitStack() as astack:
        asb = lambda name, shape, dt=F32: astack.enter_context(
            nc.sbuf_tensor(name, list(shape), dt)).ap()

        W_inT = asb("W_in_sb", [128, 8 * DPROJ], BF16)
        wout_b = [asb(f"wout_b{f}", [128, DM], BF16) for f in range(8)]
        u_t = [asb(f"u_t{t}", [128, DM]) for t in range(2)]
        lnuT = [[asb(f"lnuT{b}_{d}", [128, GT], BF16) for d in range(8)] for b in range(2)]
        z_sil = asb("z_sil", [128, DIN], BF16)
        B_fm = asb("B_fm", [128, GT], BF16)
        C_fm = asb("C_fm", [128, GT], BF16)
        dt_fm_g = asb("dt_fm_g", [NH, GT])
        xbc_w = [asb(f"xbc_w{i}", [128, GT + 3], BF16) for i in range(2)]
        halo = asb("halo", [128, 18 * 3], BF16)
        xsil = [asb(f"xsil{i}", [128, GT], BF16) for i in range(2)]
        tri64a = asb("tri64a", [64, 128])
        tri64b = asb("tri64b", [64, 128])
        dA_fm = asb("dA_fm", [NH, 128])
        neg_last = asb("neg_last", [NH, 1])
        tok64a = asb("tok64a", [128, 64])
        tok64b = asb("tok64b", [128, 64])
        Edec = asb("Edec", [128, 64])
        lhsE1 = asb("lhsE1", [66, 128], BF16)
        rhsE1 = asb("rhsE1", [66, NH * 128], BF16)
        x_tok = asb("x_tok", [128, DIN], BF16)
        B_tok = asb("B_tok", [128, 128], BF16)
        xdt = asb("xdt", [128, DIN], BF16)
        xdtd = asb("xdtd", [128, DIN], BF16)
        xD = asb("xD", [128, DIN], BF16)
        CBTm = asb("CBTm", [128, 128], BF16)
        es = [asb(f"es{i}", [128, GT], BF16) for i in range(2)]
        Mp = [asb(f"Mp{i}", [128, GT], BF16) for i in range(2)]
        S_f = asb("S_f", [128, DIN])
        S_bf = asb("S_bf", [128, DIN], BF16)
        ysb = asb("ysb", [128, DIN], BF16)
        tmp1k = asb("tmp1k", [128, 1024])
        y_n = [asb(f"y_n{i}", [128, DIN], BF16) for i in range(2)]
        mix_ln = tmp1k
        ynT = None
        y2_bf = xdt
        lnu_bf = [y_n[0][:, 0:DM], y_n[1][:, 0:DM]]
        conv_o = [ysb.bitcast(F32)[:, 0:GT], ysb.bitcast(F32)[:, GT:2 * GT],
                  xdtd.bitcast(F32)[:, 0:GT], xdtd.bitcast(F32)[:, GT:2 * GT]]
        sp_a = tmp1k[0:NH, 0:GT]
        scr32 = tmp1k[0:NH, 0:128]
        sp_l = tmp1k[0:NH, GT:2 * GT]
        ynT = [es[0], es[1], Mp[0], Mp[1], xsil[0], xsil[1], B_fm, C_fm]

        p.dma("gpsimd", lambda e: e.dma_start(
            out=W_inT.rearrange("q (dc f) -> q dc f", dc=8),
            in_=W_inT_h.rearrange("(dc q) f -> q dc f", q=128)))
        ld_w = p.cnt["gpsimd"]
        p.wait("tensor", "gpsimd", ld_w)
        p.wait("vector", "sync", ld_sync0)
        p.sig("vector", lambda e: e.memset(lhsE1[64:66, :], 1.0))
        p.dma("gpsimd", lambda e: e.dma_start(out=ysb[0:1, 0:DIN], in_=nw_h))
        p.wait("tensor", "gpsimd")
        for k in range(4):
            bA1.acquire()
            tk = p.sig("tensor", lambda e, k=k: e.matmul(
                PA[:, 512:1024], ones_row_bf, ysb[0:1, k * 512:(k + 1) * 512],
                start=True, stop=True))
            p.wait("vector", "tensor", tk)
            vk = p.sig("vector", lambda e, k=k: e.tensor_copy(
                out=nw_bc[:, k * 512:(k + 1) * 512], in_=PA[:, 512:1024]))
            bA1.release("vector", vk)
        p.sig("vector", lambda e: e.memset(S_f, 0.0))
        p.sig("vector", lambda e: e.memset(S_bf, 0.0))
        p.sig("vector", lambda e: e.memset(halo, 0.0))
        p.dma("sync", lambda e: e.dma_start(out=rhsE1[0:64, :], in_=sel_h))
        p.wait("tensor", "sync")
        p.wait("tensor", "vector")

        xbc_conv_v = [0, 0]
        conv_silu_s = [0, 0]
        xsil_st = [0, 0]
        lnu_st_par = [0, 0]
        yn_st = [0, 0]
        x1_par = [0, 0]
        hb_par = [0, 0]
        es_v = [0, 0]
        mp_pe = [0, 0]
        u_war_v = [0, 0]
        yn_v_last = 0
        z_rd_war = 0
        rhsE1_war = 0
        lnuT_guard = [0, 0]
        lnu_d_war = [0, 0]
        xs_d_war = [0, 0]
        bb_d_war = [0, 0]
        yb_d_war = [0, 0]
        e1_d_war = [0, 0]
        bc_fm_guard = 0
        ynT_guard = 0
        xk_v = 0
        xk_g = 0
        btok_pe = 0
        tok_g_guard = 0

        for g in range(ngroups):
            gb = g % 2
            st_last = 0
            u_norm_v = [0, 0]
            for tc in range(CPG):
                p.wait("sync", "vector", u_war_v[tc % 2])
                u_ld = p.dma("sync", lambda e, t=tc, g=g: e.dma_start(
                    out=u_t[t % 2], in_=u_h[(g * GT + t * CH):(g * GT + (t + 1) * CH), :]))
                p.wait("vector", "sync", u_ld)
                p.wait("scalar", "sync", u_ld)
                s_rstd = emit_ln_stats(u_t[tc % 2], DM, eps_ln_col, tmp1k)
                p.wait("vector", "scalar", s_rstd)
                p.wait("vector", "gpsimd", max(lnu_st_par[tc % 2], yn_st[tc % 2]))
                v_norm = p.sig("vector", lambda e, t=tc: e.tensor_scalar(
                    lnu_bf[t % 2], u_t[t % 2], stat[:, 2:3], stat[:, 6:7],
                    OP.subtract, OP.mult))
                u_war_v[tc % 2] = v_norm
                if tp_st is not None and g == 0 and tc == 0:
                    p.wait("sync", "vector", v_norm)
                    tap_dma(lambda e: e.dma_start(out=tp_st, in_=stat))
                    p.dma("sync", lambda e: e.dma_start(out=tp_u, in_=u_t[0]))
                p.wait("gpsimd", "vector", v_norm)
                st_last = p.dma("gpsimd", lambda e, t=tc, gb=gb: e.dma_start(
                    out=lnu_d[gb][t * CH:(t + 1) * CH, :], in_=lnu_bf[t % 2]))
                lnu_st_par[tc % 2] = st_last
                if tp_lnu is not None:
                    p.wait("sync", "gpsimd", st_last)
                    tap_dma(lambda e, t=tc, b=gb, g=g: e.dma_start(
                        out=tp_lnu[(g * GT + t * CH):(g * GT + (t + 1) * CH), :],
                        in_=lnu_d[b][t * CH:(t + 1) * CH, :]))
            p.wait("sync", "gpsimd", st_last)
            p.wait("sync", "tensor", lnuT_guard[gb])
            lnuT_ld = 0
            for dc in range(8):
                lnuT_ld = p.dma("sync", lambda e, d=dc, gb=gb: e.dma_start(
                    out=lnuT[gb][d], in_=lnu_d[gb][:, d * 128:(d + 1) * 128], transpose=True))
            lnu_d_war[gb] = lnuT_ld

            p.wait("tensor", "sync", lnuT_ld)
            for fc in range(18):
                pb = fc % 2
                bank = [bA0, bA1][pb]
                bank.acquire()
                pe_s = 0
                for kc in range(8):
                    fn = lambda e, f=fc, k=kc, b=bank, gb=gb: e.matmul(
                        b.ap, W_inT[:, k * DPROJ + DIN + f * 128: k * DPROJ + DIN + (f + 1) * 128],
                        lnuT[gb][k], start=(k == 0), stop=(k == 7))
                    if kc < 7:
                        p.raw("tensor", fn)
                    else:
                        pe_s = p.sig("tensor", fn)
                p.wait("scalar", "tensor", pe_s)
                p.wait("scalar", "vector", xbc_conv_v[pb])
                ev = p.sig("scalar", lambda e, f=fc, b=bank: e.activation(
                    out=xbc_w[f % 2][:, 3:GT + 3], in_=b.ap, func=AF.Identity, bias=b_xbc[:, f:f + 1]))
                bank.release("scalar", ev)
                p.wait("vector", "scalar", ev)
                p.sig("vector", lambda e, f=fc: e.tensor_copy(
                    out=xbc_w[f % 2][:, 0:3], in_=halo[:, f * 3:f * 3 + 3]))
                ca, cb = conv_o[2 * pb], conv_o[2 * pb + 1]
                p.wait("vector", "scalar", conv_silu_s[pb])
                p.tsync("vector")
                p.sig("vector", lambda e, f=fc, a=ca: e.tensor_scalar(
                    a, xbc_w[f % 2][:, 0:GT], conv_w[:, f * 4:f * 4 + 1],
                    conv_b[:, f:f + 1], OP.mult, OP.add))
                p.sig("vector", lambda e, f=fc, a=ca, b=cb: e.scalar_tensor_tensor(
                    b, xbc_w[f % 2][:, 1:GT + 1], conv_w[:, f * 4 + 1:f * 4 + 2], a, OP.mult, OP.add))
                p.sig("vector", lambda e, f=fc, a=ca, b=cb: e.scalar_tensor_tensor(
                    a, xbc_w[f % 2][:, 2:GT + 2], conv_w[:, f * 4 + 2:f * 4 + 3], b, OP.mult, OP.add))
                cv = p.sig("vector", lambda e, f=fc, a=ca, b=cb: e.scalar_tensor_tensor(
                    b, xbc_w[f % 2][:, 3:GT + 3], conv_w[:, f * 4 + 3:f * 4 + 4], a, OP.mult, OP.add))
                hv = p.sig("vector", lambda e, f=fc: e.tensor_copy(
                    out=halo[:, f * 3:f * 3 + 3], in_=xbc_w[f % 2][:, GT:GT + 3]))
                xbc_conv_v[pb] = hv
                p.wait("scalar", "vector", cv)
                if fc < 16:
                    p.wait("scalar", "gpsimd", xsil_st[pb])
                    sl = p.sig("scalar", lambda e, f=fc, b=cb: e.activation(
                        out=xsil[f % 2], in_=b, func=AF.Silu))
                    conv_silu_s[pb] = sl
                    p.wait("gpsimd", "scalar", sl)
                    p.wait("gpsimd", "sync", xs_d_war[gb])
                    xsil_st[pb] = p.dma("gpsimd", lambda e, f=fc, gb=gb: e.dma_start(
                        out=xs_d[gb][f * 128:(f + 1) * 128, :], in_=xsil[f % 2]))
                    if tp_xbc is not None:
                        p.wait("sync", "scalar", sl)
                        tap_dma(lambda e, f=fc, g=g: e.dma_start(
                            out=tp_xbc[f * 128:(f + 1) * 128, g * GT:(g + 1) * GT],
                            in_=xsil[f % 2]))
                else:
                    dst = B_fm if fc == 16 else C_fm
                    p.wait("scalar", "tensor", bc_fm_guard)
                    sl = p.sig("scalar", lambda e, b=cb, d=dst: e.activation(
                        out=d, in_=b, func=AF.Silu))
                    conv_silu_s[pb] = sl
                    if fc == 16:
                        p.wait("gpsimd", "scalar", sl)
                        p.wait("gpsimd", "sync", bb_d_war[gb])
                        p.dma("gpsimd", lambda e, gb=gb: e.dma_start(out=bb_d[gb], in_=B_fm))
                    if tp_xbc is not None:
                        p.wait("sync", "scalar", sl)
                        tap_dma(lambda e, f=fc, d=dst, g=g: e.dma_start(
                            out=tp_xbc[f * 128:(f + 1) * 128, g * GT:(g + 1) * GT], in_=d))
            xs_store_done = p.cnt["gpsimd"]

            zstage = [xdt, ysb]
            zst_par = [0, 0]
            for tc in range(CPG):
                zp = tc % 2
                for ng in range(4):
                    pb = (tc * 4 + ng) % 2
                    bank = [bA0, bA1][pb]
                    bank.acquire()
                    for kc in range(8):
                        p.raw("tensor", lambda e, t=tc, n=ng, k=kc, b=bank, gb=gb: e.matmul(
                            b.ap, lnuT[gb][k][:, t * CH:(t + 1) * CH],
                            W_inT[:, k * DPROJ + n * 512:k * DPROJ + (n + 1) * 512],
                            start=(k == 0), stop=False))
                    pe_s = p.sig("tensor", lambda e, n=ng, b=bank: e.matmul(
                        b.ap, ones_row_bf, b_z_bf[:, n * 512:(n + 1) * 512],
                        start=False, stop=True))
                    p.wait("scalar", "tensor", pe_s)
                    if g > 0 and tc == 0 and ng == 0:
                        p.wait("scalar", "vector", yn_v_last)
                    p.wait("scalar", "gpsimd", zst_par[zp])
                    sz = p.sig("scalar", lambda e, t=tc, n=ng, b=bank, s=zstage[zp]: e.activation(
                        out=s[:, n * 512:(n + 1) * 512], in_=b.ap, func=AF.Silu))
                    bank.release("scalar", sz)
                p.wait("gpsimd", "scalar", sz)
                zst_par[zp] = p.dma("gpsimd", lambda e, t=tc, s=zstage[zp], gb=gb: e.dma_start(
                    out=z_d[gb][t * CH:(t + 1) * CH, :], in_=s))
                if tp_z is not None:
                    p.wait("sync", "scalar", sz)
                    tap_dma(lambda e, t=tc, s=zstage[zp], g=g: e.dma_start(
                        out=tp_z[(g * GT + t * CH):(g * GT + (t + 1) * CH), :], in_=s))
            z_store_done = p.cnt["gpsimd"]

            bB0.acquire()
            pe_dt = 0
            for kc in range(8):
                fn = lambda e, k=kc, gb=gb: e.matmul(
                    PB[0:NH, 0:512], W_inT[:, k * DPROJ + DIN + CONVD:k * DPROJ + DPROJ],
                    lnuT[gb][k], start=(k == 0), stop=(k == 7))
                if kc < 7:
                    p.raw("tensor", fn)
                else:
                    pe_dt = p.sig("tensor", fn)
            lnuT_guard[gb] = pe_dt
            p.wait("scalar", "tensor", pe_dt)
            p.sig("scalar", lambda e: e.activation(
                out=sp_a, in_=PB[0:NH, 0:512], func=AF.Abs, bias=dtb))
            p.sig("scalar", lambda e: e.activation(out=sp_l, in_=sp_a, func=AF.Exp, scale=-1.0))
            p.sig("scalar", lambda e: e.activation(out=sp_a, in_=sp_l, func=AF.Ln, bias=one32c))
            s_r = p.sig("scalar", lambda e: e.activation(
                out=sp_l, in_=PB[0:NH, 0:512], func=AF.Relu, bias=dtb))
            bB0.release("scalar", s_r)
            p.wait("vector", "scalar", s_r)
            v_dt = p.sig("vector", lambda e: e.tensor_tensor(dt_fm_g, sp_a, sp_l, OP.add))
            if tp_dt is not None:
                p.wait("sync", "vector", v_dt)
                tap_dma(lambda e, g=g: e.dma_start(
                    out=tp_dt[:, g * GT:(g + 1) * GT], in_=dt_fm_g))

            if full_groups is not None and g >= full_groups:
                continue
            for cc in range(CPG):
                ch = g * CPG + cc
                cb2 = ch % 2
                if cc == 0:
                    p.wait("gpsimd", "scalar")
                p.sig("vector", lambda e, c=cc: e.tensor_scalar_mul(
                    dA_fm, dt_fm_g[:, c * CH:(c + 1) * CH], A_col))
                p.tsync("vector")
                p.sig("vector", lambda e: e.tensor_tensor_scan(
                    tri64a[0:32, :], ones32x128, dA_fm, 0.0, OP.mult, OP.add))
                p.sig("vector", lambda e, c=cc: e.tensor_copy(
                    out=tri64b[0:32, :], in_=dt_fm_g[:, c * CH:(c + 1) * CH]))
                if tp_dacs is not None:
                    p.wait("sync", "vector")
                    tap_dma(lambda e, c=ch: e.dma_start(
                        out=tp_dacs[:, c * CH:(c + 1) * CH], in_=tri64a[0:32, :]))
                p.wait("scalar", "vector")
                s_hi = p.sig("scalar", lambda e: e.activation(
                    out=lhsE1[0:32, :], in_=tri64a[0:32, :], func=AF.Copy))
                p.wait("vector", "scalar", s_hi)
                p.tsync("vector")
                v_lo = p.sig("vector", lambda e: e.tensor_tensor(
                    lhsE1[32:64, :], tri64a[0:32, :], lhsE1[0:32, :], OP.subtract))
                p.wait("gpsimd", "vector", v_lo)
                p.wait("gpsimd", "sync", e1_d_war[cb2])
                g_st = p.dma("gpsimd", lambda e, b=cb2: e.dma_start(
                    out=e1_d[b], in_=lhsE1[0:64, :]))
                p.wait("sync", "gpsimd", g_st)
                p.wait("sync", "tensor", rhsE1_war)
                rhs_ld = p.dma("sync", lambda e, b=cb2: e.dma_start(
                    out=rhsE1[64:66, :],
                    in_=e1_d[b].rearrange("(s h) i -> s (h i)", s=2)))
                e1_d_war[cb2] = rhs_ld
                p.tsync("vector")
                p.sig("vector", lambda e: e.tensor_copy(
                    out=neg_last, in_=tri64a[0:32, 127:128]))
                p.sig("vector", lambda e: e.tensor_scalar_mul(
                    tri64a[32:64, :], ones32x128, tri64a[0:32, 127:128]))
                p.wait("scalar", "vector")
                s_wd = p.sig("scalar", lambda e: e.activation(
                    out=scr32, in_=tri64a[0:32, :], func=AF.Exp, scale=-1.0, bias=neg_last))
                p.wait("vector", "scalar", s_wd)
                p.wait("vector", "gpsimdc", tok_g_guard)
                v_wdt = p.sig("vector", lambda e: e.tensor_tensor(
                    tri64b[32:64, :], scr32, tri64b[0:32, :], OP.mult))
                p.wait("tensor", "vector", v_wdt)
                bB0.acquire()
                t_t1 = p.sig("tensor", lambda e: e.transpose(
                    PB[:, 0:64], tri64a, ident_f))
                p.wait("vector", "tensor", t_t1)
                v_t1 = p.sig("vector", lambda e: e.tensor_copy(out=tok64a, in_=PB[:, 0:64]))
                bB0.release("vector", v_t1)
                bB0.acquire()
                t_t2 = p.sig("tensor", lambda e: e.transpose(
                    PB[:, 0:64], tri64b, ident_f))
                p.wait("vector", "tensor", t_t2)
                v_t2 = p.sig("vector", lambda e: e.tensor_copy(out=tok64b, in_=PB[:, 0:64]))
                bB0.release("vector", v_t2)
                p.wait("scalar", "vector", v_t1)
                s_ed = p.sig("scalar", lambda e: e.activation(out=Edec, in_=tok64a, func=AF.Exp))

                p.wait("sync", "gpsimd", xs_store_done)
                p.wait("sync", "vector", xk_v)
                p.wait("sync", "gpsimdc", xk_g)
                x_ld = p.dma("sync", lambda e, c=cc, gb=gb: e.dma_start(
                    out=x_tok, in_=xs_d[gb][:, c * CH:(c + 1) * CH], transpose=True))
                xs_d_war[gb] = x_ld
                p.wait("sync", "tensor", btok_pe)
                b_ld = p.dma("sync", lambda e, c=cc, gb=gb: e.dma_start(
                    out=B_tok, in_=bb_d[gb][:, c * CH:(c + 1) * CH], transpose=True))
                bb_d_war[gb] = b_ld
                p.wait("sync", "gpsimd", z_store_done)
                p.wait("sync", "vector", z_rd_war)
                z_ld = p.dma("sync", lambda e, c=cc, gb=gb: e.dma_start(
                    out=z_sil, in_=z_d[gb][c * CH:(c + 1) * CH, :]))
                p.wait("vector", "sync", x_ld)
                p.wait("vector", "gpsimd", z_store_done)
                p.tsync("vector")
                xk_v = p.sig("vector", lambda e: tt_b(
                    e, xdt, x_tok, tok64b[:, 0:32], OP.mult))
                p.wait("gpsimd", "sync", x_ld)
                p.wait("gpsimd", "vector", v_t2)
                g_xdtd = p.sig("gpsimd", lambda e: tt_b(
                    e, xdtd, x_tok, tok64b[:, 32:64], OP.mult))
                g_xd = p.sig("gpsimd", lambda e: tt_b(
                    e, xD, x_tok, D_cols, OP.mult))
                xk_g = g_xd
                tok_g_guard = g_xdtd
                bB0.acquire()
                t_cbt = p.sig("tensor", lambda e, c=cc: e.matmul(
                    PB[:, 0:128], B_fm[:, c * CH:(c + 1) * CH], C_fm[:, c * CH:(c + 1) * CH],
                    start=True, stop=True))
                p.wait("vector", "tensor", t_cbt)
                v_cbt = p.sig("vector", lambda e: e.tensor_tensor(
                    CBTm, PB[:, 0:128], tri_m, OP.mult))
                bB0.release("vector", v_cbt)

                p.wait("tensor", "sync", rhs_ld)
                pe_yoff = pe_ydiag = pe_st = 0
                for hf in range(2):
                    ho = hf * 1024
                    for q4 in range(4):
                        qb = q4 % 2
                        bank = [bB0, bB1][qb]
                        bank.acquire()
                        p.raw("tensor", lambda e, h=hf, q=q4, b=bank: e.matmul(
                            b.ap, lhsE1,
                            rhsE1[:, (h * 16 + q * 4) * 128:(h * 16 + q * 4 + 4) * 128],
                            start=True, stop=False))
                        pe_e1 = p.sig("tensor", lambda e, b=bank: e.matmul(
                            b.ap, strictu, maski, start=False, stop=True))
                        rhsE1_war = pe_e1
                        p.wait("scalar", "tensor", pe_e1)
                        p.wait("scalar", "vector", es_v[qb])
                        s_es = p.sig("scalar", lambda e, q=q4, b=bank: e.activation(
                            out=es[q % 2], in_=b.ap, func=AF.Exp))
                        bank.release("scalar", s_es)
                        p.wait("vector", "scalar", s_es)
                        p.wait("vector", "tensor", mp_pe[qb])
                        p.tsync("vector")
                        v_mp = p.sig("vector", lambda e, q=q4: e.scalar_tensor_tensor(
                            Mp[q % 2].rearrange("q (a b) -> q a b", a=4),
                            es[q % 2].rearrange("q (a b) -> q a b", a=4), 1e30,
                            CBTm.unsqueeze(1).broadcast_to([128, 4, 128]),
                            OP.min, OP.mult))
                        es_v[qb] = v_mp
                        p.wait("tensor", "vector", v_mp)
                        if q4 == 0:
                            bC.acquire()
                        for hh in range(4):
                            h16 = q4 * 4 + hh
                            h = hf * 16 + h16
                            fn = lambda e, q=q4, hh=hh, h=h, h16=h16: e.matmul(
                                PC[:, h16 * 64:(h16 + 1) * 64],
                                Mp[q % 2][:, hh * 128:(hh + 1) * 128],
                                xdt[:, h * 64:(h + 1) * 64],
                                start=(h16 % 8 == 0), stop=False)
                            if hh < 3:
                                p.raw("tensor", fn)
                            else:
                                mp_pe[qb] = p.sig("tensor", fn)
                    p.wait("tensor", "gpsimdc", g_xd)
                    for nn in range(2):
                        fn = lambda e, n=nn, o=ho: e.matmul(
                            PC[:, n * 512:(n + 1) * 512], ident_b,
                            xD[:, o + n * 512:o + (n + 1) * 512], start=False, stop=True)
                        if nn == 0:
                            p.raw("tensor", fn)
                        else:
                            pe_ydiag = p.sig("tensor", fn)
                    bD.acquire()
                    for nn in range(2):
                        fn = lambda e, n=nn, c=cc, o=ho: e.matmul(
                            PD[:, n * 512:(n + 1) * 512], C_fm[:, c * CH:(c + 1) * CH],
                            S_bf[:, o + n * 512:o + (n + 1) * 512], start=True, stop=True)
                        if nn == 0:
                            p.raw("tensor", fn)
                        else:
                            pe_yoff = p.sig("tensor", fn)
                    p.wait("vector", "tensor", pe_yoff)
                    p.wait("vector", "scalar", s_ed)
                    v_o1 = p.sig("vector", lambda e, h=hf: tt_b(
                        e, tmp1k, PD, Edec[:, h * 16:(h + 1) * 16], OP.mult))
                    bD.release("vector", v_o1)
                    bD.acquire()
                    p.wait("tensor", "sync", b_ld)
                    p.wait("tensor", "gpsimdc", g_xdtd)
                    for nn in range(2):
                        fn = lambda e, n=nn, o=ho: e.matmul(
                            PD[:, n * 512:(n + 1) * 512], B_tok,
                            xdtd[:, o + n * 512:o + (n + 1) * 512], start=True, stop=True)
                        if nn == 0:
                            p.raw("tensor", fn)
                        else:
                            pe_st = p.sig("tensor", fn)
                    btok_pe = pe_st
                    p.wait("vector", "tensor", pe_ydiag)
                    v_o2 = p.sig("vector", lambda e, o=ho: e.tensor_tensor(
                        ysb[:, o:o + 1024], tmp1k, PC, OP.add))
                    bC.release("vector", v_o2)
                    p.sig("vector", lambda e, h=hf, o=ho: tt_b(
                        e, tmp1k, S_f[:, o:o + 1024],
                        Edec[:, 32 + h * 16:32 + (h + 1) * 16], OP.mult))
                    p.wait("vector", "tensor", pe_st)
                    v_sf = p.sig("vector", lambda e, o=ho: e.tensor_tensor(
                        S_f[:, o:o + 1024], tmp1k, PD, OP.add))
                    bD.release("vector", v_sf)
                    p.wait("scalar", "vector", v_sf)
                    p.wait("scalar", "tensor", pe_yoff)
                    p.sig("scalar", lambda e, o=ho: e.activation(
                        out=S_bf[:, o:o + 1024], in_=S_f[:, o:o + 1024], func=AF.Copy))
                bc_fm_guard = pe_yoff

                if tp_y is not None:
                    p.wait("sync", "vector")
                    tap_dma(lambda e, c=ch: e.dma_start(
                        out=tp_y[c * CH:(c + 1) * CH, :], in_=ysb))
                p.wait("vector", "sync", z_ld)
                v_y2 = p.sig("vector", lambda e: e.tensor_tensor(
                    y2_bf, ysb, z_sil, OP.mult))
                z_rd_war = v_y2
                p.wait("scalar", "vector", v_y2)
                p.sig("scalar", lambda e: e.activation(
                    out=xdtd, in_=y2_bf, func=AF.Square, accum_out=stat[:, 7:8]))
                p.tsync("scalar")
                p.sig("scalar", lambda e: e.activation(
                    out=stat[:, 8:9], in_=stat[:, 7:8], func=AF.Ln,
                    bias=eps_rms_col, scale=1.0 / DIN))
                p.tsync("scalar")
                s_rs = p.sig("scalar", lambda e: e.activation(
                    out=stat[:, 9:10], in_=stat[:, 8:9], func=AF.Exp, scale=-0.5))
                p.wait("vector", "scalar", s_rs)
                p.wait("vector", "gpsimd", max(yn_st[cc % 2], lnu_st_par[cc % 2]))
                v_yn = p.sig("vector", lambda e, c=cc: e.scalar_tensor_tensor(
                    y_n[c % 2], y2_bf, stat[:, 9:10], nw_bc, OP.mult, OP.mult))
                yn_v_last = v_yn
                p.wait("gpsimd", "vector", v_yn)
                p.wait("gpsimd", "sync", yb_d_war[gb])
                yn_st[cc % 2] = p.dma("gpsimd", lambda e, c=cc, gb=gb: e.dma_start(
                    out=yb_d[gb][c * CH:(c + 1) * CH, :], in_=y_n[c % 2]))
                if tp_yn is not None:
                    p.wait("sync", "vector", v_yn)
                    tap_dma(lambda e, c=ch, cc=cc: e.dma_start(
                        out=tp_yn[c * CH:(c + 1) * CH, :], in_=y_n[cc % 2]))

            p.wait("tensor", "vector")
            p.wait("tensor", "scalar")
            pe_op = 0
            for fp in range(2):
                p.wait("sync", "gpsimd", max(yn_st))
                p.wait("sync", "tensor")
                ynT_ld = 0
                for f8 in range(8):
                    ynT_ld = p.dma("sync", lambda e, f=f8, o=fp * 8, gb=gb: e.dma_start(
                        out=ynT[f], in_=yb_d[gb][:, (o + f) * 128:(o + f + 1) * 128],
                        transpose=True))
                yb_d_war[gb] = ynT_ld
                p.wait("gpsimd", "tensor", ynT_guard)
                wout_ld = 0
                for f8 in range(8):
                    wout_ld = p.dma("gpsimd", lambda e, f=f8, o=fp * 8: e.dma_start(
                        out=wout_b[f], in_=W_outT_h[(o + f) * 128:(o + f + 1) * 128, :]))
                p.wait("tensor", "sync", ynT_ld)
                p.wait("tensor", "gpsimd", wout_ld)
                mm_fns = []
                for tc in range(CPG):
                    bank_ap = [PA, PB, PC, PD][tc]
                    for ng in range(2):
                        for f8 in range(8):
                            mm_fns.append(lambda e, b=bank_ap, t=tc, n=ng, f=f8,
                                          fp=fp: e.matmul(
                                b[:, n * 512:(n + 1) * 512],
                                ynT[f][:, t * CH:(t + 1) * CH],
                                wout_b[f][:, n * 512:(n + 1) * 512],
                                start=(fp == 0 and f == 0), stop=False))
                        if fp == 1:
                            mm_fns.append(lambda e, b=bank_ap, n=ng: e.matmul(
                                b[:, n * 512:(n + 1) * 512], ones_row_bf,
                                b_out_bf[:, n * 512:(n + 1) * 512], start=False, stop=True))
                for fn_ in mm_fns[:-1]:
                    p.raw("tensor", fn_)
                pe_op = p.sig("tensor", mm_fns[-1])
                ynT_guard = pe_op
            evac_v = []
            for tc in range(CPG):
                bank_ap = [PA, PB, PC, PD][tc]
                p.wait("vector", "tensor", pe_op)
                p.wait("scalar", "tensor", pe_op)
                s_rstd = emit_ln_stats(bank_ap, DM, eps_ln_col, tmp1k, psum_src=True)
                p.wait("vector", "scalar", s_rstd)
                v_ml = p.sig("vector", lambda e, b=bank_ap: e.tensor_scalar(
                    mix_ln, b, stat[:, 2:3], stat[:, 6:7], OP.subtract, OP.mult))
                evac_v.append(v_ml)
                p.wait("sync", "vector", u_war_v[tc % 2])
                u_ld2 = p.dma("sync", lambda e, t=tc, g=g: e.dma_start(
                    out=u_t[t % 2], in_=u_h[(g * GT + t * CH):(g * GT + (t + 1) * CH), :]))
                p.wait("vector", "sync", u_ld2)
                p.wait("vector", "gpsimd", x1_par[tc % 2])
                v_x1 = p.sig("vector", lambda e, t=tc: e.tensor_tensor(
                    x1_t[t % 2], mix_ln, u_t[t % 2], OP.add))
                u_war_v[tc % 2] = v_x1
                p.wait("gpsimd", "vector", v_x1)
                x1_par[tc % 2] = p.dma("gpsimd", lambda e, t=tc, g=g: e.dma_start(
                    out=x1_d[(g * GT + t * CH):(g * GT + (t + 1) * CH), :], in_=x1_t[t % 2]))
                if tp_x1 is not None:
                    p.wait("sync", "vector", v_x1)
                    tap_dma(lambda e, t=tc, g=g: e.dma_start(
                        out=tp_x1[(g * GT + t * CH):(g * GT + (t + 1) * CH), :], in_=x1_t[t % 2]))
                p.wait("scalar", "vector", v_x1)
                s_rstd2 = emit_ln_stats(x1_t[tc % 2], DM, eps_ln_col, tmp1k)
                p.wait("vector", "scalar", s_rstd2)
                p.wait("vector", "gpsimd", hb_par[tc % 2])
                v_h = p.sig("vector", lambda e, t=tc: e.tensor_scalar(
                    h_bf[t % 2], x1_t[t % 2], stat[:, 2:3], stat[:, 6:7], OP.subtract, OP.mult))
                p.wait("gpsimd", "vector", v_h)
                hb_par[tc % 2] = p.dma("gpsimd", lambda e, t=tc, g=g: e.dma_start(
                    out=hb_d[(g * GT + t * CH):(g * GT + (t + 1) * CH), :], in_=h_bf[t % 2]))
                if tp_h is not None:
                    p.wait("sync", "vector", v_h)
                    tap_dma(lambda e, t=tc, g=g: e.dma_start(
                        out=tp_h[(g * GT + t * CH):(g * GT + (t + 1) * CH), :], in_=h_bf[t % 2]))
            bA0.evac = bA1.evac = ("vector", evac_v[0])
            bB0.evac = bB1.evac = ("vector", evac_v[1])
            bC.evac = ("vector", evac_v[2])
            bD.evac = ("vector", evac_v[3])

        hb_done = p.cnt["gpsimd"]

    if upto == "a":
        p.wait("gpsimd", "sync", p.cnt["sync"])
        p.wait("gpsimd", "gpsimd", p.cnt["gpsimd"])
        p.raw("gpsimd", lambda e: e.nop())
        return p, T

    with contextlib.ExitStack() as bstack:
        bsb = lambda name, shape, dt=F32: bstack.enter_context(
            nc.sbuf_tensor(name, list(shape), dt)).ap()

        hT = [[bsb(f"hT{s5}_{d}", [128, GT], BF16) for d in range(8)]
              for s5 in range(2)]
        gw = [bsb(f"gw{i}", [128, 8 * IEXP], BF16) for i in range(2)]
        uw = [bsb(f"uw{i}", [128, 8 * IEXP], BF16) for i in range(2)]
        de4 = [bsb(f"de4{s5}", [128, 16 * DM], BF16) for s5 in range(2)]
        actAll = bsb("actAll", [128, NPE * 4 * 512], BF16)
        sgs = [bsb(f"sgs{i}", [128, 512], BF16) for i in range(3)]
        sus = [bsb(f"sus{i}", [128, 512], BF16) for i in range(3)]
        nw32 = nw_bc.bitcast(F32)
        c32a = nw32[:, 0:32]
        c32b = nw32[:, 32:64]
        c32g = nw32[:, 64:96]
        c16 = nw32[:, 96:112]
        c8w = nw32[:, 112:120]
        c4m = nw32[:, 120:124]
        c4s = nw32[:, 124:128]
        scr512b = nw_bc[:, 1024:1536]
        w_q = bsb("w_q", [128, 4 * NEXP])
        w8c = bsb("w8c", [128, 4 * NEXP], BF16)
        w8T = bsb("w8T", [16, GT], BF16)
        ffn_ln = bsb("ffn_ln", [128, DM])
        out_t = [bsb(f"out_t{i}", [128, DM]) for i in range(2)]

        p.wait("gpsimd", "gpsimd", ffn_precast)
        p.wait("sync", "gpsimd", hb_done)
        p.wait("sync", "vector")
        p.sig("vector", lambda e: e.memset(w8T, 0.0))

        def emit_gu_load(pe_i, b):
            if pe_i < NEXP:
                p.dma("sync", lambda e, i=pe_i, b=b: e.dma_start(
                    out=gw[b], in_=gate_bd[i]))
                return p.dma("sync", lambda e, i=pe_i, b=b: e.dma_start(
                    out=uw[b], in_=up_bd[i]))
            sp = pe_i - NEXP
            p.dma("sync", lambda e, s=sp, b=b: e.dma_start(
                out=gw[b].rearrange("q (dc m) -> q dc m", dc=8),
                in_=sg_bd.rearrange("q (dc m) -> q dc m", dc=8)[:, :, s * 512:(s + 1) * 512]))
            return p.dma("sync", lambda e, s=sp, b=b: e.dma_start(
                out=uw[b].rearrange("q (dc m) -> q dc m", dc=8),
                in_=su_bd.rearrange("q (dc m) -> q dc m", dc=8)[:, :, s * 512:(s + 1) * 512]))

        hT_ld_set = [0, 0]
        hT_guard = [0, 0]
        for dc in range(8):
            hT_ld_set[0] = p.dma("sync", lambda e, d=dc: e.dma_start(
                out=hT[0][d], in_=hb_d[0:GT, d * 128:(d + 1) * 128],
                transpose=True))
        for j in range(4):
            p.dma("sync", lambda e, i=j, j=j: e.dma_start(
                out=de4[0][:, j * 4 * DM:(j + 1) * 4 * DM], in_=down_bd[i]))
        for j in range(4):
            p.dma("sync", lambda e, i=4 + j, j=j: e.dma_start(
                out=de4[1][:, j * 4 * DM:(j + 1) * 4 * DM], in_=down_bd[i]))
        gu_pre = {0: emit_gu_load(8, 0), 1: emit_gu_load(9, 1)}

        dn_banks = [(Bank(p, PA), (bA0, bA1)), (Bank(p, PB), (bB0, bB1)),
                    (Bank(p, PC), (bC,)), (Bank(p, PD), (bD,))]
        out_st = [0, 0]
        de4_guard = [0]
        w8T_pe_guard = 0
        act_pe_guard = 0
        gw_pe_guard = [0, 0]
        slot_guard = [0, 0, 0]
        x1_v_guard = [0, 0]
        slot_i = 0
        pe_order = [8, 9, 10, 11, 0, 1, 2, 3, 4, 5, 6, 7]

        r3 = lambda ap: ap.rearrange("p (g x) -> p g x", g=CPG)
        bcm = c4m.unsqueeze(2).broadcast_to([128, CPG, NEXP])
        bcs = c4s.unsqueeze(2).broadcast_to([128, CPG, NEXP])

        for q in range(ngroups):
            hs = q % 2
            bD.acquire()
            if not PRE_HT and q > 0:
                p.wait("sync", "tensor", hT_guard[hs])
                for dc in range(8):
                    hT_ld_set[hs] = p.dma(
                        "sync", lambda e, d=dc, qq=q, ss=hs: e.dma_start(
                            out=hT[ss][d],
                            in_=hb_d[qq * GT:(qq + 1) * GT, d * 128:(d + 1) * 128],
                            transpose=True))
            p.wait("tensor", "sync", hT_ld_set[hs])
            pe_r = 0
            for tcg in range(CPG):
                for kc in range(8):
                    fn = lambda e, t=tcg, k=kc, ss=hs: e.matmul(
                        PD[:, t * NEXP:(t + 1) * NEXP],
                        hT[ss][k][:, t * CH:(t + 1) * CH],
                        rw[:, k * NEXP:(k + 1) * NEXP],
                        start=(k == 0), stop=(k == 7))
                    if tcg == CPG - 1 and kc == 7:
                        pe_r = p.sig("tensor", fn)
                    else:
                        p.raw("tensor", fn)

            def router_chain(q=q, pe_r=pe_r):
                p.wait("vector", "tensor", pe_r)
                v_cp = p.sig("vector", lambda e: e.tensor_copy(
                    out=c32g, in_=PD[:, 0:4 * NEXP]))
                bD.release("vector", v_cp)
                p.sig("vector", lambda e: e.tensor_tensor(
                    r3(c16), r3(c32g)[:, :, 0:4], r3(c32g)[:, :, 4:8], OP.max))
                p.sig("vector", lambda e: e.tensor_tensor(
                    r3(c8w), r3(c16)[:, :, 0:2], r3(c16)[:, :, 2:4], OP.max))
                p.sig("vector", lambda e: e.tensor_tensor(
                    r3(c4m), r3(c8w)[:, :, 0:1], r3(c8w)[:, :, 1:2], OP.max))
                v_sub = p.sig("vector", lambda e: e.tensor_tensor(
                    r3(c32a), r3(c32g), bcm, OP.subtract))
                p.wait("scalar", "vector", v_sub)
                s_exp = p.sig("scalar", lambda e: e.activation(
                    out=c32b, in_=c32a, func=AF.Exp))
                p.wait("vector", "scalar", s_exp)
                p.sig("vector", lambda e: e.tensor_tensor(
                    r3(c16), r3(c32b)[:, :, 0:4], r3(c32b)[:, :, 4:8], OP.add))
                p.sig("vector", lambda e: e.tensor_tensor(
                    r3(c8w), r3(c16)[:, :, 0:2], r3(c16)[:, :, 2:4], OP.add))
                p.sig("vector", lambda e: e.tensor_tensor(
                    r3(c4s), r3(c8w)[:, :, 0:1], r3(c8w)[:, :, 1:2], OP.add))
                p.sig("vector", lambda e: e.reciprocal(c4m, c4s))
                p.sig("vector", lambda e: e.tensor_tensor(
                    r3(w_q), r3(c32b), bcm, OP.mult))
                p.sig("vector", lambda e: e.tensor_scalar(
                    c32a, w_q, rb_col, None, OP.add))
                cur = [c32a, c32b]
                for k4 in range(4):
                    p.sig("vector", lambda e, s=cur[0]: e.tensor_tensor(
                        r3(c16), r3(s)[:, :, 0:4], r3(s)[:, :, 4:8], OP.max))
                    p.sig("vector", lambda e: e.tensor_tensor(
                        r3(c8w), r3(c16)[:, :, 0:2], r3(c16)[:, :, 2:4], OP.max))
                    p.sig("vector", lambda e: e.tensor_tensor(
                        r3(c4m), r3(c8w)[:, :, 0:1], r3(c8w)[:, :, 1:2], OP.max))
                    if k4 < 3:
                        p.sig("vector", lambda e, s=cur[0]: e.tensor_tensor(
                            r3(c32g), r3(s), bcm, OP.is_ge))
                        p.sig("vector", lambda e, s=cur[0], d=cur[1]: e.scalar_tensor_tensor(
                            d, c32g, -2.0, s, OP.mult, OP.add))
                        cur = [cur[1], cur[0]]
                p.sig("vector", lambda e, d=cur[1]: e.tensor_scalar(
                    d, w_q, rb_col, None, OP.add))
                p.sig("vector", lambda e, s=cur[1]: e.tensor_tensor(
                    r3(c32g), r3(s), bcm, OP.is_ge))
                p.sig("vector", lambda e, d=cur[1]: e.tensor_tensor(
                    d, w_q, c32g, OP.mult))
                p.sig("vector", lambda e, s=cur[1]: e.tensor_tensor(
                    r3(c16), r3(s)[:, :, 0:4], r3(s)[:, :, 4:8], OP.add))
                p.sig("vector", lambda e: e.tensor_tensor(
                    r3(c8w), r3(c16)[:, :, 0:2], r3(c16)[:, :, 2:4], OP.add))
                p.sig("vector", lambda e: e.tensor_tensor(
                    r3(c4m), r3(c8w)[:, :, 0:1], r3(c8w)[:, :, 1:2], OP.add))
                p.sig("vector", lambda e: e.tensor_scalar(
                    c4s, c4m, 1e-20, None, OP.add))
                p.sig("vector", lambda e: e.reciprocal(c4m, c4s))
                v_w = p.sig("vector", lambda e, s=cur[1]: e.tensor_tensor(
                    r3(w_q), r3(s), bcm, OP.mult))
                if tp_w is not None:
                    p.wait("sync", "vector", v_w)
                    for t5 in range(CPG):
                        tap_dma(lambda e, t=t5, q=q: e.dma_start(
                            out=tp_w[(q * GT + t * CH):(q * GT + (t + 1) * CH), :],
                            in_=w_q[:, t * NEXP:(t + 1) * NEXP]))
                return v_w

            p.wait("vector", "tensor", act_pe_guard)
            w8T_v = 0
            s_wc = 0
            last_pe_u = 0
            for idx, pe_i in enumerate(pe_order):
                wb = idx % 2
                if idx == 0:
                    v_w = router_chain()
                elif idx == 1:
                    p.wait("scalar", "vector", v_w)
                    s_wc = p.sig("scalar", lambda e: e.activation(
                        out=w8c, in_=w_q, func=AF.Copy))
                    if TWT_PD7:
                        for t4 in range(4):
                            p.wait("tensor", "scalar", s_wc)
                            t_wt = p.sig("tensor", lambda e, t=t4: e.transpose(
                                PD.bitcast(BF16)[0:NEXP, 1024 + t * 128:1024 + (t + 1) * 128],
                                w8c[:, t * NEXP:(t + 1) * NEXP], ident_b))
                            p.wait("vector", "tensor", t_wt)
                            if t4 == 0:
                                p.wait("vector", "tensor", w8T_pe_guard)
                            w8T_v = p.sig("vector", lambda e, t=t4: e.tensor_copy(
                                out=w8T[0:NEXP, t * CH:(t + 1) * CH],
                                in_=PD.bitcast(BF16)[0:NEXP, 1024 + t * 128:1024 + (t + 1) * 128]))
                    else:
                        for t4 in range(4):
                            bB0.acquire()
                            p.wait("tensor", "scalar", s_wc)
                            t_wt = p.sig("tensor", lambda e, t=t4: e.transpose(
                                PB.bitcast(BF16)[0:NEXP, 0:128],
                                w8c[:, t * NEXP:(t + 1) * NEXP], ident_b))
                            p.wait("vector", "tensor", t_wt)
                            if t4 == 0:
                                p.wait("vector", "tensor", w8T_pe_guard)
                            w8T_v = p.sig("vector", lambda e, t=t4: e.tensor_copy(
                                out=w8T[0:NEXP, t * CH:(t + 1) * CH],
                                in_=PB.bitcast(BF16)[0:NEXP, 0:128]))
                            bB0.release("vector", w8T_v)
                if idx in gu_pre:
                    g_ld = gu_pre.pop(idx)
                else:
                    p.wait("sync", "tensor", gw_pe_guard[wb])
                    g_ld = emit_gu_load(pe_i, wb)
                if pe_i < NEXP:
                    bC.acquire()
                    p.wait("tensor", "vector", w8T_v)
                    pe_wbc = p.sig("tensor", lambda e, i=pe_i: e.matmul(
                        PC[:, 0:512], sel16[:, i * 128:(i + 1) * 128], w8T,
                        start=True, stop=True))
                    w8T_pe_guard = pe_wbc
                p.wait("tensor", "sync", g_ld)
                for ic in range(4):
                    pbk = ic % 2
                    bkg = [bA0, bA1][pbk]
                    bku = [bB0, bB1][pbk]
                    bkg.acquire()
                    pe_g = 0
                    for kc in range(8):
                        fn = lambda e, i=ic, k=kc, b=bkg, w=wb, ss=hs: e.matmul(
                            b.ap, gw[w][:, k * 512 + i * 128:k * 512 + (i + 1) * 128],
                            hT[ss][k], start=(k == 0), stop=(k == 7))
                        if kc < 7:
                            p.raw("tensor", fn)
                        else:
                            pe_g = p.sig("tensor", fn)
                    bku.acquire()
                    pe_u = 0
                    for kc in range(8):
                        fn = lambda e, i=ic, k=kc, b=bku, w=wb, ss=hs: e.matmul(
                            b.ap, uw[w][:, k * 512 + i * 128:k * 512 + (i + 1) * 128],
                            hT[ss][k], start=(k == 0), stop=(k == 7))
                        if kc < 7:
                            p.raw("tensor", fn)
                        else:
                            pe_u = p.sig("tensor", fn)
                    gw_pe_guard[wb] = pe_u
                    last_pe_u = pe_u
                    slot = slot_i % 3
                    slot_i += 1
                    p.wait("scalar", "tensor", pe_g)
                    p.wait("scalar", "vector", slot_guard[slot])
                    s_sg = p.sig("scalar", lambda e, b=bkg, d=slot: e.activation(
                        out=sgs[d], in_=b.ap, func=AF.Silu))
                    bkg.release("scalar", s_sg)
                    p.wait("scalar", "tensor", pe_u)
                    s_cp = p.sig("scalar", lambda e, b=bku, d=slot: e.activation(
                        out=sus[d], in_=b.ap, func=AF.Copy))
                    bku.release("scalar", s_cp)
                    blk = pe_i * 4 + ic
                    p.wait("vector", "scalar", s_cp)
                    if pe_i < NEXP:
                        v_a1 = p.sig("vector", lambda e, d=slot: e.tensor_tensor(
                            scr512b, sgs[d], sus[d], OP.mult))
                        slot_guard[slot] = v_a1
                        v_a2 = p.sig("vector", lambda e, k=blk: e.tensor_tensor(
                            actAll[:, k * 512:(k + 1) * 512], scr512b, PC[:, 0:512], OP.mult))
                        if ic == 3:
                            bC.release("vector", v_a2)
                    else:
                        v_a2 = p.sig("vector", lambda e, k=blk, d=slot: e.tensor_tensor(
                            actAll[:, k * 512:(k + 1) * 512], sgs[d], sus[d], OP.mult))
                        slot_guard[slot] = v_a2
            hT_guard[hs] = last_pe_u
            acts_done_v = p.cnt["vector"]
            acts_done_s = p.cnt["scalar"]

            p.wait("tensor", "vector", acts_done_v)
            p.wait("tensor", "scalar", acts_done_s)
            pass_sig = [0, 0, 0]
            for ps_i in range(3):
                dbuf = 1 if ps_i == 1 else 0
                if not PRE_DE4:
                    dbuf = 0
                    p.wait("sync", "tensor", de4_guard[0])
                    for j in range(4):
                        pe_j = ps_i * 4 + j
                        if pe_j < NEXP:
                            p.dma("sync", lambda e, i=pe_j, j=j: e.dma_start(
                                out=de4[0][:, j * 4 * DM:(j + 1) * 4 * DM],
                                in_=down_bd[i]))
                        else:
                            p.dma("sync", lambda e, s=pe_j - NEXP, j=j: e.dma_start(
                                out=de4[0][:, j * 4 * DM:(j + 1) * 4 * DM],
                                in_=sd_bd[:, s * 4 * DM:(s + 1) * 4 * DM]))
                    p.wait("tensor", "sync")
                elif ps_i == 0:
                    p.wait("tensor", "sync")
                elif ps_i == 2:
                    p.wait("tensor", "sync")
                    if q + 1 < ngroups and PRE_HT:
                        p.wait("sync", "tensor", hT_guard[1 - hs])
                        for dc in range(8):
                            hT_ld_set[1 - hs] = p.dma(
                                "sync", lambda e, d=dc, qq=q + 1, ns=1 - hs: e.dma_start(
                                    out=hT[ns][d],
                                    in_=hb_d[qq * GT:(qq + 1) * GT, d * 128:(d + 1) * 128],
                                    transpose=True))
                    if q + 1 < ngroups and PRE_GU:
                        p.wait("sync", "tensor", gw_pe_guard[0])
                        gu_pre[0] = emit_gu_load(8, 0)
                        p.wait("sync", "tensor", gw_pe_guard[1])
                        gu_pre[1] = emit_gu_load(9, 1)
                mm_fns = []
                for tc in range(CPG):
                    bank_ap = [PA, PB, PC, PD][tc]
                    for j in range(4):
                        pe_j = ps_i * 4 + j
                        for ic in range(4):
                            blk = pe_j * 4 + ic
                            for nn in range(2):
                                first = (ps_i == 0 and j == 0 and ic == 0)
                                last = (ps_i == 2 and j == 3 and ic == 3)
                                mm_fns.append(lambda e, b=bank_ap, j=j, ic=ic, nn=nn,
                                      t=tc, k=blk, f=first, l=last, db=dbuf: e.matmul(
                                    b[:, nn * 512:(nn + 1) * 512],
                                    actAll[:, k * 512 + t * CH:k * 512 + (t + 1) * CH],
                                    de4[db][:, (j * 4 + ic) * DM + nn * 512:
                                         (j * 4 + ic) * DM + (nn + 1) * 512],
                                    start=f, stop=l))
                for fn_ in mm_fns[:-1]:
                    p.raw("tensor", fn_)
                pass_sig[ps_i] = p.sig("tensor", mm_fns[-1])
                de4_guard[0] = pass_sig[ps_i]
                if PRE_DE4 and ps_i == 0:
                    p.wait("sync", "tensor", pass_sig[0])
                    for j in range(4):
                        p.dma("sync", lambda e, s=j, j=j: e.dma_start(
                            out=de4[0][:, j * 4 * DM:(j + 1) * 4 * DM],
                            in_=sd_bd[:, s * 4 * DM:(s + 1) * 4 * DM]))
            pe_down_last = pass_sig[2]
            act_pe_guard = pe_down_last

            evac_sigs = []
            for tc in range(CPG):
                bank_ap = [PA, PB, PC, PD][tc]
                p.wait("vector", "tensor", pe_down_last)
                p.wait("scalar", "tensor", pe_down_last)
                if tp_ffn is not None:
                    p.sig("vector", lambda e, b=bank_ap: e.tensor_copy(out=ffn_ln, in_=b))
                    p.wait("sync", "vector")
                    tap_dma(lambda e, t=tc, q=q: e.dma_start(
                        out=tp_ffn[(q * GT + t * CH):(q * GT + (t + 1) * CH), :], in_=ffn_ln))
                s_rstd = emit_ln_stats(bank_ap, DM, eps_ln_col, ffn_ln, psum_src=True)
                p.wait("vector", "scalar", s_rstd)
                v_ln = p.sig("vector", lambda e, b=bank_ap: e.tensor_scalar(
                    ffn_ln, b, stat[:, 2:3], stat[:, 6:7], OP.subtract, OP.mult))
                evac_sigs.append(v_ln)
                p.wait("sync", "vector", x1_v_guard[tc % 2])
                x1_ld = p.dma("sync", lambda e, t=tc, q=q: e.dma_start(
                    out=x1_t[t % 2], in_=x1_d[(q * GT + t * CH):(q * GT + (t + 1) * CH), :]))
                p.wait("vector", "sync", x1_ld)
                p.wait("vector", "gpsimd", out_st[tc % 2])
                v_out = p.sig("vector", lambda e, t=tc: e.tensor_tensor(
                    out_t[t % 2], ffn_ln, x1_t[t % 2], OP.add))
                x1_v_guard[tc % 2] = v_out
                p.wait("gpsimd", "vector", v_out)
                out_st[tc % 2] = p.dma("gpsimd", lambda e, t=tc, q=q: e.dma_start(
                    out=out_h[(q * GT + t * CH):(q * GT + (t + 1) * CH), :], in_=out_t[t % 2]))
            for tc, (full, parts) in enumerate(dn_banks):
                for b_ in parts:
                    b_.release("vector", evac_sigs[tc])
            if PRE_DE4 and q + 1 < ngroups:
                p.wait("sync", "tensor", pe_down_last)
                for j in range(4):
                    p.dma("sync", lambda e, i=j, j=j: e.dma_start(
                        out=de4[0][:, j * 4 * DM:(j + 1) * 4 * DM], in_=down_bd[i]))

        p.wait("gpsimd", "gpsimd", p.cnt["gpsimd"])
        p.raw("gpsimd", lambda e: e.nop())

    return p, T


def host_layout(inputs):
    f32c = lambda a: np.ascontiguousarray(np.asarray(a), dtype=np.float32)
    W_in = f32c(inputs["L0_attn_W_in"])
    W_out = f32c(inputs["L0_attn_W_out"])
    conv_w = f32c(inputs["L0_attn_conv_w"])
    conv_b = f32c(inputs["L0_attn_conv_b"])
    b_in = f32c(inputs["L0_attn_b_in"])
    sel = np.zeros((64, NH * 128), np.float32)
    for r in range(64):
        h = r % 32
        sel[r, h * 128:(h + 1) * 128] = -1.0
    sel16 = np.zeros((16, NEXP * 128), np.float32)
    for e_ in range(NEXP):
        sel16[e_, e_ * 128:(e_ + 1) * 128] = 1.0
    strict_u = (np.arange(128)[:, None] < np.arange(128)[None, :]).astype(np.float32)
    mask_i = np.tile(-1e38 * np.eye(128, dtype=np.float32), (1, 4))
    common = {
        "W_inT": np.ascontiguousarray(W_in.T),
        "W_outT": np.ascontiguousarray(W_out.T),
        "conv_w_fm": np.ascontiguousarray(
            conv_w.reshape(18, 128, KCONV).transpose(1, 0, 2).reshape(128, 18 * KCONV)),
        "conv_b_fm": np.ascontiguousarray(conv_b.reshape(18, 128).T),
        "b_xbc_fm": np.ascontiguousarray(b_in[DIN:DIN + CONVD].reshape(18, 128).T),
        "b_z_row": np.ascontiguousarray(b_in[:DIN].reshape(1, DIN)),
        "b_dt_col": np.ascontiguousarray(b_in[DIN + CONVD:].reshape(NH, 1)),
        "dt_bias_col": np.ascontiguousarray(f32c(inputs["L0_attn_dt_bias"]).reshape(NH, 1)),
        "A_log_col": np.ascontiguousarray(f32c(inputs["L0_attn_A_log"]).reshape(NH, 1)),
        "D_row": np.ascontiguousarray(f32c(inputs["L0_attn_D"]).reshape(1, NH)),
        "nw_row": np.ascontiguousarray(f32c(inputs["L0_attn_norm_w"]).reshape(1, DIN)),
        "b_out_row": np.ascontiguousarray(f32c(inputs["L0_attn_b_out"]).reshape(1, DM)),
        "router_bias11": np.ascontiguousarray(
            f32c(inputs["L0_ffn_router_bias"]).reshape(1, 1)),
        "router_w_fm": np.ascontiguousarray(
            f32c(inputs["L0_ffn_router_w"]).reshape(8, 128, NEXP).transpose(1, 0, 2)
            .reshape(128, 8 * NEXP)),
        "gate_w": f32c(inputs["L0_ffn_gate_w"]),
        "up_w": f32c(inputs["L0_ffn_up_w"]),
        "down_w": f32c(inputs["L0_ffn_down_w"]),
        "s_gate_w": f32c(inputs["L0_ffn_s_gate_w"]),
        "s_up_w": f32c(inputs["L0_ffn_s_up_w"]),
        "s_down_w": f32c(inputs["L0_ffn_s_down_w"]),
        "ident": np.eye(64, dtype=np.float32),
        "ident_bf": np.eye(128, dtype=np.float32).astype(BF16NP),
        "tri": np.ascontiguousarray(np.tril(np.ones((128, 128), np.float32)).T).astype(BF16NP),
        "sel_bf": sel.astype(BF16NP),
        "sel16_bf": sel16.astype(BF16NP),
        "strictu_bf": strict_u.astype(BF16NP),
        "maski_bf": mask_i.astype(BF16NP),
        "ones32": np.ones((32, 128), np.float32),
    }
    u = f32c(inputs["L0_attn_u"])
    return u, common


def kernel(**inputs):
    u, common = host_layout(inputs)
    nc = bass.Bass("TRN2", target_bir_lowering=False)
    p, T = build_core(nc)
    p.build()
    in_maps = []
    for c in range(8):
        m = dict(common)
        m["u"] = np.ascontiguousarray(u[c])
        in_maps.append(m)
    res = run_bass_kernel_spmd(nc, in_maps, core_ids=list(range(8)))
    out = np.stack([np.asarray(r["out"], dtype=np.float32) for r in res.results], axis=0)
    return out


# version 07
# speedup vs baseline: 1.0734x; 1.0141x over previous
import contextlib
import numpy as np
import ml_dtypes
import concourse.bass as bass
import concourse.mybir as mybir
from concourse.bass_utils import run_bass_kernel_spmd
from concourse.alu_op_type import AluOpType as OP

F32 = mybir.dt.float32
BF16 = mybir.dt.bfloat16
AF = mybir.ActivationFunctionType
AX = mybir.AxisListType
BF16NP = ml_dtypes.bfloat16

L = 4096
DM = 1024
DIN = 2048
DST = 128
NH = 32
HD = 64
KCONV = 4
CONVD = DIN + 2 * DST
DPROJ = 2 * DIN + 2 * DST + NH
CH = 128
GT = 512
NGRP = L // GT
CPG = GT // CH
NEXP = 8
IEXP = 512
ISH = 2048
NPE = 12
EPS_LN = 1e-5
EPS_RMS = 1e-12
PRE_HT = False
PRE_GU = True
PRE_DE4 = True
TWT_PD7 = False

ENGS = ("sync", "tensor", "vector", "scalar", "gpsimd")


class Prog:
    def __init__(self, nc):
        self.nc = nc
        self.thunks = {e: [] for e in ENGS}
        self.cnt = {e: 0 for e in ENGS}
        self.cnt["gpsimdc"] = 0
        self.sem = {}
        self.last_fence = {}

    def raw(self, eng, fn):
        self.thunks[eng].append((fn, 0, None))

    def sig(self, eng, fn, inc=1):
        ch = "gpsimdc" if (eng == "gpsimd" and inc == 1) else eng
        if inc == 1 and eng in ("vector", "scalar", "gpsimd"):
            v = self.cnt[ch]
            if v > 0 and self.last_fence.get(ch + "_self", -1) < v:
                self.thunks[eng].append(
                    (lambda e, s=ch, vv=v: e.wait_ge(self.sem[s], vv), 0, None))
                self.last_fence[ch + "_self"] = v
        self.thunks[eng].append((fn, inc, ch))
        self.cnt[ch] += inc
        return self.cnt[ch]

    def dma(self, eng, fn):
        ch = eng
        self.thunks[eng].append((fn, 16, ch))
        self.cnt[ch] += 16
        return self.cnt[ch]

    def tsync(self, eng):
        self.wait(eng, eng, self.cnt[eng])

    def wait(self, eng, src, val=None):
        if val is None or src in ("sync", "gpsimd"):
            val = self.cnt[src]
            if src in ("sync", "gpsimd") and src != eng:
                if self.last_fence.get(src, -1) < val:
                    self.thunks[src].append(
                        (lambda e, s=src, v=val: e.wait_ge(self.sem[s], v), 0, None))
                    self.last_fence[src] = val
        if val <= 0:
            return
        self.thunks[eng].append((lambda e, s=src, v=val: e.wait_ge(self.sem[s], v), 0, None))

    def build(self):
        nc = self.nc
        with contextlib.ExitStack() as stack:
            for e in list(ENGS) + ["gpsimdc"]:
                self.sem[e] = stack.enter_context(nc.semaphore(f"sem_{e}"))
            with nc.Block() as block:
                for e in ENGS:
                    def body(eng, e=e):
                        for fn, inc, ch in self.thunks[e]:
                            ins = fn(eng)
                            if inc:
                                ins.then_inc(self.sem[ch], inc)
                    getattr(block, e)(body)


class Bank:

    def __init__(self, p, ap):
        self.p = p
        self.ap = ap
        self.evac = None

    def acquire(self):
        if self.evac is not None:
            self.p.wait("tensor", *self.evac)
            self.evac = None

    def release(self, eng, sig):
        self.evac = (eng, sig)


def build_core(nc, dbg=(), upto="all", ngroups=NGRP, full_groups=None):
    p = Prog(nc)
    dbgset = set(dbg)
    T = {}

    def din(name, shape, dt=F32):
        T[name] = nc.dram_tensor(name, list(shape), dt, kind="ExternalInput").ap()
        return T[name]

    def dout(name, shape, dt=F32):
        T[name] = nc.dram_tensor(name, list(shape), dt, kind="ExternalOutput").ap()
        return T[name]

    def dint(name, shape, dt):
        T[name] = nc.dram_tensor(name, list(shape), dt, kind="Internal").ap()
        return T[name]

    u_h = din("u", [L, DM])
    W_inT_h = din("W_inT", [DM, DPROJ])
    W_outT_h = din("W_outT", [DIN, DM])
    conv_w_h = din("conv_w_fm", [128, 18 * KCONV])
    conv_b_h = din("conv_b_fm", [128, 18])
    b_xbc_h = din("b_xbc_fm", [128, 18])
    b_z_h = din("b_z_row", [1, DIN])
    b_dt_h = din("b_dt_col", [NH, 1])
    dt_bias_h = din("dt_bias_col", [NH, 1])
    A_log_h = din("A_log_col", [NH, 1])
    D_h = din("D_row", [1, NH])
    nw_h = din("nw_row", [1, DIN])
    b_out_h = din("b_out_row", [1, DM])
    rw_h = din("router_w_fm", [128, 8 * NEXP])
    rb_h = din("router_bias11", [1, 1])
    gate_h = din("gate_w", [NEXP, DM, IEXP])
    up_h = din("up_w", [NEXP, DM, IEXP])
    down_h = din("down_w", [NEXP, IEXP, DM])
    sg_h = din("s_gate_w", [DM, ISH])
    su_h = din("s_up_w", [DM, ISH])
    sd_h = din("s_down_w", [ISH, DM])
    ident_h = din("ident", [64, 64])
    ident_bf_h = din("ident_bf", [128, 128], BF16)
    tri_h = din("tri", [128, 128], BF16)
    sel_h = din("sel_bf", [64, NH * 128], BF16)
    sel16_h = din("sel16_bf", [16, NEXP * 128], BF16)
    strictu_h = din("strictu_bf", [128, 128], BF16)
    maski_h = din("maski_bf", [128, 512], BF16)
    ones32_h = din("ones32", [32, 128])

    out_h = dout("out", [L, DM])

    def dtap(name, shape, dt=F32):
        return dout("dbg_" + name, shape, dt) if name in dbgset else None

    def tap_dma(fn):
        sig = p.dma("sync", fn)
        for e_ in ("vector", "scalar", "gpsimd", "tensor"):
            p.wait(e_, "sync", sig)

    tp_lnu = dtap("lnu", [L, DM], BF16)
    tp_xbc = dtap("xbc", [18 * 128, L], BF16)
    tp_z = dtap("z", [L, DIN], BF16)
    tp_dt = dtap("dt", [NH, L])
    tp_dacs = dtap("dacs", [NH, L])
    tp_y = dtap("y", [L, DIN], BF16)
    tp_yn = dtap("yn", [L, DIN], BF16)
    tp_x1 = dtap("x1", [L, DM])
    tp_h = dtap("h", [L, DM], BF16)
    tp_w = dtap("w", [L, NEXP])
    tp_ffn = dtap("ffn", [L, DM])
    tp_st = dtap("st", [128, 16])
    tp_u = dtap("u0", [128, DM])

    lnu_d = [dint(f"lnu_d{i}", [GT, DM], BF16) for i in range(2)]
    xs_d = [dint(f"xs_d{i}", [DIN, GT], BF16) for i in range(2)]
    bb_d = [dint(f"bb_d{i}", [128, GT], BF16) for i in range(2)]
    yb_d = [dint(f"yb_d{i}", [GT, DIN], BF16) for i in range(2)]
    z_d = [dint(f"z_d{i}", [GT, DIN], BF16) for i in range(2)]
    e1_d = [dint(f"e1_d{i}", [64, 128], BF16) for i in range(2)]
    x1_d = dint("x1_d", [L, DM], F32)
    hb_d = dint("hb_d", [L, DM], BF16)
    gate_bd = dint("gate_bd", [NEXP, 128, 8 * IEXP], BF16)
    up_bd = dint("up_bd", [NEXP, 128, 8 * IEXP], BF16)
    down_bd = dint("down_bd", [NEXP, 128, 4 * DM], BF16)
    sg_bd = dint("sg_bd", [128, 8 * ISH], BF16)
    su_bd = dint("su_bd", [128, 8 * ISH], BF16)
    sd_bd = dint("sd_bd", [128, 16 * DM], BF16)

    sbuf = lambda name, shape, dt=F32: nc.alloc_sbuf_tensor(name, list(shape), dt).ap()

    conv_w = sbuf("conv_w", [128, 18 * KCONV])
    conv_b = sbuf("conv_b", [128, 18])
    b_xbc = sbuf("b_xbc", [128, 18])
    b_z_bf = sbuf("b_z_bf", [1, DIN], BF16)
    dtb = sbuf("dtb", [NH, 1])
    A_col = sbuf("A_col", [NH, 1])
    D_cols = sbuf("D_cols", [128, NH])
    nw_bc = sbuf("nw_bc", [128, DIN], BF16)
    b_out_bf = sbuf("b_out_bf", [1, DM], BF16)
    rw = sbuf("rw", [128, 8 * NEXP], BF16)
    ident_f = sbuf("ident_f", [64, 64])
    ident_b = sbuf("ident_b", [128, 128], BF16)
    tri_m = sbuf("tri_m", [128, 128], BF16)
    sel16 = sbuf("sel16", [16, NEXP * 128], BF16)
    strictu = sbuf("strictu", [128, 128], BF16)
    maski = sbuf("maski", [128, 512], BF16)
    ones_row_bf = sbuf("ones_row_bf", [1, 128], BF16)
    ones32x128 = sbuf("ones32x128", [32, 128])
    eps_ln_col = sbuf("eps_ln_col", [128, 1])
    eps_rms_col = sbuf("eps_rms_col", [128, 1])
    one32c = sbuf("one32c", [NH, 1])
    rb_sb = sbuf("rb_sb", [1, 1], BF16)
    rb_col = sbuf("rb_col", [128, 1])
    stat = sbuf("stat", [128, 16])
    x1_t = [sbuf(f"x1_t{i}", [128, DM]) for i in range(2)]
    h_bf = [sbuf(f"h_bf{i}", [128, DM], BF16) for i in range(2)]
    D_row_sb = sbuf("D_row_sb", [1, NH], BF16)

    PA = nc.alloc_psum_tensor("PA", [128, 1024], F32).ap()
    PB = nc.alloc_psum_tensor("PB", [128, 1024], F32).ap()
    PC = nc.alloc_psum_tensor("PC", [128, 1024], F32).ap()
    PD = nc.alloc_psum_tensor("PD", [128, 1024], F32).ap()
    bA0, bA1 = Bank(p, PA[:, 0:512]), Bank(p, PA[:, 512:1024])
    bB0, bB1 = Bank(p, PB[:, 0:512]), Bank(p, PB[:, 512:1024])
    bC = Bank(p, PC)
    bD = Bank(p, PD)

    def tt_b(e, out, in0, col, op):
        a = col.shape[1]
        return e.tensor_tensor(
            out.rearrange("q (x y) -> q x y", x=a),
            in0.rearrange("q (x y) -> q x y", x=a),
            col.unsqueeze(2).broadcast_to([128, a, HD]), op)

    for dst, src in [(conv_w, conv_w_h), (conv_b, conv_b_h), (b_xbc, b_xbc_h),
                     (ident_f, ident_h), (ident_b, ident_bf_h), (tri_m, tri_h),
                     (ones32x128, ones32_h), (dtb, b_dt_h),
                     (A_col, A_log_h),
                     (strictu, strictu_h), (maski, maski_h),
                     (sel16, sel16_h)]:
        p.dma("sync", lambda e, d=dst, s=src: e.dma_start(out=d, in_=s))
    p.dma("sync", lambda e: e.dma_start(out=stat[0:NH, 15:16], in_=dt_bias_h))
    p.dma("sync", lambda e: e.dma_start(out=stat[0:NH, 14:15], in_=b_dt_h))
    p.dma("sync", lambda e: e.dma_start(out=stat[0:NH, 13:14], in_=A_log_h))
    p.dma("gpsimd", lambda e: e.dma_start(out=b_z_bf, in_=b_z_h))
    p.dma("gpsimd", lambda e: e.dma_start(out=b_out_bf, in_=b_out_h))
    p.dma("gpsimd", lambda e: e.dma_start(out=rw, in_=rw_h))
    p.dma("gpsimd", lambda e: e.dma_start(out=D_row_sb, in_=D_h))
    p.dma("gpsimd", lambda e: e.dma_start(out=rb_sb, in_=rb_h))
    ld_sync0 = p.cnt["sync"]

    for e_ in range(NEXP):
        p.dma("gpsimd", lambda e, i=e_: e.dma_start(
            out=gate_bd[i].rearrange("q (dc i) -> q dc i", dc=8),
            in_=gate_h[i].rearrange("(dc q) i -> q dc i", q=128)))
        p.dma("gpsimd", lambda e, i=e_: e.dma_start(
            out=up_bd[i].rearrange("q (dc i) -> q dc i", dc=8),
            in_=up_h[i].rearrange("(dc q) i -> q dc i", q=128)))
        p.dma("gpsimd", lambda e, i=e_: e.dma_start(
            out=down_bd[i].rearrange("q (ic d) -> q ic d", ic=4),
            in_=down_h[i].rearrange("(ic q) d -> q ic d", q=128)))
    p.dma("gpsimd", lambda e: e.dma_start(
        out=sg_bd.rearrange("q (dc m) -> q dc m", dc=8),
        in_=sg_h.rearrange("(dc q) m -> q dc m", q=128)))
    p.dma("gpsimd", lambda e: e.dma_start(
        out=su_bd.rearrange("q (dc m) -> q dc m", dc=8),
        in_=su_h.rearrange("(dc q) m -> q dc m", q=128)))
    p.dma("gpsimd", lambda e: e.dma_start(
        out=sd_bd.rearrange("q (mc d) -> q mc d", mc=16),
        in_=sd_h.rearrange("(mc q) d -> q mc d", q=128)))
    ffn_precast = p.cnt["gpsimd"]

    p.wait("vector", "sync", ld_sync0)
    for ap_, v in [(ones_row_bf, 1.0), (eps_ln_col, EPS_LN),
                   (eps_rms_col, EPS_RMS), (one32c, 1.0)]:
        p.sig("vector", lambda e, a=ap_, x=v: e.memset(a, x))
    p.sig("vector", lambda e: e.tensor_tensor(dtb, stat[0:NH, 14:15], stat[0:NH, 15:16], OP.add))
    p.wait("scalar", "sync", ld_sync0)
    s_a = p.sig("scalar", lambda e: e.activation(out=stat[0:NH, 12:13], in_=stat[0:NH, 13:14], func=AF.Exp))
    p.wait("vector", "scalar", s_a)
    v_init = p.sig("vector", lambda e: e.tensor_scalar_mul(A_col, stat[0:NH, 12:13], -1.0))
    p.wait("tensor", "vector", v_init)
    p.wait("tensor", "gpsimd")
    bA0.acquire()
    t_d = p.sig("tensor", lambda e: e.matmul(PA[:, 0:NH], ones_row_bf, D_row_sb, start=True, stop=True))
    p.wait("vector", "tensor", t_d)
    vd = p.sig("vector", lambda e: e.tensor_copy(out=D_cols, in_=PA[:, 0:NH]))
    bA0.release("vector", vd)
    bA0.acquire()
    t_rb = p.sig("tensor", lambda e: e.matmul(PA[:, 0:1], ones_row_bf, rb_sb, start=True, stop=True))
    p.wait("vector", "tensor", t_rb)
    v_rb = p.sig("vector", lambda e: e.tensor_copy(out=rb_col, in_=PA[:, 0:1]))
    bA0.release("vector", v_rb)

    def emit_ln_stats(src_ap, width, eps_col, dump, psum_src=False):
        v_red = p.sig("vector", lambda e, a=src_ap: e.tensor_reduce(
            stat[:, 0:1], a, AX.X, OP.add))
        if psum_src:
            p.wait("scalar", "vector", v_red)
        s_sq = p.sig("scalar", lambda e, a=src_ap, w=width, d=dump: e.activation(
            out=d[:, 0:w], in_=a, func=AF.Square, accum_out=stat[:, 1:2]))
        p.wait("vector", "scalar", s_sq)
        p.tsync("vector")
        p.sig("vector", lambda e, w=width: e.tensor_scalar_mul(stat[:, 2:3], stat[:, 0:1], 1.0 / w))
        p.sig("vector", lambda e, w=width: e.tensor_scalar_mul(stat[:, 3:4], stat[:, 1:2], 1.0 / w))
        p.tsync("vector")
        p.sig("vector", lambda e: e.tensor_tensor(stat[:, 4:5], stat[:, 2:3], stat[:, 2:3], OP.mult))
        p.tsync("vector")
        v_var = p.sig("vector", lambda e: e.tensor_tensor(stat[:, 6:7], stat[:, 3:4], stat[:, 4:5], OP.subtract))
        p.wait("scalar", "vector", v_var)
        p.sig("scalar", lambda e, c=eps_col: e.activation(out=stat[:, 5:6], in_=stat[:, 6:7], func=AF.Ln, bias=c))
        p.tsync("scalar")
        return p.sig("scalar", lambda e: e.activation(out=stat[:, 6:7], in_=stat[:, 5:6], func=AF.Exp, scale=-0.5))

    hb_done = 0

    with contextlib.ExitStack() as astack:
        asb = lambda name, shape, dt=F32: astack.enter_context(
            nc.sbuf_tensor(name, list(shape), dt)).ap()

        W_inT = asb("W_in_sb", [128, 8 * DPROJ], BF16)
        wout_b = [asb(f"wout_b{f}", [128, DM], BF16) for f in range(8)]
        u_t = [asb(f"u_t{t}", [128, DM]) for t in range(2)]
        lnuT = [[asb(f"lnuT{b}_{d}", [128, GT], BF16) for d in range(8)] for b in range(2)]
        z_sil = asb("z_sil", [128, DIN], BF16)
        B_fm = asb("B_fm", [128, GT], BF16)
        C_fm = asb("C_fm", [128, GT], BF16)
        dt_fm_g = asb("dt_fm_g", [NH, GT])
        xbc_w = [asb(f"xbc_w{i}", [128, GT + 3], BF16) for i in range(2)]
        halo = asb("halo", [128, 18 * 3], BF16)
        xsil = [asb(f"xsil{i}", [128, GT], BF16) for i in range(2)]
        tri64a = asb("tri64a", [64, 128])
        tri64b = asb("tri64b", [64, 128])
        dA_fm = asb("dA_fm", [NH, 128])
        neg_last = asb("neg_last", [NH, 1])
        tok64a = asb("tok64a", [128, 64])
        tok64b = asb("tok64b", [128, 64])
        Edec = asb("Edec", [128, 64])
        lhsE1 = asb("lhsE1", [66, 128], BF16)
        rhsE1 = asb("rhsE1", [66, NH * 128], BF16)
        x_tok = asb("x_tok", [128, DIN], BF16)
        B_tok = asb("B_tok", [128, 128], BF16)
        xdt = asb("xdt", [128, DIN], BF16)
        xdtd = asb("xdtd", [128, DIN], BF16)
        xD = asb("xD", [128, DIN], BF16)
        CBTm = asb("CBTm", [128, 128], BF16)
        es = [asb(f"es{i}", [128, GT], BF16) for i in range(2)]
        Mp = [asb(f"Mp{i}", [128, GT], BF16) for i in range(2)]
        S_f = asb("S_f", [128, DIN])
        S_bf = asb("S_bf", [128, DIN], BF16)
        ysb = asb("ysb", [128, DIN], BF16)
        tmp1k = asb("tmp1k", [128, 1024])
        y_n = [asb(f"y_n{i}", [128, DIN], BF16) for i in range(2)]
        mix_ln = tmp1k
        ynT = None
        y2_bf = xdt
        lnu_bf = [y_n[0][:, 0:DM], y_n[1][:, 0:DM]]
        conv_o = [ysb.bitcast(F32)[:, 0:GT], ysb.bitcast(F32)[:, GT:2 * GT],
                  xdtd.bitcast(F32)[:, 0:GT], xdtd.bitcast(F32)[:, GT:2 * GT]]
        sp_a = tmp1k[0:NH, 0:GT]
        scr32 = tmp1k[0:NH, 0:128]
        sp_l = tmp1k[0:NH, GT:2 * GT]
        ynT = [es[0], es[1], Mp[0], Mp[1], xsil[0], xsil[1], B_fm, C_fm]

        p.dma("gpsimd", lambda e: e.dma_start(
            out=W_inT.rearrange("q (dc f) -> q dc f", dc=8),
            in_=W_inT_h.rearrange("(dc q) f -> q dc f", q=128)))
        ld_w = p.cnt["gpsimd"]
        p.wait("tensor", "gpsimd", ld_w)
        p.wait("vector", "sync", ld_sync0)
        p.sig("vector", lambda e: e.memset(lhsE1[64:66, :], 1.0))
        p.dma("gpsimd", lambda e: e.dma_start(out=ysb[0:1, 0:DIN], in_=nw_h))
        p.wait("tensor", "gpsimd")
        for k in range(4):
            bA1.acquire()
            tk = p.sig("tensor", lambda e, k=k: e.matmul(
                PA[:, 512:1024], ones_row_bf, ysb[0:1, k * 512:(k + 1) * 512],
                start=True, stop=True))
            p.wait("vector", "tensor", tk)
            vk = p.sig("vector", lambda e, k=k: e.tensor_copy(
                out=nw_bc[:, k * 512:(k + 1) * 512], in_=PA[:, 512:1024]))
            bA1.release("vector", vk)
        p.sig("vector", lambda e: e.memset(S_f, 0.0))
        p.sig("vector", lambda e: e.memset(S_bf, 0.0))
        p.sig("vector", lambda e: e.memset(halo, 0.0))
        p.dma("sync", lambda e: e.dma_start(out=rhsE1[0:64, :], in_=sel_h))
        p.wait("tensor", "sync")
        p.wait("tensor", "vector")

        xbc_conv_v = [0, 0]
        conv_silu_s = [0, 0]
        xsil_st = [0, 0]
        lnu_st_par = [0, 0]
        yn_st = [0, 0]
        x1_par = [0, 0]
        hb_par = [0, 0]
        es_v = [0, 0]
        mp_pe = [0, 0]
        u_war_v = [0, 0]
        yn_v_last = 0
        z_rd_war = 0
        rhsE1_war = 0
        lnuT_guard = [0, 0]
        lnu_d_war = [0, 0]
        xs_d_war = [0, 0]
        bb_d_war = [0, 0]
        yb_d_war = [0, 0]
        e1_d_war = [0, 0]
        bc_fm_guard = 0
        ynT_guard = 0
        xk_v = 0
        xk_g = 0
        btok_pe = 0
        tok_g_guard = 0

        for g in range(ngroups):
            gb = g % 2
            st_last = 0
            u_norm_v = [0, 0]
            for tc in range(CPG):
                p.wait("sync", "vector", u_war_v[tc % 2])
                u_ld = p.dma("sync", lambda e, t=tc, g=g: e.dma_start(
                    out=u_t[t % 2], in_=u_h[(g * GT + t * CH):(g * GT + (t + 1) * CH), :]))
                p.wait("vector", "sync", u_ld)
                p.wait("scalar", "sync", u_ld)
                s_rstd = emit_ln_stats(u_t[tc % 2], DM, eps_ln_col, tmp1k)
                p.wait("vector", "scalar", s_rstd)
                p.wait("vector", "gpsimd", max(lnu_st_par[tc % 2], yn_st[tc % 2]))
                v_norm = p.sig("vector", lambda e, t=tc: e.tensor_scalar(
                    lnu_bf[t % 2], u_t[t % 2], stat[:, 2:3], stat[:, 6:7],
                    OP.subtract, OP.mult))
                u_war_v[tc % 2] = v_norm
                if tp_st is not None and g == 0 and tc == 0:
                    p.wait("sync", "vector", v_norm)
                    tap_dma(lambda e: e.dma_start(out=tp_st, in_=stat))
                    p.dma("sync", lambda e: e.dma_start(out=tp_u, in_=u_t[0]))
                p.wait("gpsimd", "vector", v_norm)
                st_last = p.dma("gpsimd", lambda e, t=tc, gb=gb: e.dma_start(
                    out=lnu_d[gb][t * CH:(t + 1) * CH, :], in_=lnu_bf[t % 2]))
                lnu_st_par[tc % 2] = st_last
                if tp_lnu is not None:
                    p.wait("sync", "gpsimd", st_last)
                    tap_dma(lambda e, t=tc, b=gb, g=g: e.dma_start(
                        out=tp_lnu[(g * GT + t * CH):(g * GT + (t + 1) * CH), :],
                        in_=lnu_d[b][t * CH:(t + 1) * CH, :]))
            p.wait("sync", "gpsimd", st_last)
            p.wait("sync", "tensor", lnuT_guard[gb])
            lnuT_ld = 0
            for dc in range(8):
                lnuT_ld = p.dma("sync", lambda e, d=dc, gb=gb: e.dma_start(
                    out=lnuT[gb][d], in_=lnu_d[gb][:, d * 128:(d + 1) * 128], transpose=True))
            lnu_d_war[gb] = lnuT_ld

            p.wait("tensor", "sync", lnuT_ld)
            for fc in range(18):
                pb = fc % 2
                bank = [bA0, bA1][pb]
                bank.acquire()
                pe_s = 0
                for kc in range(8):
                    fn = lambda e, f=fc, k=kc, b=bank, gb=gb: e.matmul(
                        b.ap, W_inT[:, k * DPROJ + DIN + f * 128: k * DPROJ + DIN + (f + 1) * 128],
                        lnuT[gb][k], start=(k == 0), stop=(k == 7))
                    if kc < 7:
                        p.raw("tensor", fn)
                    else:
                        pe_s = p.sig("tensor", fn)
                p.wait("scalar", "tensor", pe_s)
                p.wait("scalar", "vector", xbc_conv_v[pb])
                ev = p.sig("scalar", lambda e, f=fc, b=bank: e.activation(
                    out=xbc_w[f % 2][:, 3:GT + 3], in_=b.ap, func=AF.Identity, bias=b_xbc[:, f:f + 1]))
                bank.release("scalar", ev)
                p.wait("vector", "scalar", ev)
                p.sig("vector", lambda e, f=fc: e.tensor_copy(
                    out=xbc_w[f % 2][:, 0:3], in_=halo[:, f * 3:f * 3 + 3]))
                ca, cb = conv_o[2 * pb], conv_o[2 * pb + 1]
                p.wait("vector", "scalar", conv_silu_s[pb])
                p.tsync("vector")
                p.sig("vector", lambda e, f=fc, a=ca: e.tensor_scalar(
                    a, xbc_w[f % 2][:, 0:GT], conv_w[:, f * 4:f * 4 + 1],
                    conv_b[:, f:f + 1], OP.mult, OP.add))
                p.sig("vector", lambda e, f=fc, a=ca, b=cb: e.scalar_tensor_tensor(
                    b, xbc_w[f % 2][:, 1:GT + 1], conv_w[:, f * 4 + 1:f * 4 + 2], a, OP.mult, OP.add))
                p.sig("vector", lambda e, f=fc, a=ca, b=cb: e.scalar_tensor_tensor(
                    a, xbc_w[f % 2][:, 2:GT + 2], conv_w[:, f * 4 + 2:f * 4 + 3], b, OP.mult, OP.add))
                cv = p.sig("vector", lambda e, f=fc, a=ca, b=cb: e.scalar_tensor_tensor(
                    b, xbc_w[f % 2][:, 3:GT + 3], conv_w[:, f * 4 + 3:f * 4 + 4], a, OP.mult, OP.add))
                hv = p.sig("vector", lambda e, f=fc: e.tensor_copy(
                    out=halo[:, f * 3:f * 3 + 3], in_=xbc_w[f % 2][:, GT:GT + 3]))
                xbc_conv_v[pb] = hv
                p.wait("scalar", "vector", cv)
                if fc < 16:
                    p.wait("scalar", "gpsimd", xsil_st[pb])
                    sl = p.sig("scalar", lambda e, f=fc, b=cb: e.activation(
                        out=xsil[f % 2], in_=b, func=AF.Silu))
                    conv_silu_s[pb] = sl
                    p.wait("gpsimd", "scalar", sl)
                    p.wait("gpsimd", "sync", xs_d_war[gb])
                    xsil_st[pb] = p.dma("gpsimd", lambda e, f=fc, gb=gb: e.dma_start(
                        out=xs_d[gb][f * 128:(f + 1) * 128, :], in_=xsil[f % 2]))
                    if tp_xbc is not None:
                        p.wait("sync", "scalar", sl)
                        tap_dma(lambda e, f=fc, g=g: e.dma_start(
                            out=tp_xbc[f * 128:(f + 1) * 128, g * GT:(g + 1) * GT],
                            in_=xsil[f % 2]))
                else:
                    dst = B_fm if fc == 16 else C_fm
                    p.wait("scalar", "tensor", bc_fm_guard)
                    sl = p.sig("scalar", lambda e, b=cb, d=dst: e.activation(
                        out=d, in_=b, func=AF.Silu))
                    conv_silu_s[pb] = sl
                    if fc == 16:
                        p.wait("gpsimd", "scalar", sl)
                        p.wait("gpsimd", "sync", bb_d_war[gb])
                        p.dma("gpsimd", lambda e, gb=gb: e.dma_start(out=bb_d[gb], in_=B_fm))
                    if tp_xbc is not None:
                        p.wait("sync", "scalar", sl)
                        tap_dma(lambda e, f=fc, d=dst, g=g: e.dma_start(
                            out=tp_xbc[f * 128:(f + 1) * 128, g * GT:(g + 1) * GT], in_=d))
            xs_store_done = p.cnt["gpsimd"]

            zstage = [xdt, ysb]
            zst_par = [0, 0]
            for tc in range(CPG):
                zp = tc % 2
                for ng in range(4):
                    pb = (tc * 4 + ng) % 2
                    bank = [bA0, bA1][pb]
                    bank.acquire()
                    for kc in range(8):
                        p.raw("tensor", lambda e, t=tc, n=ng, k=kc, b=bank, gb=gb: e.matmul(
                            b.ap, lnuT[gb][k][:, t * CH:(t + 1) * CH],
                            W_inT[:, k * DPROJ + n * 512:k * DPROJ + (n + 1) * 512],
                            start=(k == 0), stop=False))
                    pe_s = p.sig("tensor", lambda e, n=ng, b=bank: e.matmul(
                        b.ap, ones_row_bf, b_z_bf[:, n * 512:(n + 1) * 512],
                        start=False, stop=True))
                    p.wait("scalar", "tensor", pe_s)
                    if g > 0 and tc == 0 and ng == 0:
                        p.wait("scalar", "vector", yn_v_last)
                    p.wait("scalar", "gpsimd", zst_par[zp])
                    sz = p.sig("scalar", lambda e, t=tc, n=ng, b=bank, s=zstage[zp]: e.activation(
                        out=s[:, n * 512:(n + 1) * 512], in_=b.ap, func=AF.Silu))
                    bank.release("scalar", sz)
                p.wait("gpsimd", "scalar", sz)
                zst_par[zp] = p.dma("gpsimd", lambda e, t=tc, s=zstage[zp], gb=gb: e.dma_start(
                    out=z_d[gb][t * CH:(t + 1) * CH, :], in_=s))
                if tp_z is not None:
                    p.wait("sync", "scalar", sz)
                    tap_dma(lambda e, t=tc, s=zstage[zp], g=g: e.dma_start(
                        out=tp_z[(g * GT + t * CH):(g * GT + (t + 1) * CH), :], in_=s))
            z_store_done = p.cnt["gpsimd"]

            bB0.acquire()
            pe_dt = 0
            for kc in range(8):
                fn = lambda e, k=kc, gb=gb: e.matmul(
                    PB[0:NH, 0:512], W_inT[:, k * DPROJ + DIN + CONVD:k * DPROJ + DPROJ],
                    lnuT[gb][k], start=(k == 0), stop=(k == 7))
                if kc < 7:
                    p.raw("tensor", fn)
                else:
                    pe_dt = p.sig("tensor", fn)
            lnuT_guard[gb] = pe_dt
            p.wait("scalar", "tensor", pe_dt)
            p.sig("scalar", lambda e: e.activation(
                out=sp_a, in_=PB[0:NH, 0:512], func=AF.Abs, bias=dtb))
            p.sig("scalar", lambda e: e.activation(out=sp_l, in_=sp_a, func=AF.Exp, scale=-1.0))
            p.sig("scalar", lambda e: e.activation(out=sp_a, in_=sp_l, func=AF.Ln, bias=one32c))
            s_r = p.sig("scalar", lambda e: e.activation(
                out=sp_l, in_=PB[0:NH, 0:512], func=AF.Relu, bias=dtb))
            bB0.release("scalar", s_r)
            p.wait("vector", "scalar", s_r)
            v_dt = p.sig("vector", lambda e: e.tensor_tensor(dt_fm_g, sp_a, sp_l, OP.add))
            if tp_dt is not None:
                p.wait("sync", "vector", v_dt)
                tap_dma(lambda e, g=g: e.dma_start(
                    out=tp_dt[:, g * GT:(g + 1) * GT], in_=dt_fm_g))

            if full_groups is not None and g >= full_groups:
                continue
            for cc in range(CPG):
                ch = g * CPG + cc
                cb2 = ch % 2
                if cc == 0:
                    p.wait("gpsimd", "scalar")
                p.sig("vector", lambda e, c=cc: e.tensor_scalar_mul(
                    dA_fm, dt_fm_g[:, c * CH:(c + 1) * CH], A_col))
                p.tsync("vector")
                p.sig("vector", lambda e: e.tensor_tensor_scan(
                    tri64a[0:32, :], ones32x128, dA_fm, 0.0, OP.mult, OP.add))
                p.sig("vector", lambda e, c=cc: e.tensor_copy(
                    out=tri64b[0:32, :], in_=dt_fm_g[:, c * CH:(c + 1) * CH]))
                if tp_dacs is not None:
                    p.wait("sync", "vector")
                    tap_dma(lambda e, c=ch: e.dma_start(
                        out=tp_dacs[:, c * CH:(c + 1) * CH], in_=tri64a[0:32, :]))
                p.wait("scalar", "vector")
                s_hi = p.sig("scalar", lambda e: e.activation(
                    out=lhsE1[0:32, :], in_=tri64a[0:32, :], func=AF.Copy))
                p.wait("vector", "scalar", s_hi)
                p.tsync("vector")
                v_lo = p.sig("vector", lambda e: e.tensor_tensor(
                    lhsE1[32:64, :], tri64a[0:32, :], lhsE1[0:32, :], OP.subtract))
                p.wait("gpsimd", "vector", v_lo)
                p.wait("gpsimd", "sync", e1_d_war[cb2])
                g_st = p.dma("gpsimd", lambda e, b=cb2: e.dma_start(
                    out=e1_d[b], in_=lhsE1[0:64, :]))
                p.wait("sync", "gpsimd", g_st)
                p.wait("sync", "tensor", rhsE1_war)
                rhs_ld = p.dma("sync", lambda e, b=cb2: e.dma_start(
                    out=rhsE1[64:66, :],
                    in_=e1_d[b].rearrange("(s h) i -> s (h i)", s=2)))
                e1_d_war[cb2] = rhs_ld
                p.tsync("vector")
                p.sig("vector", lambda e: e.tensor_copy(
                    out=neg_last, in_=tri64a[0:32, 127:128]))
                p.sig("vector", lambda e: e.tensor_scalar_mul(
                    tri64a[32:64, :], ones32x128, tri64a[0:32, 127:128]))
                p.wait("scalar", "vector")
                s_wd = p.sig("scalar", lambda e: e.activation(
                    out=scr32, in_=tri64a[0:32, :], func=AF.Exp, scale=-1.0, bias=neg_last))
                p.wait("vector", "scalar", s_wd)
                p.wait("vector", "gpsimdc", tok_g_guard)
                v_wdt = p.sig("vector", lambda e: e.tensor_tensor(
                    tri64b[32:64, :], scr32, tri64b[0:32, :], OP.mult))
                p.wait("tensor", "vector", v_wdt)
                bB0.acquire()
                t_t1 = p.sig("tensor", lambda e: e.transpose(
                    PB[:, 0:64], tri64a, ident_f))
                p.wait("vector", "tensor", t_t1)
                v_t1 = p.sig("vector", lambda e: e.tensor_copy(out=tok64a, in_=PB[:, 0:64]))
                bB0.release("vector", v_t1)
                bB0.acquire()
                t_t2 = p.sig("tensor", lambda e: e.transpose(
                    PB[:, 0:64], tri64b, ident_f))
                p.wait("vector", "tensor", t_t2)
                v_t2 = p.sig("vector", lambda e: e.tensor_copy(out=tok64b, in_=PB[:, 0:64]))
                bB0.release("vector", v_t2)
                p.wait("scalar", "vector", v_t1)
                s_ed = p.sig("scalar", lambda e: e.activation(out=Edec, in_=tok64a, func=AF.Exp))

                p.wait("sync", "gpsimd", xs_store_done)
                p.wait("sync", "vector", xk_v)
                p.wait("sync", "gpsimdc", xk_g)
                x_ld = p.dma("sync", lambda e, c=cc, gb=gb: e.dma_start(
                    out=x_tok, in_=xs_d[gb][:, c * CH:(c + 1) * CH], transpose=True))
                xs_d_war[gb] = x_ld
                p.wait("sync", "tensor", btok_pe)
                b_ld = p.dma("sync", lambda e, c=cc, gb=gb: e.dma_start(
                    out=B_tok, in_=bb_d[gb][:, c * CH:(c + 1) * CH], transpose=True))
                bb_d_war[gb] = b_ld
                p.wait("sync", "gpsimd", z_store_done)
                p.wait("sync", "vector", z_rd_war)
                z_ld = p.dma("sync", lambda e, c=cc, gb=gb: e.dma_start(
                    out=z_sil, in_=z_d[gb][c * CH:(c + 1) * CH, :]))
                p.wait("vector", "sync", x_ld)
                p.wait("vector", "gpsimd", z_store_done)
                p.tsync("vector")
                xk_v = p.sig("vector", lambda e: tt_b(
                    e, xdt, x_tok, tok64b[:, 0:32], OP.mult))
                p.wait("gpsimd", "sync", x_ld)
                p.wait("gpsimd", "vector", v_t2)
                g_xdtd = p.sig("gpsimd", lambda e: tt_b(
                    e, xdtd, x_tok, tok64b[:, 32:64], OP.mult))
                g_xd = p.sig("gpsimd", lambda e: tt_b(
                    e, xD, x_tok, D_cols, OP.mult))
                xk_g = g_xd
                tok_g_guard = g_xdtd
                bB0.acquire()
                t_cbt = p.sig("tensor", lambda e, c=cc: e.matmul(
                    PB[:, 0:128], B_fm[:, c * CH:(c + 1) * CH], C_fm[:, c * CH:(c + 1) * CH],
                    start=True, stop=True))
                p.wait("vector", "tensor", t_cbt)
                v_cbt = p.sig("vector", lambda e: e.tensor_tensor(
                    CBTm, PB[:, 0:128], tri_m, OP.mult))
                bB0.release("vector", v_cbt)

                p.wait("tensor", "sync", rhs_ld)
                pe_yoff = pe_ydiag = pe_st = 0
                for hf in range(2):
                    ho = hf * 1024
                    for q4 in range(4):
                        qb = q4 % 2
                        bank = [bB0, bB1][qb]
                        bank.acquire()
                        p.raw("tensor", lambda e, h=hf, q=q4, b=bank: e.matmul(
                            b.ap, lhsE1,
                            rhsE1[:, (h * 16 + q * 4) * 128:(h * 16 + q * 4 + 4) * 128],
                            start=True, stop=False))
                        pe_e1 = p.sig("tensor", lambda e, b=bank: e.matmul(
                            b.ap, strictu, maski, start=False, stop=True))
                        rhsE1_war = pe_e1
                        p.wait("scalar", "tensor", pe_e1)
                        p.wait("scalar", "vector", es_v[qb])
                        s_es = p.sig("scalar", lambda e, q=q4, b=bank: e.activation(
                            out=es[q % 2], in_=b.ap, func=AF.Exp))
                        bank.release("scalar", s_es)
                        p.wait("vector", "scalar", s_es)
                        p.wait("vector", "tensor", mp_pe[qb])
                        p.tsync("vector")
                        v_mp = p.sig("vector", lambda e, q=q4: e.scalar_tensor_tensor(
                            Mp[q % 2].rearrange("q (a b) -> q a b", a=4),
                            es[q % 2].rearrange("q (a b) -> q a b", a=4), 1e30,
                            CBTm.unsqueeze(1).broadcast_to([128, 4, 128]),
                            OP.min, OP.mult))
                        es_v[qb] = v_mp
                        p.wait("tensor", "vector", v_mp)
                        if q4 == 0:
                            bC.acquire()
                        for hh in range(4):
                            h16 = q4 * 4 + hh
                            h = hf * 16 + h16
                            fn = lambda e, q=q4, hh=hh, h=h, h16=h16: e.matmul(
                                PC[:, h16 * 64:(h16 + 1) * 64],
                                Mp[q % 2][:, hh * 128:(hh + 1) * 128],
                                xdt[:, h * 64:(h + 1) * 64],
                                start=(h16 % 8 == 0), stop=False)
                            if hh < 3:
                                p.raw("tensor", fn)
                            else:
                                mp_pe[qb] = p.sig("tensor", fn)
                    p.wait("tensor", "gpsimdc", g_xd)
                    for nn in range(2):
                        fn = lambda e, n=nn, o=ho: e.matmul(
                            PC[:, n * 512:(n + 1) * 512], ident_b,
                            xD[:, o + n * 512:o + (n + 1) * 512], start=False, stop=True)
                        if nn == 0:
                            p.raw("tensor", fn)
                        else:
                            pe_ydiag = p.sig("tensor", fn)
                    bD.acquire()
                    for nn in range(2):
                        fn = lambda e, n=nn, c=cc, o=ho: e.matmul(
                            PD[:, n * 512:(n + 1) * 512], C_fm[:, c * CH:(c + 1) * CH],
                            S_bf[:, o + n * 512:o + (n + 1) * 512], start=True, stop=True)
                        if nn == 0:
                            p.raw("tensor", fn)
                        else:
                            pe_yoff = p.sig("tensor", fn)
                    p.wait("vector", "tensor", pe_yoff)
                    p.wait("vector", "scalar", s_ed)
                    v_o1 = p.sig("vector", lambda e, h=hf: tt_b(
                        e, tmp1k, PD, Edec[:, h * 16:(h + 1) * 16], OP.mult))
                    bD.release("vector", v_o1)
                    bD.acquire()
                    p.wait("tensor", "sync", b_ld)
                    p.wait("tensor", "gpsimdc", g_xdtd)
                    for nn in range(2):
                        fn = lambda e, n=nn, o=ho: e.matmul(
                            PD[:, n * 512:(n + 1) * 512], B_tok,
                            xdtd[:, o + n * 512:o + (n + 1) * 512], start=True, stop=True)
                        if nn == 0:
                            p.raw("tensor", fn)
                        else:
                            pe_st = p.sig("tensor", fn)
                    btok_pe = pe_st
                    p.wait("vector", "tensor", pe_ydiag)
                    v_o2 = p.sig("vector", lambda e, o=ho: e.tensor_tensor(
                        ysb[:, o:o + 1024], tmp1k, PC, OP.add))
                    bC.release("vector", v_o2)
                    p.sig("vector", lambda e, h=hf, o=ho: tt_b(
                        e, tmp1k, S_f[:, o:o + 1024],
                        Edec[:, 32 + h * 16:32 + (h + 1) * 16], OP.mult))
                    p.wait("vector", "tensor", pe_st)
                    v_sf = p.sig("vector", lambda e, o=ho: e.tensor_tensor(
                        S_f[:, o:o + 1024], tmp1k, PD, OP.add))
                    bD.release("vector", v_sf)
                    p.wait("scalar", "vector", v_sf)
                    p.wait("scalar", "tensor", pe_yoff)
                    p.sig("scalar", lambda e, o=ho: e.activation(
                        out=S_bf[:, o:o + 1024], in_=S_f[:, o:o + 1024], func=AF.Copy))
                bc_fm_guard = pe_yoff

                if tp_y is not None:
                    p.wait("sync", "vector")
                    tap_dma(lambda e, c=ch: e.dma_start(
                        out=tp_y[c * CH:(c + 1) * CH, :], in_=ysb))
                p.wait("vector", "sync", z_ld)
                v_y2 = p.sig("vector", lambda e: e.tensor_tensor(
                    y2_bf, ysb, z_sil, OP.mult))
                z_rd_war = v_y2
                p.wait("scalar", "vector", v_y2)
                p.sig("scalar", lambda e: e.activation(
                    out=xdtd, in_=y2_bf, func=AF.Square, accum_out=stat[:, 7:8]))
                p.tsync("scalar")
                p.sig("scalar", lambda e: e.activation(
                    out=stat[:, 8:9], in_=stat[:, 7:8], func=AF.Ln,
                    bias=eps_rms_col, scale=1.0 / DIN))
                p.tsync("scalar")
                s_rs = p.sig("scalar", lambda e: e.activation(
                    out=stat[:, 9:10], in_=stat[:, 8:9], func=AF.Exp, scale=-0.5))
                p.wait("vector", "scalar", s_rs)
                p.wait("vector", "gpsimd", max(yn_st[cc % 2], lnu_st_par[cc % 2]))
                v_yn = p.sig("vector", lambda e, c=cc: e.scalar_tensor_tensor(
                    y_n[c % 2], y2_bf, stat[:, 9:10], nw_bc, OP.mult, OP.mult))
                yn_v_last = v_yn
                p.wait("gpsimd", "vector", v_yn)
                p.wait("gpsimd", "sync", yb_d_war[gb])
                yn_st[cc % 2] = p.dma("gpsimd", lambda e, c=cc, gb=gb: e.dma_start(
                    out=yb_d[gb][c * CH:(c + 1) * CH, :], in_=y_n[c % 2]))
                if tp_yn is not None:
                    p.wait("sync", "vector", v_yn)
                    tap_dma(lambda e, c=ch, cc=cc: e.dma_start(
                        out=tp_yn[c * CH:(c + 1) * CH, :], in_=y_n[cc % 2]))

            p.wait("tensor", "vector")
            p.wait("tensor", "scalar")
            pe_op = 0
            for fp in range(2):
                p.wait("sync", "gpsimd", max(yn_st))
                p.wait("sync", "tensor")
                ynT_ld = 0
                for f8 in range(8):
                    ynT_ld = p.dma("sync", lambda e, f=f8, o=fp * 8, gb=gb: e.dma_start(
                        out=ynT[f], in_=yb_d[gb][:, (o + f) * 128:(o + f + 1) * 128],
                        transpose=True))
                yb_d_war[gb] = ynT_ld
                p.wait("gpsimd", "tensor", ynT_guard)
                wout_ld = 0
                for f8 in range(8):
                    wout_ld = p.dma("gpsimd", lambda e, f=f8, o=fp * 8: e.dma_start(
                        out=wout_b[f], in_=W_outT_h[(o + f) * 128:(o + f + 1) * 128, :]))
                p.wait("tensor", "sync", ynT_ld)
                p.wait("tensor", "gpsimd", wout_ld)
                mm_fns = []
                for tc in range(CPG):
                    bank_ap = [PA, PB, PC, PD][tc]
                    for ng in range(2):
                        for f8 in range(8):
                            mm_fns.append(lambda e, b=bank_ap, t=tc, n=ng, f=f8,
                                          fp=fp: e.matmul(
                                b[:, n * 512:(n + 1) * 512],
                                ynT[f][:, t * CH:(t + 1) * CH],
                                wout_b[f][:, n * 512:(n + 1) * 512],
                                start=(fp == 0 and f == 0), stop=False))
                        if fp == 1:
                            mm_fns.append(lambda e, b=bank_ap, n=ng: e.matmul(
                                b[:, n * 512:(n + 1) * 512], ones_row_bf,
                                b_out_bf[:, n * 512:(n + 1) * 512], start=False, stop=True))
                for fn_ in mm_fns[:-1]:
                    p.raw("tensor", fn_)
                pe_op = p.sig("tensor", mm_fns[-1])
                ynT_guard = pe_op
            evac_v = []
            for tc in range(CPG):
                bank_ap = [PA, PB, PC, PD][tc]
                p.wait("vector", "tensor", pe_op)
                p.wait("scalar", "tensor", pe_op)
                s_rstd = emit_ln_stats(bank_ap, DM, eps_ln_col, tmp1k, psum_src=True)
                p.wait("vector", "scalar", s_rstd)
                v_ml = p.sig("vector", lambda e, b=bank_ap: e.tensor_scalar(
                    mix_ln, b, stat[:, 2:3], stat[:, 6:7], OP.subtract, OP.mult))
                evac_v.append(v_ml)
                p.wait("sync", "vector", u_war_v[tc % 2])
                u_ld2 = p.dma("sync", lambda e, t=tc, g=g: e.dma_start(
                    out=u_t[t % 2], in_=u_h[(g * GT + t * CH):(g * GT + (t + 1) * CH), :]))
                p.wait("vector", "sync", u_ld2)
                p.wait("vector", "gpsimd", x1_par[tc % 2])
                v_x1 = p.sig("vector", lambda e, t=tc: e.tensor_tensor(
                    x1_t[t % 2], mix_ln, u_t[t % 2], OP.add))
                u_war_v[tc % 2] = v_x1
                p.wait("gpsimd", "vector", v_x1)
                x1_par[tc % 2] = p.dma("gpsimd", lambda e, t=tc, g=g: e.dma_start(
                    out=x1_d[(g * GT + t * CH):(g * GT + (t + 1) * CH), :], in_=x1_t[t % 2]))
                if tp_x1 is not None:
                    p.wait("sync", "vector", v_x1)
                    tap_dma(lambda e, t=tc, g=g: e.dma_start(
                        out=tp_x1[(g * GT + t * CH):(g * GT + (t + 1) * CH), :], in_=x1_t[t % 2]))
                p.wait("scalar", "vector", v_x1)
                s_rstd2 = emit_ln_stats(x1_t[tc % 2], DM, eps_ln_col, tmp1k)
                p.wait("vector", "scalar", s_rstd2)
                p.wait("vector", "gpsimd", hb_par[tc % 2])
                v_h = p.sig("vector", lambda e, t=tc: e.tensor_scalar(
                    h_bf[t % 2], x1_t[t % 2], stat[:, 2:3], stat[:, 6:7], OP.subtract, OP.mult))
                p.wait("gpsimd", "vector", v_h)
                hb_par[tc % 2] = p.dma("gpsimd", lambda e, t=tc, g=g: e.dma_start(
                    out=hb_d[(g * GT + t * CH):(g * GT + (t + 1) * CH), :], in_=h_bf[t % 2]))
                if tp_h is not None:
                    p.wait("sync", "vector", v_h)
                    tap_dma(lambda e, t=tc, g=g: e.dma_start(
                        out=tp_h[(g * GT + t * CH):(g * GT + (t + 1) * CH), :], in_=h_bf[t % 2]))
            bA0.evac = bA1.evac = ("vector", evac_v[0])
            bB0.evac = bB1.evac = ("vector", evac_v[1])
            bC.evac = ("vector", evac_v[2])
            bD.evac = ("vector", evac_v[3])

        hb_done = p.cnt["gpsimd"]

    if upto == "a":
        p.wait("gpsimd", "sync", p.cnt["sync"])
        p.wait("gpsimd", "gpsimd", p.cnt["gpsimd"])
        p.raw("gpsimd", lambda e: e.nop())
        return p, T

    with contextlib.ExitStack() as bstack:
        bsb = lambda name, shape, dt=F32: bstack.enter_context(
            nc.sbuf_tensor(name, list(shape), dt)).ap()

        hT = [[bsb(f"hT{s5}_{d}", [128, GT], BF16) for d in range(8)]
              for s5 in range(2)]
        gw = [bsb(f"gw{i}", [128, 8 * IEXP], BF16) for i in range(2)]
        uw = [bsb(f"uw{i}", [128, 8 * IEXP], BF16) for i in range(2)]
        de4 = [bsb(f"de4{s5}", [128, 16 * DM], BF16) for s5 in range(2)]
        actAll = bsb("actAll", [128, NPE * 4 * 512], BF16)
        sgs = [bsb(f"sgs{i}", [128, 512], BF16) for i in range(3)]
        sus = [bsb(f"sus{i}", [128, 512], BF16) for i in range(3)]
        nw32 = nw_bc.bitcast(F32)
        c32a = nw32[:, 0:32]
        c32b = nw32[:, 32:64]
        c32g = nw32[:, 64:96]
        c16 = nw32[:, 96:112]
        c8w = nw32[:, 112:120]
        c4m = nw32[:, 120:124]
        c4s = nw32[:, 124:128]
        scr512b = nw_bc[:, 1024:1536]
        w_q = bsb("w_q", [128, 4 * NEXP])
        w8c = bsb("w8c", [128, 4 * NEXP], BF16)
        w8T = bsb("w8T", [16, GT], BF16)
        ffn_ln = bsb("ffn_ln", [128, DM])
        out_t = [bsb(f"out_t{i}", [128, DM]) for i in range(2)]

        p.wait("gpsimd", "gpsimd", ffn_precast)
        p.wait("sync", "gpsimd", hb_done)
        p.wait("sync", "vector")
        p.sig("vector", lambda e: e.memset(w8T, 0.0))

        def emit_gu_load(pe_i, b):
            if pe_i < NEXP:
                p.dma("sync", lambda e, i=pe_i, b=b: e.dma_start(
                    out=gw[b], in_=gate_bd[i]))
                return p.dma("sync", lambda e, i=pe_i, b=b: e.dma_start(
                    out=uw[b], in_=up_bd[i]))
            sp = pe_i - NEXP
            p.dma("sync", lambda e, s=sp, b=b: e.dma_start(
                out=gw[b].rearrange("q (dc m) -> q dc m", dc=8),
                in_=sg_bd.rearrange("q (dc m) -> q dc m", dc=8)[:, :, s * 512:(s + 1) * 512]))
            return p.dma("sync", lambda e, s=sp, b=b: e.dma_start(
                out=uw[b].rearrange("q (dc m) -> q dc m", dc=8),
                in_=su_bd.rearrange("q (dc m) -> q dc m", dc=8)[:, :, s * 512:(s + 1) * 512]))

        hT_ld_set = [0, 0]
        hT_guard = [0, 0]
        for dc in range(8):
            hT_ld_set[0] = p.dma("sync", lambda e, d=dc: e.dma_start(
                out=hT[0][d], in_=hb_d[0:GT, d * 128:(d + 1) * 128],
                transpose=True))
        for j in range(4):
            p.dma("sync", lambda e, i=j, j=j: e.dma_start(
                out=de4[0][:, j * 4 * DM:(j + 1) * 4 * DM], in_=down_bd[i]))
        for j in range(4):
            p.dma("sync", lambda e, i=4 + j, j=j: e.dma_start(
                out=de4[1][:, j * 4 * DM:(j + 1) * 4 * DM], in_=down_bd[i]))
        gu_pre = {0: emit_gu_load(8, 0), 1: emit_gu_load(9, 1)}

        dn_banks = [(Bank(p, PA), (bA0, bA1)), (Bank(p, PB), (bB0, bB1)),
                    (Bank(p, PC), (bC,)), (Bank(p, PD), (bD,))]
        out_st = [0, 0]
        de4_guard = [0]
        w8T_pe_guard = 0
        act_pe_guard = 0
        gw_pe_guard = [0, 0]
        slot_guard = [0, 0, 0]
        x1_v_guard = [0, 0]
        slot_i = 0
        pe_order = [8, 9, 10, 11, 0, 1, 2, 3, 4, 5, 6, 7]

        r3 = lambda ap: ap.rearrange("p (g x) -> p g x", g=CPG)
        bcm = c4m.unsqueeze(2).broadcast_to([128, CPG, NEXP])
        bcs = c4s.unsqueeze(2).broadcast_to([128, CPG, NEXP])

        for q in range(ngroups):
            hs = q % 2
            bD.acquire()
            if not PRE_HT and q > 0:
                p.wait("sync", "tensor", hT_guard[hs])
                for dc in range(8):
                    hT_ld_set[hs] = p.dma(
                        "sync", lambda e, d=dc, qq=q, ss=hs: e.dma_start(
                            out=hT[ss][d],
                            in_=hb_d[qq * GT:(qq + 1) * GT, d * 128:(d + 1) * 128],
                            transpose=True))
            p.wait("tensor", "sync", hT_ld_set[hs])
            pe_r = 0
            for tcg in range(CPG):
                for kc in range(8):
                    fn = lambda e, t=tcg, k=kc, ss=hs: e.matmul(
                        PD[:, t * NEXP:(t + 1) * NEXP],
                        hT[ss][k][:, t * CH:(t + 1) * CH],
                        rw[:, k * NEXP:(k + 1) * NEXP],
                        start=(k == 0), stop=(k == 7))
                    if tcg == CPG - 1 and kc == 7:
                        pe_r = p.sig("tensor", fn)
                    else:
                        p.raw("tensor", fn)

            def router_chain(q=q, pe_r=pe_r):
                p.wait("vector", "tensor", pe_r)
                v_cp = p.sig("vector", lambda e: e.tensor_copy(
                    out=c32g, in_=PD[:, 0:4 * NEXP]))
                bD.release("vector", v_cp)
                p.sig("vector", lambda e: e.tensor_tensor(
                    r3(c16), r3(c32g)[:, :, 0:4], r3(c32g)[:, :, 4:8], OP.max))
                p.sig("vector", lambda e: e.tensor_tensor(
                    r3(c8w), r3(c16)[:, :, 0:2], r3(c16)[:, :, 2:4], OP.max))
                p.sig("vector", lambda e: e.tensor_tensor(
                    r3(c4m), r3(c8w)[:, :, 0:1], r3(c8w)[:, :, 1:2], OP.max))
                v_sub = p.sig("vector", lambda e: e.tensor_tensor(
                    r3(c32a), r3(c32g), bcm, OP.subtract))
                p.wait("scalar", "vector", v_sub)
                s_exp = p.sig("scalar", lambda e: e.activation(
                    out=c32b, in_=c32a, func=AF.Exp))
                p.wait("vector", "scalar", s_exp)
                p.sig("vector", lambda e: e.tensor_tensor(
                    r3(c16), r3(c32b)[:, :, 0:4], r3(c32b)[:, :, 4:8], OP.add))
                p.sig("vector", lambda e: e.tensor_tensor(
                    r3(c8w), r3(c16)[:, :, 0:2], r3(c16)[:, :, 2:4], OP.add))
                p.sig("vector", lambda e: e.tensor_tensor(
                    r3(c4s), r3(c8w)[:, :, 0:1], r3(c8w)[:, :, 1:2], OP.add))
                p.sig("vector", lambda e: e.reciprocal(c4m, c4s))
                p.sig("vector", lambda e: e.tensor_tensor(
                    r3(w_q), r3(c32b), bcm, OP.mult))
                p.sig("vector", lambda e: e.tensor_scalar(
                    c32a, w_q, rb_col, None, OP.add))
                cur = [c32a, c32b]
                for k4 in range(4):
                    p.sig("vector", lambda e, s=cur[0]: e.tensor_tensor(
                        r3(c16), r3(s)[:, :, 0:4], r3(s)[:, :, 4:8], OP.max))
                    p.sig("vector", lambda e: e.tensor_tensor(
                        r3(c8w), r3(c16)[:, :, 0:2], r3(c16)[:, :, 2:4], OP.max))
                    p.sig("vector", lambda e: e.tensor_tensor(
                        r3(c4m), r3(c8w)[:, :, 0:1], r3(c8w)[:, :, 1:2], OP.max))
                    if k4 < 3:
                        p.sig("vector", lambda e, s=cur[0]: e.tensor_tensor(
                            r3(c32g), r3(s), bcm, OP.is_ge))
                        p.sig("vector", lambda e, s=cur[0], d=cur[1]: e.scalar_tensor_tensor(
                            d, c32g, -2.0, s, OP.mult, OP.add))
                        cur = [cur[1], cur[0]]
                p.sig("vector", lambda e, d=cur[1]: e.tensor_scalar(
                    d, w_q, rb_col, None, OP.add))
                p.sig("vector", lambda e, s=cur[1]: e.tensor_tensor(
                    r3(c32g), r3(s), bcm, OP.is_ge))
                p.sig("vector", lambda e, d=cur[1]: e.tensor_tensor(
                    d, w_q, c32g, OP.mult))
                p.sig("vector", lambda e, s=cur[1]: e.tensor_tensor(
                    r3(c16), r3(s)[:, :, 0:4], r3(s)[:, :, 4:8], OP.add))
                p.sig("vector", lambda e: e.tensor_tensor(
                    r3(c8w), r3(c16)[:, :, 0:2], r3(c16)[:, :, 2:4], OP.add))
                p.sig("vector", lambda e: e.tensor_tensor(
                    r3(c4m), r3(c8w)[:, :, 0:1], r3(c8w)[:, :, 1:2], OP.add))
                p.sig("vector", lambda e: e.tensor_scalar(
                    c4s, c4m, 1e-20, None, OP.add))
                p.sig("vector", lambda e: e.reciprocal(c4m, c4s))
                v_w = p.sig("vector", lambda e, s=cur[1]: e.tensor_tensor(
                    r3(w_q), r3(s), bcm, OP.mult))
                if tp_w is not None:
                    p.wait("sync", "vector", v_w)
                    for t5 in range(CPG):
                        tap_dma(lambda e, t=t5, q=q: e.dma_start(
                            out=tp_w[(q * GT + t * CH):(q * GT + (t + 1) * CH), :],
                            in_=w_q[:, t * NEXP:(t + 1) * NEXP]))
                return v_w

            p.wait("vector", "tensor", act_pe_guard)
            w8T_v = 0
            s_wc = 0
            last_pe_u = 0
            for idx, pe_i in enumerate(pe_order):
                wb = idx % 2
                if idx == 0:
                    v_w = router_chain()
                elif idx == 1:
                    p.wait("scalar", "vector", v_w)
                    s_wc = p.sig("scalar", lambda e: e.activation(
                        out=w8c, in_=w_q, func=AF.Copy))
                    if TWT_PD7:
                        for t4 in range(4):
                            p.wait("tensor", "scalar", s_wc)
                            t_wt = p.sig("tensor", lambda e, t=t4: e.transpose(
                                PD.bitcast(BF16)[0:NEXP, 1024 + t * 128:1024 + (t + 1) * 128],
                                w8c[:, t * NEXP:(t + 1) * NEXP], ident_b))
                            p.wait("vector", "tensor", t_wt)
                            if t4 == 0:
                                p.wait("vector", "tensor", w8T_pe_guard)
                            w8T_v = p.sig("vector", lambda e, t=t4: e.tensor_copy(
                                out=w8T[0:NEXP, t * CH:(t + 1) * CH],
                                in_=PD.bitcast(BF16)[0:NEXP, 1024 + t * 128:1024 + (t + 1) * 128]))
                    else:
                        for t4 in range(4):
                            bB0.acquire()
                            p.wait("tensor", "scalar", s_wc)
                            t_wt = p.sig("tensor", lambda e, t=t4: e.transpose(
                                PB.bitcast(BF16)[0:NEXP, 0:128],
                                w8c[:, t * NEXP:(t + 1) * NEXP], ident_b))
                            p.wait("vector", "tensor", t_wt)
                            if t4 == 0:
                                p.wait("vector", "tensor", w8T_pe_guard)
                            w8T_v = p.sig("vector", lambda e, t=t4: e.tensor_copy(
                                out=w8T[0:NEXP, t * CH:(t + 1) * CH],
                                in_=PB.bitcast(BF16)[0:NEXP, 0:128]))
                            bB0.release("vector", w8T_v)
                if idx in gu_pre:
                    g_ld = gu_pre.pop(idx)
                else:
                    p.wait("sync", "tensor", gw_pe_guard[wb])
                    g_ld = emit_gu_load(pe_i, wb)
                if pe_i < NEXP:
                    bC.acquire()
                    p.wait("tensor", "vector", w8T_v)
                    pe_wbc = p.sig("tensor", lambda e, i=pe_i: e.matmul(
                        PC[:, 0:512], sel16[:, i * 128:(i + 1) * 128], w8T,
                        start=True, stop=True))
                    w8T_pe_guard = pe_wbc
                p.wait("tensor", "sync", g_ld)
                for ic in range(4):
                    pbk = ic % 2
                    bkg = [bA0, bA1][pbk]
                    bku = [bB0, bB1][pbk]
                    bkg.acquire()
                    pe_g = 0
                    for kc in range(8):
                        fn = lambda e, i=ic, k=kc, b=bkg, w=wb, ss=hs: e.matmul(
                            b.ap, gw[w][:, k * 512 + i * 128:k * 512 + (i + 1) * 128],
                            hT[ss][k], start=(k == 0), stop=(k == 7))
                        if kc < 7:
                            p.raw("tensor", fn)
                        else:
                            pe_g = p.sig("tensor", fn)
                    bku.acquire()
                    pe_u = 0
                    for kc in range(8):
                        fn = lambda e, i=ic, k=kc, b=bku, w=wb, ss=hs: e.matmul(
                            b.ap, uw[w][:, k * 512 + i * 128:k * 512 + (i + 1) * 128],
                            hT[ss][k], start=(k == 0), stop=(k == 7))
                        if kc < 7:
                            p.raw("tensor", fn)
                        else:
                            pe_u = p.sig("tensor", fn)
                    gw_pe_guard[wb] = pe_u
                    last_pe_u = pe_u
                    slot = slot_i % 3
                    slot_i += 1
                    p.wait("scalar", "tensor", pe_g)
                    p.wait("scalar", "vector", slot_guard[slot])
                    s_sg = p.sig("scalar", lambda e, b=bkg, d=slot: e.activation(
                        out=sgs[d], in_=b.ap, func=AF.Silu))
                    bkg.release("scalar", s_sg)
                    p.wait("scalar", "tensor", pe_u)
                    s_cp = p.sig("scalar", lambda e, b=bku, d=slot: e.activation(
                        out=sus[d], in_=b.ap, func=AF.Copy))
                    bku.release("scalar", s_cp)
                    blk = pe_i * 4 + ic
                    p.wait("vector", "scalar", s_cp)
                    if pe_i < NEXP:
                        v_a1 = p.sig("vector", lambda e, d=slot: e.tensor_tensor(
                            scr512b, sgs[d], sus[d], OP.mult))
                        slot_guard[slot] = v_a1
                        v_a2 = p.sig("vector", lambda e, k=blk: e.tensor_tensor(
                            actAll[:, k * 512:(k + 1) * 512], scr512b, PC[:, 0:512], OP.mult))
                        if ic == 3:
                            bC.release("vector", v_a2)
                    else:
                        v_a2 = p.sig("vector", lambda e, k=blk, d=slot: e.tensor_tensor(
                            actAll[:, k * 512:(k + 1) * 512], sgs[d], sus[d], OP.mult))
                        slot_guard[slot] = v_a2
            hT_guard[hs] = last_pe_u
            acts_done_v = p.cnt["vector"]
            acts_done_s = p.cnt["scalar"]

            p.wait("tensor", "vector", acts_done_v)
            p.wait("tensor", "scalar", acts_done_s)
            pass_sig = [0, 0, 0]
            for ps_i in range(3):
                dbuf = 1 if ps_i == 1 else 0
                if not PRE_DE4:
                    dbuf = 0
                    p.wait("sync", "tensor", de4_guard[0])
                    for j in range(4):
                        pe_j = ps_i * 4 + j
                        if pe_j < NEXP:
                            p.dma("sync", lambda e, i=pe_j, j=j: e.dma_start(
                                out=de4[0][:, j * 4 * DM:(j + 1) * 4 * DM],
                                in_=down_bd[i]))
                        else:
                            p.dma("sync", lambda e, s=pe_j - NEXP, j=j: e.dma_start(
                                out=de4[0][:, j * 4 * DM:(j + 1) * 4 * DM],
                                in_=sd_bd[:, s * 4 * DM:(s + 1) * 4 * DM]))
                    p.wait("tensor", "sync")
                elif ps_i == 0:
                    p.wait("tensor", "sync")
                elif ps_i == 2:
                    p.wait("tensor", "sync")
                    if q + 1 < ngroups and PRE_HT:
                        p.wait("sync", "tensor", hT_guard[1 - hs])
                        for dc in range(8):
                            hT_ld_set[1 - hs] = p.dma(
                                "sync", lambda e, d=dc, qq=q + 1, ns=1 - hs: e.dma_start(
                                    out=hT[ns][d],
                                    in_=hb_d[qq * GT:(qq + 1) * GT, d * 128:(d + 1) * 128],
                                    transpose=True))
                    if q + 1 < ngroups and PRE_GU:
                        p.wait("sync", "tensor", gw_pe_guard[0])
                        gu_pre[0] = emit_gu_load(8, 0)
                        p.wait("sync", "tensor", gw_pe_guard[1])
                        gu_pre[1] = emit_gu_load(9, 1)
                mm_fns = []
                for tc in range(CPG):
                    bank_ap = [PA, PB, PC, PD][tc]
                    for j in range(4):
                        pe_j = ps_i * 4 + j
                        for ic in range(4):
                            blk = pe_j * 4 + ic
                            for nn in range(2):
                                first = (ps_i == 0 and j == 0 and ic == 0)
                                last = (ps_i == 2 and j == 3 and ic == 3)
                                mm_fns.append(lambda e, b=bank_ap, j=j, ic=ic, nn=nn,
                                      t=tc, k=blk, f=first, l=last, db=dbuf: e.matmul(
                                    b[:, nn * 512:(nn + 1) * 512],
                                    actAll[:, k * 512 + t * CH:k * 512 + (t + 1) * CH],
                                    de4[db][:, (j * 4 + ic) * DM + nn * 512:
                                         (j * 4 + ic) * DM + (nn + 1) * 512],
                                    start=f, stop=l))
                for fn_ in mm_fns[:-1]:
                    p.raw("tensor", fn_)
                pass_sig[ps_i] = p.sig("tensor", mm_fns[-1])
                de4_guard[0] = pass_sig[ps_i]
                if PRE_DE4 and ps_i == 0:
                    p.wait("sync", "tensor", pass_sig[0])
                    for j in range(4):
                        p.dma("sync", lambda e, s=j, j=j: e.dma_start(
                            out=de4[0][:, j * 4 * DM:(j + 1) * 4 * DM],
                            in_=sd_bd[:, s * 4 * DM:(s + 1) * 4 * DM]))
            pe_down_last = pass_sig[2]
            act_pe_guard = pe_down_last

            evac_sigs = []
            for tc in range(CPG):
                bank_ap = [PA, PB, PC, PD][tc]
                p.wait("vector", "tensor", pe_down_last)
                p.wait("scalar", "tensor", pe_down_last)
                if tp_ffn is not None:
                    p.sig("vector", lambda e, b=bank_ap: e.tensor_copy(out=ffn_ln, in_=b))
                    p.wait("sync", "vector")
                    tap_dma(lambda e, t=tc, q=q: e.dma_start(
                        out=tp_ffn[(q * GT + t * CH):(q * GT + (t + 1) * CH), :], in_=ffn_ln))
                s_rstd = emit_ln_stats(bank_ap, DM, eps_ln_col, ffn_ln, psum_src=True)
                p.wait("vector", "scalar", s_rstd)
                v_ln = p.sig("vector", lambda e, b=bank_ap: e.tensor_scalar(
                    ffn_ln, b, stat[:, 2:3], stat[:, 6:7], OP.subtract, OP.mult))
                evac_sigs.append(v_ln)
                p.wait("sync", "vector", x1_v_guard[tc % 2])
                x1_ld = p.dma("sync", lambda e, t=tc, q=q: e.dma_start(
                    out=x1_t[t % 2], in_=x1_d[(q * GT + t * CH):(q * GT + (t + 1) * CH), :]))
                p.wait("vector", "sync", x1_ld)
                p.wait("vector", "gpsimd", out_st[tc % 2])
                v_out = p.sig("vector", lambda e, t=tc: e.tensor_tensor(
                    out_t[t % 2], ffn_ln, x1_t[t % 2], OP.add))
                x1_v_guard[tc % 2] = v_out
                p.wait("gpsimd", "vector", v_out)
                out_st[tc % 2] = p.dma("gpsimd", lambda e, t=tc, q=q: e.dma_start(
                    out=out_h[(q * GT + t * CH):(q * GT + (t + 1) * CH), :], in_=out_t[t % 2]))
            for tc, (full, parts) in enumerate(dn_banks):
                for b_ in parts:
                    b_.release("vector", evac_sigs[tc])
            if PRE_DE4 and q + 1 < ngroups:
                p.wait("sync", "tensor", pe_down_last)
                for j in range(4):
                    p.dma("sync", lambda e, i=j, j=j: e.dma_start(
                        out=de4[0][:, j * 4 * DM:(j + 1) * 4 * DM], in_=down_bd[i]))

        p.wait("gpsimd", "gpsimd", p.cnt["gpsimd"])
        p.raw("gpsimd", lambda e: e.nop())

    return p, T


def host_layout(inputs):
    f32c = lambda a: np.ascontiguousarray(np.asarray(a), dtype=np.float32)
    W_in = f32c(inputs["L0_attn_W_in"])
    W_out = f32c(inputs["L0_attn_W_out"])
    conv_w = f32c(inputs["L0_attn_conv_w"])
    conv_b = f32c(inputs["L0_attn_conv_b"])
    b_in = f32c(inputs["L0_attn_b_in"])
    sel = np.zeros((64, NH * 128), np.float32)
    for r in range(64):
        h = r % 32
        sel[r, h * 128:(h + 1) * 128] = -1.0
    sel16 = np.zeros((16, NEXP * 128), np.float32)
    for e_ in range(NEXP):
        sel16[e_, e_ * 128:(e_ + 1) * 128] = 1.0
    strict_u = (np.arange(128)[:, None] < np.arange(128)[None, :]).astype(np.float32)
    mask_i = np.tile(-1e38 * np.eye(128, dtype=np.float32), (1, 4))
    common = {
        "W_inT": np.ascontiguousarray(W_in.T),
        "W_outT": np.ascontiguousarray(W_out.T),
        "conv_w_fm": np.ascontiguousarray(
            conv_w.reshape(18, 128, KCONV).transpose(1, 0, 2).reshape(128, 18 * KCONV)),
        "conv_b_fm": np.ascontiguousarray(conv_b.reshape(18, 128).T),
        "b_xbc_fm": np.ascontiguousarray(b_in[DIN:DIN + CONVD].reshape(18, 128).T),
        "b_z_row": np.ascontiguousarray(b_in[:DIN].reshape(1, DIN)),
        "b_dt_col": np.ascontiguousarray(b_in[DIN + CONVD:].reshape(NH, 1)),
        "dt_bias_col": np.ascontiguousarray(f32c(inputs["L0_attn_dt_bias"]).reshape(NH, 1)),
        "A_log_col": np.ascontiguousarray(f32c(inputs["L0_attn_A_log"]).reshape(NH, 1)),
        "D_row": np.ascontiguousarray(f32c(inputs["L0_attn_D"]).reshape(1, NH)),
        "nw_row": np.ascontiguousarray(f32c(inputs["L0_attn_norm_w"]).reshape(1, DIN)),
        "b_out_row": np.ascontiguousarray(f32c(inputs["L0_attn_b_out"]).reshape(1, DM)),
        "router_bias11": np.ascontiguousarray(
            f32c(inputs["L0_ffn_router_bias"]).reshape(1, 1)),
        "router_w_fm": np.ascontiguousarray(
            f32c(inputs["L0_ffn_router_w"]).reshape(8, 128, NEXP).transpose(1, 0, 2)
            .reshape(128, 8 * NEXP)),
        "gate_w": f32c(inputs["L0_ffn_gate_w"]),
        "up_w": f32c(inputs["L0_ffn_up_w"]),
        "down_w": f32c(inputs["L0_ffn_down_w"]),
        "s_gate_w": f32c(inputs["L0_ffn_s_gate_w"]),
        "s_up_w": f32c(inputs["L0_ffn_s_up_w"]),
        "s_down_w": f32c(inputs["L0_ffn_s_down_w"]),
        "ident": np.eye(64, dtype=np.float32),
        "ident_bf": np.eye(128, dtype=np.float32).astype(BF16NP),
        "tri": np.ascontiguousarray(np.tril(np.ones((128, 128), np.float32)).T).astype(BF16NP),
        "sel_bf": sel.astype(BF16NP),
        "sel16_bf": sel16.astype(BF16NP),
        "strictu_bf": strict_u.astype(BF16NP),
        "maski_bf": mask_i.astype(BF16NP),
        "ones32": np.ones((32, 128), np.float32),
    }
    u = f32c(inputs["L0_attn_u"])
    return u, common


def kernel(**inputs):
    u, common = host_layout(inputs)
    nc = bass.Bass("TRN2", target_bir_lowering=False)
    p, T = build_core(nc)
    p.build()
    in_maps = []
    for c in range(8):
        m = dict(common)
        m["u"] = np.ascontiguousarray(u[c])
        in_maps.append(m)
    res = run_bass_kernel_spmd(nc, in_maps, core_ids=list(range(8)))
    out = np.stack([np.asarray(r["out"], dtype=np.float32) for r in res.results], axis=0)
    return out
